# version 08
# baseline (speedup 1.0000x reference)
import numpy as np
import ml_dtypes
import concourse.bass as bass
import concourse.mybir as mybir
from concourse.bass_utils import run_bass_kernel_spmd

F32 = mybir.dt.float32
BF16 = mybir.dt.bfloat16
AF = mybir.ActivationFunctionType
ALU = mybir.AluOpType
AX = mybir.AxisListType
BFD = ml_dtypes.bfloat16

TOK = 2048
NT = 16
D = 1024
DT = 8
H = 16
DH = 64
FFN = 4096
NE = 8
F_RAW = 3264
FP = 3328
FT = FP // 128
NMAC = 4
EPS = 1e-5
MOE_CHUNKS = [(i * 512, 512) for i in range(6)] + [(3072, 256)]


class Prog:
    ENGS = ("T", "V", "S", "P", "Y")

    DMA_STREAMS = ("dy", "dp", "w0", "w1", "m0", "m1", "eb0", "eb1", "dbbs", "dbg")

    def __init__(self):
        self.instrs = {e: [] for e in Prog.ENGS}
        self.cnt = {k: 0 for k in (("T", "V", "S", "P", "Y", "CC") + Prog.DMA_STREAMS)}
        self.seen = {e: {} for e in Prog.ENGS}
        self.last_w = {}
        self.readers = {}
        self.sems = {}

    def _need(self, need, tok):
        k, v = tok
        if need.get(k, 0) < v:
            need[k] = v

    def _deps(self, reads, writes):
        need = {}
        for r in reads:
            t = self.last_w.get(r)
            if t:
                self._need(need, t)
        for w in writes:
            t = self.last_w.get(w)
            if t:
                self._need(need, t)
            for t in self.readers.get(w, ()):
                self._need(need, t)
        return need

    SELF_EDGE_SKIP = {("T", "T")}

    def _emit_waits(self, eng, need, self_key=None):
        for k, v in sorted(need.items()):
            if (eng, k) in Prog.SELF_EDGE_SKIP:
                continue
            if k == eng and self.cnt[k] < v:
                continue
            if self.seen[eng].get(k, 0) >= v:
                continue
            self.seen[eng][k] = v
            self.instrs[eng].append(("w", k, v))

    def _record(self, tok, reads, writes):
        for w in writes:
            self.last_w[w] = tok
            self.readers[w] = []
        for r in reads:
            self.readers.setdefault(r, []).append(tok)

    def emit(self, eng, fn, reads=(), writes=(), dma=False, collective=False):
        if getattr(self, "cap", None) is not None:
            self.cap.append(("e", eng, fn, reads, writes, dma, collective))
            return None
        need = self._deps(reads, writes)
        if collective:
            key = "CC"
        elif dma:
            key = dma if isinstance(dma, str) else "dy"
        else:
            key = eng
        self._emit_waits(eng, need)
        amt = 16 if dma else 1
        self.cnt[key] += amt
        tok = (key, self.cnt[key])
        self.instrs[eng].append(("i", fn, key, amt))
        self._record(tok, reads, writes)
        return tok

    def replay_rr(self, lists, block=8):
        idx = [0] * len(lists)
        while True:
            done = True
            for li, l in enumerate(lists):
                for _ in range(block):
                    if idx[li] >= len(l):
                        break
                    done = False
                    it = l[idx[li]]
                    idx[li] += 1
                    if it[0] == "e":
                        _, eng, fn, reads, writes, dma, coll = it
                        self.emit(eng, fn, reads, writes, dma, coll)
                    else:
                        _, eng, fns, reads, writes = it
                        self.group(eng, fns, reads, writes)
            if done:
                break

    def retoken(self, keys):
        toks = [self.last_w[k] for k in keys if k in self.last_w]
        if not toks:
            return
        best = max(toks, key=lambda t: t[1])
        for k in keys:
            if k in self.last_w:
                self.last_w[k] = best

    def group(self, eng, fns, reads=(), writes=()):
        if getattr(self, "cap", None) is not None:
            self.cap.append(("g", eng, fns, reads, writes))
            return None
        need = self._deps(reads, writes)
        self._emit_waits(eng, need)
        for fn in fns[:-1]:
            self.instrs[eng].append(("q", fn))
        self.cnt[eng] += 1
        tok = (eng, self.cnt[eng])
        self.instrs[eng].append(("i", fns[-1], eng, 1))
        self._record(tok, reads, writes)
        return tok

    def fence(self):
        snap = {k: v for k, v in self.cnt.items() if v > 0}
        for e in Prog.ENGS:
            self._emit_waits(e, snap)
        self.last_w = {}
        self.readers = {}

    def final_waits(self):
        snap = {k: v for k, v in self.cnt.items() if v > 0}
        for k, v in sorted(snap.items()):
            if k == "Y":
                continue
            self.instrs["Y"].append(("w", k, v))

    def run(self, block):
        sems = self.sems

        def runner(eng_obj, lst):
            for it in lst:
                if it[0] == "w":
                    eng_obj.wait_ge(sems[it[1]], it[2])
                elif it[0] == "q":
                    it[1](eng_obj)
                else:
                    _, fn, key, amt = it
                    fn(eng_obj).then_inc(sems[key], amt)

        @block.tensor
        def _(t):
            runner(t, self.instrs["T"])

        @block.vector
        def _(v):
            runner(v, self.instrs["V"])

        @block.scalar
        def _(s):
            runner(s, self.instrs["S"])

        @block.gpsimd
        def _(g):
            runner(g, self.instrs["P"])

        @block.sync
        def _(y):
            runner(y, self.instrs["Y"])


def build_nc(debug=None):
    nc = bass.Bass("TRN2", target_bir_lowering=False)
    p = Prog()

    def din(name, shape, dt=F32):
        return nc.dram_tensor(name, shape, dt, kind="ExternalInput").ap()

    x_d = din("x", [TOK, D])
    wq_d = din("wq", [D, D]); wk_d = din("wk", [D, D]); wv_d = din("wv", [D, D])
    wb_d = din("wb", [D, H]); wg_d = din("wg", [D, H]); wo_d = din("wo", [D, D])
    w1_d = din("w1", [D, FFN]); w2_d = din("w2", [FFN, D])
    gwt_d = din("gwt", [D, NE])
    upw_d = din("upw", [NE, D, FP]); gpw_d = din("gpw", [NE, D, FP])
    dwt_d = din("dwt", [NE, FP, D])
    upb_d = din("upb", [NE, 128, FT]); gpb_d = din("gpb", [NE, 128, FT])
    dbb_d = din("dbb", [NE, 128, D])
    bob_d = din("bob", [128, D]); b2b_d = din("b2b", [128, D])
    b1c_d = din("b1c", [128, FFN // 128])
    g1c_d = din("g1c", [128, DT]); b1lc_d = din("b1lc", [128, DT])
    g2c_d = din("g2c", [128, DT]); b2lc_d = din("b2lc", [128, DT])
    gbb_d = din("gbb", [128, NE])
    identf_d = din("identf", [128, 128])
    identb_d = din("identb", [128, 128], BF16)
    miu_d = din("miu", [128, 128])
    ones128_d = din("ones128", [128, 128])
    msu_d = din("msu", [128, 128], BF16)
    miub_d = din("miub", [128, 128], BF16)
    onesb_d = din("onesb", [128, 128], BF16)
    cmask_d = din("cmask", [128, 8])
    out_d = nc.dram_tensor("out", [TOK, D], F32, kind="ExternalOutput").ap()
    cc_in_d = nc.dram_tensor("cc_in", [32, DH * DH], F32).ap()
    cc_out_d = nc.dram_tensor("cc_out", [128, DH * DH], F32).ap()

    dbg = {}

    def add_dbg(name, shape, dt=F32):
        dbg[name] = nc.dram_tensor("dbg_" + name, shape, dt, kind="ExternalOutput").ap()
        return dbg[name]

    ctx = []

    def sb(name, shape, dt):
        t = nc.sbuf_tensor("sb_" + name, shape, dt)
        ctx.append(t)
        return t.__enter__()

    def psmt(name, shape, dt):
        t = nc.psum_tensor(name, shape, dt)
        ctx.append(t)
        return t.__enter__()

    BUF_X = sb("BUF_X", [128, NT * D], F32)
    BUF_A = sb("BUF_A", [128, 16384], BF16)
    BUF_B = sb("BUF_B", [128, 16384], BF16)
    BUF_C = sb("BUF_C", [128, 16384], BF16)
    SCR_B = sb("SCR_B", [128, 8192], BF16)
    PSV = sb("PSV", [128, 8192], BF16)
    SCR_F = sb("SCR_F", [128, 896], F32)
    lgcol = sb("lgcol", [128, 256], F32)
    lgcc = sb("lgcc", [128, 256], F32)
    bcol = sb("bcol", [128, 256], F32)
    slocb = sb("slocb", [128, 512], BF16)
    pib = sb("pib", [128, 512], BF16)
    s0b = sb("s0b", [128, 512], BF16)
    comb = sb("comb", [128, 128], F32)
    identf = sb("identf", [128, 128], F32)
    identb = sb("identb", [128, 128], BF16)
    miu_f = sb("miu_f", [128, 128], F32)
    ones128 = sb("ones128", [128, 128], F32)
    msu = sb("msu", [128, 128], BF16)
    miu_b = sb("miu_b", [128, 128], BF16)
    onesb = sb("onesb", [128, 128], BF16)
    cmask = sb("cmask", [128, 8], F32)
    b1c = sb("b1c", [128, FFN // 128], F32)
    b1tot = sb("b1tot", [128, FFN // 128], F32)
    g1c = sb("g1c", [128, DT], F32); b1lc = sb("b1lc", [128, DT], F32)
    g2c = sb("g2c", [128, DT], F32); b2lc = sb("b2lc", [128, DT], F32)
    gbb = sb("gbb", [128, NE], F32)
    upbt = sb("upbt", [128, 2 * FT], F32)
    gpbt = sb("gpbt", [128, 2 * FT], F32)
    gwc = sb("gwc", [128, DT * NE], BF16)
    stats = sb("stats", [128, 64], F32)
    bqkv = sb("bqkv", [128, 2 * DT], F32)
    zbias = sb("zbias", [128, 2 * H], F32)
    onec = sb("onec", [128, 1], F32)
    b1lb = sb("b1lb", [128, DT], BF16)
    b2lb = sb("b2lb", [128, DT], BF16)

    pb = [psmt(f"pb{i}", [128, 512], F32) for i in range(6)]
    pbtA = psmt("pbtA", [128, 1024], BF16)
    pbtB = psmt("pbtB", [128, 1024], BF16)

    for k in ("T", "V", "S", "P", "Y", "CC") + Prog.DMA_STREAMS:
        h_ = nc.semaphore("sem_" + k)
        ctx.append(h_)
        p.sems[k] = h_.__enter__()

    def xv(t):
        return BUF_X[:, t * D:(t + 1) * D]

    def Aview(h, c):
        return BUF_A[(h % 2) * 64:(h % 2) * 64 + 64,
                     (h // 2) * TOK + c * 128:(h // 2) * TOK + (c + 1) * 128]

    def Bview(h, c):
        return BUF_B[(h % 2) * 64:(h % 2) * 64 + 64,
                     (h // 2) * TOK + c * 128:(h // 2) * TOK + (c + 1) * 128]

    def vview(h, c):
        return BUF_C[:, c * D + h * 64: c * D + (h + 1) * 64]

    def hm(buf, h):
        return buf[(h % 2) * 64:(h % 2) * 64 + 64, (h // 2) * 64:(h // 2) * 64 + 64]

    def psvv(h, c):
        return PSV[(h % 2) * 64:(h % 2) * 64 + 64,
                   ((h // 2) * 16 + c) * 64:((h // 2) * 16 + c) * 64 + 64]

    def cs(buf, c, h):
        return buf[:, c * H + h: c * H + h + 1]

    def pr(h):
        return (h % 2) * 64

    for dst, src, key in [
        (identf[:, :], identf_d, "identf"), (identb[:, :], identb_d, "identb"),
        (miu_f[:, :], miu_d, "miu_f"), (ones128[:, :], ones128_d, "ones128"),
        (msu[:, :], msu_d, "msu"), (miu_b[:, :], miub_d, "miu_b"),
        (onesb[:, :], onesb_d, "onesb"), (cmask[:, :], cmask_d, "cmask"),
        (b1c[:, :], b1c_d, "b1c"), (g1c[:, :], g1c_d, "g1c"),
        (b1lc[:, :], b1lc_d, "b1lc"), (g2c[:, :], g2c_d, "g2c"),
        (b2lc[:, :], b2lc_d, "b2lc"), (gbb[:, :], gbb_d, "gbb"),
    ]:
        p.emit("Y", lambda y, d=dst, s=src: y.dma_start(out=d, in_=s), writes=[key], dma=True)
    p.emit("P", lambda g: g.dma_start(out=gwc[:, :].rearrange("p (k f) -> p k f", k=DT),
                                      in_=gwt_d.rearrange("(k p) f -> p k f", p=128)),
           writes=["gwc"], dma="dp")
    p.emit("Y", lambda y: y.dma_start(
        out=BUF_X[:, :].rearrange("p (t d) -> p t d", t=NT),
        in_=x_d.rearrange("(t p) d -> p t d", p=128)),
        writes=[f"x{t}" for t in range(NT)], dma=True)
    p.retoken(["identf", "identb", "miu_f", "ones128", "msu", "miu_b", "onesb", "cmask",
               "b1c", "g1c", "b1lc", "g2c", "b2lc", "gbb"] + [f"x{t}" for t in range(NT)])
    p.emit("V", lambda v: v.memset(onec[:, :], 1.0), writes=["onec"])
    p.emit("V", lambda v: v.tensor_copy(b1lb[:, :], b1lc[:, :]), reads=["b1lc"], writes=["b1lb"])
    p.emit("V", lambda v: v.tensor_copy(b2lb[:, :], b2lc[:, :]), reads=["b2lc"], writes=["b2lb"])

    def emit_ln(t, out_ap, out_keys):
        neg = stats[:, 0:1]; rst = stats[:, 4:5]
        sa = stats[:, 12:13]; sbb = stats[:, 16:17]; tmp = stats[:, 8:9]
        p.emit("V", lambda v: v.reduce_sum(neg, xv(t), axis=AX.X, negate=True),
               reads=[f"x{t}"], writes=["st_neg"])
        p.emit("V", lambda v: v.tensor_scalar(neg, neg, 1.0 / D, None, ALU.mult),
               reads=["st_neg"], writes=["st_neg"])
        p.emit("S", lambda s: s.activation(pb[5][:, 0:512], xv(t)[:, 0:512], AF.Square,
                                           accum_out=sa),
               reads=[f"x{t}"], writes=["pb5", "st_sa"])
        p.emit("S", lambda s: s.activation(pb[5][:, 0:512], xv(t)[:, 512:1024], AF.Square,
                                           accum_out=sbb),
               reads=[f"x{t}"], writes=["pb5", "st_sb"])
        p.emit("V", lambda v: v.tensor_tensor(tmp, sa, sbb, ALU.add),
               reads=["st_sa", "st_sb"], writes=["st_tmp"])
        p.emit("V", lambda v: v.tensor_scalar(tmp, tmp, 1.0 / D, None, ALU.mult),
               reads=["st_tmp"], writes=["st_tmp"])
        p.emit("V", lambda v: v.scalar_tensor_tensor(tmp, neg, neg, tmp, ALU.mult,
                                                     ALU.subtract),
               reads=["st_neg", "st_tmp"], writes=["st_tmp2"])
        p.emit("V", lambda v: v.tensor_scalar(tmp, tmp, -1.0, EPS, ALU.mult, ALU.add),
               reads=["st_tmp2"], writes=["st_tmp"])
        p.emit("V", lambda v: v.tensor_scalar(tmp, tmp, 1e-12, None, ALU.max),
               reads=["st_tmp"], writes=["st_tmp"])
        p.emit("S", lambda s: s.activation(rst, tmp, AF.Ln), reads=["st_tmp"], writes=["st_rst"])
        p.emit("S", lambda s: s.activation(rst, rst, AF.Exp, scale=-0.5),
               reads=["st_rst"], writes=["st_rst"])
        p.emit("V", lambda v: v.tensor_scalar(out_ap, xv(t), neg, rst, ALU.add, ALU.mult),
               reads=[f"x{t}", "st_neg", "st_rst"], writes=list(out_keys))

    def xnT(dt_, j):
        return SCR_B[:, dt_ * 512 + j * 128: dt_ * 512 + (j + 1) * 128]

    def wch(par):
        return SCR_B[:, 4096 + par * 2048: 4096 + (par + 1) * 2048]

    xntr = PSV[:, 1024:2048]
    vbias = PSV[:, 0:1024]
    GPAT = PSV[:, 2560:4608]
    GPATZ = PSV[:, 4608:4736]
    for kk in range(DT):
        for hh2 in range(2):
            p.emit("V", lambda v, k2=kk, j2=hh2: v.tensor_scalar(
                GPAT[:, k2 * 256 + j2 * 128:k2 * 256 + (j2 + 1) * 128], onesb[:, 0:128],
                g1c[:, k2:k2 + 1], None, ALU.mult),
                reads=["g1c", "onesb"], writes=["GPAT"])
        p.emit("V", lambda v, k2=kk: v.tensor_scalar(
            GPATZ[:, k2 * H:(k2 + 1) * H], onesb[:, 0:H],
            g1c[:, k2:k2 + 1], None, ALU.mult),
            reads=["g1c", "onesb"], writes=["GPATZ"])
    zcol_b = SCR_F[:, 320:384]
    zcol_g = SCR_F[:, 384:448]
    rowtmp = SCR_F[0:1, 448:576]

    for mac in range(NMAC):
        tt0 = mac * 4
        for j in range(4):
            t = tt0 + j
            emit_ln(t, xntr[:, :], [f"xn{t}"])
            for dt_ in range(DT):
                p.group("T", [lambda te, d=dt_: te.transpose(
                    pbtA[:, 0:128], xntr[:, d * 128:(d + 1) * 128], identb[:, :])],
                    reads=[f"xn{t}", "identb"], writes=["pbtA"])
                p.emit("V", lambda v, d=dt_, jj=j: v.tensor_copy(xnT(d, jj), pbtA[:, 0:128]),
                       reads=["pbtA"], writes=[f"xnT{t}_{dt_}"])
        xnk = [f"xnT{tt0 + j}_{d}" for j in range(4) for d in range(DT)]

        for wi, (wname, wdram) in enumerate((("q", wq_d), ("k", wk_d))):
            for chn in range(4):
                par = chn % 2
                p.emit("P", lambda g, w=wdram, c=chn, pa=par: g.dma_start(
                    out=wch(pa).rearrange("p (k f) -> p k f", k=DT),
                    in_=w[:, c * 256:(c + 1) * 256].rearrange("(k p) f -> p k f", p=128)),
                    writes=[f"wch{par}"], dma=f"w{par}")
                if mac == 0:
                    for sub in range(2):
                        do = chn * 2 + sub
                        p.group("T", [
                            (lambda te, kk=kk, s_=sub, pa=par, d_o=do: te.matmul(
                                pb[4][:, (d_o % 2):(d_o % 2) + 1],
                                wch(pa)[:, kk * 256 + s_ * 128: kk * 256 + (s_ + 1) * 128],
                                b1lb[:, kk:kk + 1], start=(kk == 0), stop=(kk == DT - 1)))
                            for kk in range(DT)],
                            reads=[f"wch{par}", "b1lb"], writes=["pb4"])
                        p.emit("V", lambda v, w=wi, d_o=do: v.tensor_copy(
                            bqkv[:, w * DT + d_o: w * DT + d_o + 1], pb[4][:, (d_o % 2):(d_o % 2) + 1]),
                            reads=["pb4"], writes=[f"bq{wi}"])
                p.emit("V", lambda v, pa=par: v.tensor_tensor(
                    wch(pa)[:, :], wch(pa)[:, :], GPAT[:, :], ALU.mult),
                    reads=[f"wch{par}", "GPAT"], writes=[f"wch{par}"])
                for sub in range(2):
                    do = chn * 2 + sub
                    bank = do % 2
                    p.group("T", [
                        (lambda te, kk=kk, s_=sub, pa=par, b=bank: te.matmul(
                            pb[b][:, 0:512],
                            wch(pa)[:, kk * 256 + s_ * 128: kk * 256 + (s_ + 1) * 128],
                            SCR_B[:, kk * 512:(kk + 1) * 512],
                            start=(kk == 0), stop=(kk == DT - 1)))
                        for kk in range(DT)],
                        reads=[f"wch{par}"] + xnk, writes=[f"pb{bank}"])
                    dst = (BUF_A if wname == "q" else BUF_B)
                    wk_keys = [f"{'A' if wname == 'q' else 'B'}{2 * do + i}_{tt0 + j}"
                               for i in range(2) for j in range(4)]
                    p.emit("S", lambda s, d_o=do, b=bank, w=wi, _d=dst, _t0=tt0: s.activation(
                        _d[:, d_o * TOK + _t0 * 128: d_o * TOK + _t0 * 128 + 512],
                        pb[b][:, 0:512], AF.Identity,
                        bias=bqkv[:, w * DT + d_o: w * DT + d_o + 1]),
                        reads=[f"pb{bank}", f"bq{wi}"], writes=wk_keys)

        for chn in range(4):
            par = chn % 2
            p.emit("P", lambda g, c=chn, pa=par: g.dma_start(
                out=wch(pa).rearrange("p (k f) -> p k f", k=DT),
                in_=wv_d[:, c * 256:(c + 1) * 256].rearrange("(k p) f -> p k f", p=128)),
                writes=[f"wch{par}"], dma=f"w{par}")
            if mac == 0:
                p.group("T", [
                    (lambda te, kk=kk, pa=par: te.matmul(
                        pb[4][0:1, 0:256], b1lb[:, kk:kk + 1], wch(pa)[:, kk * 256:(kk + 1) * 256],
                        start=(kk == 0), stop=(kk == DT - 1)))
                    for kk in range(DT)],
                    reads=[f"wch{par}", "b1lb"], writes=["pb4"])
                p.emit("V", lambda v: v.tensor_copy(PSV[0:1, 2048:2176], pb[4][0:1, 0:128]),
                       reads=["pb4"], writes=["vbr"])
                p.emit("V", lambda v: v.tensor_copy(PSV[0:1, 2176:2304], pb[4][0:1, 128:256]),
                       reads=["pb4"], writes=["vbr2"])
                p.group("T", [lambda te: te.matmul(
                    pb[5][:, 256:384], onesb[0:1, 0:128], PSV[0:1, 2048:2176], start=True, stop=True)],
                    reads=["vbr", "onesb"], writes=["pb5"])
                p.emit("V", lambda v, c=chn: v.tensor_copy(vbias[:, c * 256:c * 256 + 128],
                                                           pb[5][:, 256:384]),
                       reads=["pb5"], writes=["vbias"])
                p.group("T", [lambda te: te.matmul(
                    pb[5][:, 384:512], onesb[0:1, 0:128], PSV[0:1, 2176:2304], start=True, stop=True)],
                    reads=["vbr2", "onesb"], writes=["pb5"])
                p.emit("V", lambda v, c=chn: v.tensor_copy(vbias[:, c * 256 + 128:(c + 1) * 256],
                                                           pb[5][:, 384:512]),
                       reads=["pb5"], writes=["vbias"])
            p.emit("V", lambda v, pa=par: v.tensor_tensor(
                wch(pa)[:, :], wch(pa)[:, :], GPAT[:, :], ALU.mult),
                reads=[f"wch{par}", "GPAT"], writes=[f"wch{par}"])
            for j in range(4):
                t = tt0 + j
                bank = 2 + j % 2
                p.group("T", [
                    (lambda te, kk=kk, pa=par, jj=j, b=bank: te.matmul(
                        pb[b][:, 0:256], xnT(kk, jj), wch(pa)[:, kk * 256:(kk + 1) * 256],
                        start=(kk == 0), stop=(kk == DT - 1)))
                    for kk in range(DT)],
                    reads=[f"wch{par}"] + xnk, writes=[f"pb{bank}"])
                p.emit("V", lambda v, tg=t, c=chn, b=bank: v.tensor_tensor(
                    BUF_C[:, tg * D + c * 256: tg * D + (c + 1) * 256],
                    pb[b][:, 0:256], vbias[:, c * 256:(c + 1) * 256], ALU.add),
                    reads=[f"pb{bank}", "vbias"], writes=[f"C{t}_{chn}"])

        for zi, (zdram, zc) in enumerate(((wb_d, zcol_b), (wg_d, zcol_g))):
            par = zi
            p.emit("P", lambda g, w=zdram, pa=par: g.dma_start(
                out=wch(pa)[:, 0:DT * H].rearrange("p (k f) -> p k f", k=DT),
                in_=w.rearrange("(k p) f -> p k f", p=128)),
                writes=[f"wch{par}"], dma=f"w{par}")
            if mac == 0:
                p.group("T", [
                    (lambda te, kk=kk, pa=par: te.matmul(
                        pb[4][0:1, 256:256 + H], b1lb[:, kk:kk + 1], wch(pa)[:, kk * H:(kk + 1) * H],
                        start=(kk == 0), stop=(kk == DT - 1)))
                    for kk in range(DT)],
                    reads=[f"wch{par}", "b1lb"], writes=["pb4"])
                p.emit("V", lambda v: v.tensor_copy(PSV[0:1, 2304:2304 + H], pb[4][0:1, 256:256 + H]),
                       reads=["pb4"], writes=["zbr"])
                p.group("T", [lambda te: te.matmul(
                    pb[5][:, 256:256 + H], onesb[0:1, 0:128], PSV[0:1, 2304:2304 + H], start=True, stop=True)],
                    reads=["zbr", "onesb"], writes=["pb5"])
                p.emit("V", lambda v, z=zi: v.tensor_copy(zbias[:, z * H:(z + 1) * H],
                                                          pb[5][:, 256:256 + H]),
                       reads=["pb5"], writes=[f"zbias{zi}"])
            p.emit("V", lambda v, pa=par: v.tensor_tensor(
                wch(pa)[:, 0:DT * H], wch(pa)[:, 0:DT * H], GPATZ[:, :], ALU.mult),
                reads=[f"wch{par}", "GPATZ"], writes=[f"wch{par}"])
            for j in range(4):
                bank = 4 + zi
                p.group("T", [
                    (lambda te, kk=kk, pa=par, jj=j, b=bank: te.matmul(
                        pb[b][:, 0:H], xnT(kk, jj), wch(pa)[:, kk * H:(kk + 1) * H],
                        start=(kk == 0), stop=(kk == DT - 1)))
                    for kk in range(DT)],
                    reads=[f"wch{par}"] + xnk, writes=[f"pb{bank}"])
                p.emit("V", lambda v, jj=j, b=bank, z=zi, _zc=zc: v.tensor_tensor(
                    _zc[:, jj * H:(jj + 1) * H], pb[b][:, 0:H], zbias[:, z * H:(z + 1) * H], ALU.add),
                    reads=[f"pb{bank}", f"zbias{zi}"], writes=[f"z{zi}_{j}"])
        for zi, zc in ((0, zcol_b), (1, zcol_g)):
            a_ = SCR_F[:, 0:64]; e_ = SCR_F[:, 64:128]; l_ = SCR_F[:, 128:192]
            zk = [f"z{zi}_{j}" for j in range(4)]
            p.emit("S", lambda s, _zc=zc: s.activation(a_, _zc[:, :], AF.Abs),
                   reads=zk, writes=["ls_a"])
            p.emit("S", lambda s: s.activation(e_, a_, AF.Exp, scale=-1.0),
                   reads=["ls_a"], writes=["ls_e"])
            p.emit("S", lambda s: s.activation(l_, e_, AF.Ln, bias=onec[:, 0:1]),
                   reads=["ls_e", "onec"], writes=["ls_l"])
            p.emit("V", lambda v, _zc=zc: v.tensor_scalar(a_, _zc[:, :], 0.0, None, ALU.min),
                   reads=zk, writes=["ls_a"])
            p.emit("V", lambda v, _zc=zc: v.tensor_tensor(_zc[:, :], a_, l_, ALU.subtract),
                   reads=["ls_a", "ls_l"], writes=zk)
        p.emit("S", lambda s, m=mac: s.activation(bcol[:, m * 64:(m + 1) * 64], zcol_b[:, :], AF.Exp),
               reads=[f"z0_{j}" for j in range(4)], writes=[f"bcol{mac}"])
        for j in range(4):
            t = tt0 + j
            p.group("T", [lambda te, jj=j: te.matmul(
                pb[5][:, 0:H], miu_f[:, :], zcol_g[:, jj * H:(jj + 1) * H], start=True, stop=True)],
                reads=[f"z1_{j}", "miu_f"], writes=["pb5"])
            p.emit("V", lambda v, tg=t: v.tensor_copy(lgcol[:, tg * H:(tg + 1) * H], pb[5][:, 0:H]),
                   reads=["pb5"], writes=[f"lgcol{t}"])
            p.group("T", [lambda te, jj=j: te.matmul(
                pb[5][:, H:2 * H], ones128[:, :], zcol_g[:, jj * H:(jj + 1) * H], start=True, stop=True)],
                reads=[f"z1_{j}", "ones128"], writes=["pb5"])
            p.emit("V", lambda v, tg=t: v.tensor_copy(lgcc[:, tg * H:(tg + 1) * H], pb[5][:, H:2 * H]),
                   reads=["pb5"], writes=[f"lgcc{t}"])

    if debug == "A1":
        p.fence()
        for nm, buf in (("qT", BUF_A), ("kT", BUF_B), ("v", BUF_C)):
            dd = add_dbg(nm, [128, 16384])
            p.emit("P", lambda g, d=dd, b=buf: g.dma_start(out=d, in_=b[:, :]), dma="dbg")
        for nm, buf in (("lgcol", lgcol), ("lgcc", lgcc)):
            dd = add_dbg(nm, [128, 256])
            p.emit("Y", lambda y, d=dd, b=buf: y.dma_start(out=d, in_=b[:, :]), dma=True)
        add_dbg("bcol", [128, 256])
        p.emit("P", lambda g: g.dma_start(out=dbg["bcol"], in_=bcol[:, :]), dma="dbg")
        add_dbg("scrb", [128, 8192])
        p.emit("P", lambda g: g.dma_start(out=dbg["scrb"], in_=SCR_B[:, :]), dma="dbg")
        add_dbg("stats", [128, 64])
        p.emit("Y", lambda y: y.dma_start(out=dbg["stats"], in_=stats[:, :]), dma=True)
        add_dbg("xbuf", [128, NT * D])
        p.emit("Y", lambda y: y.dma_start(out=dbg["xbuf"], in_=BUF_X[:, :]), dma=True)
        p.fence(); p.final_waits()
        return _finish(nc, p, ctx, dbg)

    p.fence()

    def scrb(par, a, b):
        return SCR_B[:, par * 2048 + a: par * 2048 + b]

    p.emit("V", lambda v: v.memset(slocb[:, :], 0.0), writes=[f"sl{h}" for h in range(H)])
    for h in range(H):
        p.emit("V", lambda v, hh=h: v.tensor_copy(
            hm(pib, hh), identf[pr(hh):pr(hh) + 64, pr(hh):pr(hh) + 64]),
            reads=["identf"], writes=[f"pi{h}"])

    import os as _os
    _NTU = int(_os.environ.get("KUNITS", NT))

    def emit_unit(c, h):
            par = (c * H + h) % 2
            b0 = pr(h)
            TMP1 = SCR_F[:, 0:128] if par == 0 else SCR_F[:, 768:896]
            pbA = pb[3 * par]
            pbB = pb[3 * par + 1]
            pbC = pb[3 * par + 2]
            kA = f"pb{3 * par}"; kB = f"pb{3 * par + 1}"; kC = f"pb{3 * par + 2}"
            DMP = SCR_F[:, 128 + par * 128:256 + par * 128]
            DM = DMP
            XF = SCR_F[:, 384 + par * 128:512 + par * 128]
            RN = SCR_F[b0:b0 + 1, 640:768]
            POW = [scrb(par, i * 128, (i + 1) * 128) for i in range(7)]
            TT = scrb(par, 896, 1024)
            khT = SCR_B[b0:b0 + 64, par * 2048 + 1024: par * 2048 + 1152]
            QtT = SCR_B[b0:b0 + 64, par * 2048 + 1152: par * 2048 + 1280]
            Kc = scrb(par, 1280, 1344)
            KG = scrb(par, 1344, 1408)
            BXv = scrb(par, 1408, 1472)
            BW = scrb(par, 1472, 1536)
            BWT = SCR_B[b0:b0 + 64, par * 2048 + 1536: par * 2048 + 1664]
            Rt = scrb(par, 1664, 1728)
            PT = scrb(par, 1728, 1856)
            Gb64 = SCR_B[b0:b0 + 64, par * 2048 + 1856: par * 2048 + 1984]
            K2 = SCR_B[b0:b0 + 64, 4096:4224]
            XB = SCR_B[:, 4224 + par * 128: 4352 + par * 128]
            RNb = SCR_B[b0:b0 + 1, 4480:4608]
            AcT = SCR_B[b0:b0 + 64, 4608:4672]
            u = f"{par}"
            kq = f"A{h}_{c}"; kk_ = f"B{h}_{c}"; kv_ = f"C{c}_{h // 4}"
            lgc_ = cs(lgcol, c, h); bc_ = cs(bcol, c, h); lgcc_ = cs(lgcc, c, h)
            gcc_ = stats[b0:b0 + 64, 52 + 4 * par:53 + 4 * par]

            p.emit("V", lambda v, l=lgc_: v.tensor_scalar(TMP1, ones128[:, :], l, None, ALU.mult),
                   reads=[f"lgcol{c}", "ones128"], writes=[f"TMP1{par}"])
            p.group("T", [lambda te, _pa=pbA: te.transpose(_pa[:, 0:128], TMP1, identf[:, :])],
                    reads=[f"TMP1{par}", "identf"], writes=[kA])
            p.emit("V", lambda v, l=lgc_, _d=DMP, _pa=pbA: v.tensor_scalar(
                _d, _pa[:, 0:128], l, 0.0, ALU.subtract, ALU.min),
                reads=[kA, f"lgcol{c}"], writes=[f"DMP{u}"])
            p.emit("S", lambda s, _d=DMP: s.activation(_d, _d, AF.Exp),
                   reads=[f"DMP{u}"], writes=[f"DM{u}"])
            p.emit("S", lambda s, _g=Gb64, _b=b0, _pa=pbA: s.activation(_g, _pa[_b:_b + 64, 0:128], AF.Exp),
                   reads=[kA], writes=[f"Gb{u}"])
            p.emit("V", lambda v, _q=QtT, _g=Gb64, ap=Aview(h, c): v.scalar_tensor_tensor(
                _q, ap, 0.125, _g, ALU.mult, ALU.mult),
                reads=[kq, f"Gb{u}"], writes=[f"QtT{u}"])
            p.emit("V", lambda v, _k=K2, ap=Bview(h, c): v.tensor_tensor(_k, ap, ap, ALU.mult),
                   reads=[kk_], writes=[f"K2{u}"])
            p.group("T", [lambda te, _k=K2, _b=b0, _pa=pbA: te.matmul(
                _pa[_b:_b + 1, 128:256], onesb[_b:_b + 64, 0:1], _k, start=True, stop=True)],
                reads=[f"K2{u}", "onesb"], writes=[kA])
            p.emit("S", lambda s, _r=RN, _b=b0, _pa=pbA: s.activation(_r, _pa[_b:_b + 1, 128:256], AF.Ln),
                   reads=[kA], writes=[f"RN{u}"])
            p.emit("S", lambda s, _r=RN: s.activation(_r, _r, AF.Exp, scale=0.5),
                   reads=[f"RN{u}"], writes=[f"RN{u}"])
            p.emit("V", lambda v, _r=RN: v.tensor_scalar(_r, _r, 1e-6, None, ALU.add),
                   reads=[f"RN{u}"], writes=[f"RN{u}"])
            def _recip(v, _r=RN, _o=RNb):
                with nc.allow_low_precision(reason="bf16 k-norm reciprocal row"):
                    return v.reciprocal(_o, _r)
            p.emit("V", _recip, reads=[f"RN{u}"], writes=[f"RNb{u}"])
            p.group("T", [lambda te, _o=RNb, _b=b0, _pa=pbA: te.matmul(
                _pa[_b:_b + 64, 256:384], onesb[_b:_b + 1, 0:64], _o, start=True, stop=True)],
                reads=[f"RNb{u}", "onesb"], writes=[kA])
            p.emit("V", lambda v, _k=khT, ap=Bview(h, c), _b=b0, _pa=pbA: v.tensor_tensor(
                _k, ap, _pa[_b:_b + 64, 256:384], ALU.mult),
                reads=[kk_, kA], writes=[f"khT{u}"])
            p.group("T", [lambda te, _k=khT, _pb=pbB: te.matmul(_pb[:, 0:128], _k, _k, start=True, stop=True)],
                    reads=[f"khT{u}"], writes=[kB])
            p.emit("V", lambda v, _d=DM, _pb=pbB: v.tensor_tensor(TMP1, _pb[:, 0:128], _d, ALU.mult),
                   reads=[kB, f"DM{u}", f"TMP1{par}"], writes=[f"TMP1{par}"])
            p.emit("V", lambda v: v.tensor_tensor(TMP1, TMP1, msu[:, :], ALU.mult),
                   reads=[f"TMP1{par}", "msu"], writes=[f"TMP1{par}"])
            p.emit("V", lambda v, _p=POW[0], b=bc_: v.tensor_scalar(
                _p, TMP1, b, -1.0, ALU.mult, ALU.mult),
                reads=[f"TMP1{par}", f"bcol{c // 4}"], writes=[f"POW0{u}"])
            DPW = [SCR_B[:, 4800 + par * 768 + i * 128: 4800 + par * 768 + (i + 1) * 128]
                   for i in range(6)]
            p.group("T", [lambda te, _p=POW[0]: te.transpose(pbtA[:, 0:128], _p, identb[:, :])],
                    reads=[f"POW0{u}", "identb"], writes=["pbtA"])
            p.emit("V", lambda v, _d=DPW[0]: v.tensor_copy(_d, pbtA[:, 0:128]),
                   reads=["pbtA"], writes=[f"DP0{u}"])
            for jj in range(6):
                p.group("T", [lambda te, _d=DPW[jj], _p=POW[jj], _pc=pbC: te.matmul(
                    _pc[:, 0:128], _d, _p, start=True, stop=True)],
                    reads=[f"DP{jj}{u}", f"POW{jj}{u}"], writes=[kC])
                p.emit("V", lambda v, _p=POW[jj + 1], _pc=pbC: v.tensor_copy(_p, _pc[:, 0:128]),
                       reads=[kC], writes=[f"POW{jj + 1}{u}"])
                if jj < 5:
                    p.group("T", [lambda te, _d=DPW[jj], _p=POW[jj], _pb=pbB: te.matmul(
                        _pb[:, 384:512], _p, _d, start=True, stop=True)],
                        reads=[f"DP{jj}{u}", f"POW{jj}{u}"], writes=[kB])
                    p.emit("V", lambda v, _d=DPW[jj + 1], _pb=pbB: v.tensor_copy(_d, _pb[:, 384:512]),
                           reads=[kB], writes=[f"DP{jj + 1}{u}"])
            p.group("T", [lambda te, _k=khT, _b=b0: te.transpose(
                pbtA[:, 128:192], _k, identb[_b:_b + 64, _b:_b + 64])],
                    reads=[f"khT{u}", "identb"], writes=["pbtA"])
            p.emit("V", lambda v, _k=Kc: v.tensor_copy(_k, pbtA[:, 128:192]),
                   reads=["pbtA"], writes=[f"Kc{u}"])
            gcol_ = stats[:, 20 + 4 * par:21 + 4 * par]
            p.emit("S", lambda s, l=lgc_, _g=gcol_: s.activation(_g, l, AF.Exp),
                   reads=[f"lgcol{c}"], writes=[f"gcol{u}"])
            p.emit("V", lambda v, _b=XB, ap=vview(h, c): v.tensor_copy(_b[:, 0:64], ap),
                   reads=[kv_], writes=[f"XB{u}"])
            p.emit("V", lambda v, _b=XB, _k=Kc, _g=gcol_: v.tensor_scalar(
                _b[:, 64:128], _k, _g, None, ALU.mult),
                reads=[f"Kc{u}", f"gcol{u}"], writes=[f"XB{u}"])
            regs = [(128, 256), (256, 384)]
            p.group("T", [
                (lambda te, _b=XB, _pc=pbC: te.matmul(
                    _pc[:, 128:256], identb[:, :], _b, start=True, stop=False)),
                (lambda te, _p=POW[6], _b=XB, _pc=pbC: te.matmul(
                    _pc[:, 128:256], _p, _b, start=False, stop=True))],
                reads=[f"XB{u}", f"POW6{u}", "identb"], writes=[kC])
            cur = 0
            for jj in range(5, -1, -1):
                r0 = regs[cur]
                r1 = regs[1 - cur]
                p.emit("V", lambda v, _b=XB, _pc=pbC, _r=r0: v.tensor_copy(_b, _pc[:, _r[0]:_r[1]]),
                       reads=[kC], writes=[f"XB{u}"])
                p.group("T", [
                    (lambda te, _b=XB, _pc=pbC, _r=r1: te.matmul(
                        _pc[:, _r[0]:_r[1]], identb[:, :], _b, start=True, stop=False)),
                    (lambda te, _p=POW[jj], _b=XB, _pc=pbC, _r=r1: te.matmul(
                        _pc[:, _r[0]:_r[1]], _p, _b, start=False, stop=True))],
                    reads=[f"POW{jj}{u}", f"XB{u}", "identb"], writes=[kC])
                cur = 1 - cur
            xf0 = regs[cur][0]
            p.emit("V", lambda v, _o=BXv, b=bc_, _pc=pbC, _x=xf0: v.tensor_scalar(
                _o, _pc[:, _x:_x + 64], b, None, ALU.mult),
                reads=[kC, f"bcol{c // 4}"], writes=[f"BXv{u}"])
            p.emit("V", lambda v, _o=BW, b=bc_, _pc=pbC, _x=xf0: v.tensor_scalar(
                _o, _pc[:, _x + 64:_x + 128], b, -1.0, ALU.mult, ALU.mult),
                reads=[kC, f"bcol{c // 4}"], writes=[f"BW{u}"])
            p.group("T", [lambda te, _k=khT, ap=Aview(h, c), _pb=pbB: te.matmul(
                _pb[:, 128:256], _k, ap, start=True, stop=True)],
                reads=[f"khT{u}", kq], writes=[kB])
            p.emit("V", lambda v, _d=DM, _pb=pbB: v.tensor_tensor(TMP1, _pb[:, 128:256], _d, ALU.mult),
                   reads=[kB, f"DM{u}", f"TMP1{par}"], writes=[f"TMP1{par}"])
            p.emit("V", lambda v, _p=PT: v.scalar_tensor_tensor(
                _p, TMP1, 0.125, miu_b[:, :], ALU.mult, ALU.mult),
                reads=[f"TMP1{par}", "miu_b"], writes=[f"PT{u}"])
            p.group("T", [lambda te, _w=BW, _p=PT, _b=b0, _pb=pbB: te.matmul(
                _pb[_b:_b + 64, 256:384], _w, _p, start=True, stop=True)],
                reads=[f"BW{u}", f"PT{u}"], writes=[kB])
            p.emit("V", lambda v, _q=QtT, ap=Aview(h, c), _b=b0, _pb=pbB: v.tensor_tensor(
                ap, _q, _pb[_b:_b + 64, 256:384], ALU.add),
                reads=[f"QtT{u}", kB], writes=[kq])
            p.group("T", [lambda te, _w=BW, _b=b0: te.transpose(
                pbtA[_b:_b + 64, 192:320], _w, identb[:, :])],
                reads=[f"BW{u}", "identb"], writes=["pbtA"])
            p.emit("V", lambda v, _w=BWT, _b=b0: v.tensor_copy(_w, pbtA[_b:_b + 64, 192:320]),
                   reads=["pbtA"], writes=[f"BWT{u}"])
            p.group("T", [lambda te, _w=BWT, hh=h, _pb=pbB: te.matmul(
                _pb[:, 384:448], _w, hm(slocb, hh), start=True, stop=True)],
                reads=[f"BWT{u}", f"sl{h}"], writes=[kB])
            p.emit("V", lambda v, _r=Rt, _x=BXv, _pb=pbB: v.tensor_tensor(_r, _x, _pb[:, 384:448], ALU.add),
                   reads=[f"BXv{u}", kB], writes=[f"Rt{u}"])
            p.group("T", [
                (lambda te, _r=Rt, _p=PT, _b=b0, _pb=pbB: te.matmul(
                    _pb[_b:_b + 64, 0:128], _r, _p, start=True, stop=False)),
                (lambda te, hh=h, _q=QtT, _b=b0, _pb=pbB: te.matmul(
                    _pb[_b:_b + 64, 0:128], hm(slocb, hh), _q, start=False, stop=True))],
                reads=[f"Rt{u}", f"PT{u}", f"sl{h}", f"QtT{u}"], writes=[kB])
            p.emit("S", lambda s, ap=Bview(h, c), _b=b0, _pb=pbB: s.copy(ap, _pb[_b:_b + 64, 0:128]),
                   reads=[kB], writes=[kk_])
            grc = stats[:, 28 + 4 * par:29 + 4 * par]
            p.emit("S", lambda s, l=lgc_, lc=lgcc_, _g=grc: s.activation(
                _g, l, AF.Exp, scale=-1.0, bias=lc),
                reads=[f"lgcol{c}", f"lgcc{c}"], writes=[f"grc{u}"])
            p.emit("V", lambda v, _k=Kc, _o=KG, _g=grc: v.tensor_scalar(_o, _k, _g, None, ALU.mult),
                   reads=[f"Kc{u}", f"grc{u}"], writes=[f"KG{u}"])
            p.group("T", [lambda te, _w=BW, _k=KG, _b=b0, _pa=pbA: te.matmul(
                _pa[_b:_b + 64, 384:448], _w, _k, start=True, stop=True)],
                reads=[f"BW{u}", f"KG{u}"], writes=[kA])
            p.emit("S", lambda s, lc=lgcc_, g=stats[:, 52 + 4 * par:53 + 4 * par]: s.activation(
                g, lc, AF.Exp),
                reads=[f"lgcc{c}"], writes=[f"gccs{u}"])
            p.emit("V", lambda v, _a=AcT, g=gcc_, _b=b0, _pa=pbA: v.scalar_tensor_tensor(
                _a, identf[_b:_b + 64, _b:_b + 64], g, _pa[_b:_b + 64, 384:448],
                ALU.mult, ALU.add),
                reads=[kA, f"gccs{u}", "identf"], writes=[f"AcT{u}"])
            p.group("T", [lambda te, hh=h, _b=b0: te.transpose(
                pbtB[_b:_b + 64, 0:64], hm(pib, hh), identb[_b:_b + 64, _b:_b + 64])],
                reads=[f"pi{h}", "identb"], writes=["pbtB"])
            p.emit("V", lambda v, hh=h, cc=c, _b=b0: v.tensor_copy(psvv(hh, cc), pbtB[_b:_b + 64, 0:64]),
                   reads=["pbtB"], writes=[f"psv{h}_{c}"])
            p.group("T", [lambda te, _a=AcT, hh=h, _b=b0, _pa=pbA: te.matmul(
                _pa[_b:_b + 64, 192:256], _a, hm(pib, hh), start=True, stop=True)],
                reads=[f"AcT{u}", f"pi{h}"], writes=[kA])
            p.emit("V", lambda v, hh=h, _b=b0, _pa=pbA: v.tensor_copy(hm(pib, hh), _pa[_b:_b + 64, 192:256]),
                   reads=[kA], writes=[f"pi{h}"])
            p.group("T", [
                (lambda te, _k=KG, _x=BXv, _b=b0, _pa=pbA: te.matmul(
                    _pa[_b:_b + 64, 448:512], _k, _x, start=True, stop=False)),
                (lambda te, _a=AcT, hh=h, _b=b0, _pa=pbA: te.matmul(
                    _pa[_b:_b + 64, 448:512], _a, hm(slocb, hh), start=False, stop=True))],
                reads=[f"KG{u}", f"BXv{u}", f"AcT{u}", f"sl{h}"], writes=[kA])
            p.emit("V", lambda v, hh=h, _b=b0, _pa=pbA: v.tensor_copy(hm(slocb, hh), _pa[_b:_b + 64, 448:512]),
                   reads=[kA], writes=[f"sl{h}"])

    for c in range(_NTU):
        for hp in range(0, H, 2):
            p.cap = []
            emit_unit(c, hp)
            l0 = p.cap
            p.cap = []
            emit_unit(c, hp + 1)
            l1 = p.cap
            p.cap = None
            p.replay_rr([l0, l1])

    for h in range(H):
        b0 = pr(h)
        p.group("T", [lambda te, hh=h, _b=b0: te.transpose(
            pbtB[_b:_b + 64, 64:128], hm(pib, hh), identb[_b:_b + 64, _b:_b + 64])],
            reads=[f"pi{h}", "identb"], writes=["pbtB"])
        p.emit("V", lambda v, hh=h, _b=b0: v.tensor_copy(
            BUF_C[_b:_b + 64, (hh // 2) * 64:(hh // 2) * 64 + 64],
            pbtB[_b:_b + 64, 64:128]),
            reads=["pbtB"], writes=[f"pif{h}"])
    for pa in range(2):
        p.emit("P", lambda g, _p=pa: g.dma_start(
            out=cc_in_d[_p * 8:(_p + 1) * 8, :].rearrange("m (a b) -> a m b", a=64),
            in_=BUF_C[_p * 64:_p * 64 + 64, 0:512].rearrange("a (m b) -> a m b", m=8)),
            reads=[f"pif{h}" for h in range(H)], writes=["ccin"], dma="dp")
        p.emit("P", lambda g, _p=pa: g.dma_start(
            out=cc_in_d[16 + _p * 8:16 + (_p + 1) * 8, :].rearrange("m (a b) -> a m b", a=64),
            in_=slocb[_p * 64:_p * 64 + 64, :].rearrange("a (m b) -> a m b", m=8)),
            reads=[f"sl{h}" for h in range(H)], writes=["ccin"], dma="dp")
    import os as _os
    if _os.environ.get("KSKIP_CC"):
        for _r in range(4):
            p.emit("P", lambda g, rr=_r: g.dma_start(out=cc_out_d[rr * 32:(rr + 1) * 32, :], in_=cc_in_d),
                   reads=["ccin"], writes=["ccout"], dma="dp")
    else:
        p.emit("P", lambda g: g.collective_compute(
            "AllGather", ALU.bypass, replica_groups=[[0, 1, 2, 3], [4, 5, 6, 7]],
            ins=[cc_in_d.opt()], outs=[cc_out_d.opt()]),
            reads=["ccin"], writes=["ccout"], collective=True)
    GB0 = 512
    for j4 in range(4):
        for kind in range(2):
            for pa in range(2):
                p.emit("P", lambda g, jj=j4, kd=kind, _p=pa: g.dma_start(
                    out=BUF_C[_p * 64:_p * 64 + 64,
                              GB0 + (jj * 16 + kd * 8) * 64:GB0 + (jj * 16 + kd * 8 + 8) * 64
                              ].rearrange("a (m b) -> a m b", m=8),
                    in_=cc_out_d[jj * 32 + kd * 16 + _p * 8: jj * 32 + kd * 16 + (_p + 1) * 8, :
                                 ].rearrange("m (a b) -> a m b", a=64)),
                    reads=["ccout"], writes=[f"G{j4}_{kind}_{pa}"], dma="dp")

    p.retoken([f"G{j}_{k}_{pa}" for j in range(4) for k in range(2) for pa in range(2)])

    def gmat(j4, kind, h):
        base = GB0 + (j4 * 16 + kind * 8 + h // 2) * 64
        return BUF_C[pr(h):pr(h) + 64, base:base + 64]

    for h in range(H):
        b0 = pr(h)
        TMA = SCR_F[b0:b0 + 64, 0:64]
        AeffT = SCR_B[b0:b0 + 64, 4672:4736]
        p.emit("V", lambda v, hh=h: v.memset(hm(s0b, hh), 0.0), writes=[f"s0{h}"])
        gks = [f"G{j}_{k}_{h % 2}" for j in range(4) for k in range(2)]
        for j4 in range(4):
            p.emit("V", lambda v, jj=j4, hh=h, _t=TMA, _b=b0: v.tensor_scalar(
                _t, gmat(jj, 0, hh), cmask[_b:_b + 64, jj:jj + 1], None, ALU.mult),
                reads=gks + ["cmask"], writes=["TMA"])
            p.emit("V", lambda v, jj=j4, _t=TMA, _a=AeffT, _b=b0: v.scalar_tensor_tensor(
                _a, identf[_b:_b + 64, _b:_b + 64], cmask[_b:_b + 64, 4 + jj:5 + jj], _t,
                ALU.mult, ALU.add),
                reads=["TMA", "cmask", "identf"], writes=["AeffT"])
            p.group("T", [lambda te, hh=h, _a=AeffT, _b=b0: te.matmul(
                pb[3 * (hh % 2)][_b:_b + 64, 320:384], _a, hm(s0b, hh), start=True, stop=True)],
                reads=["AeffT", f"s0{h}"], writes=[f"pb{3 * (h % 2)}"])
            p.emit("V", lambda v, jj=j4, hh=h, _t=TMA, _b=b0: v.tensor_scalar(
                _t, gmat(jj, 1, hh), cmask[_b:_b + 64, jj:jj + 1], None, ALU.mult),
                reads=gks + ["cmask"], writes=["TMA"])
            p.emit("V", lambda v, hh=h, _t=TMA, _b=b0: v.tensor_tensor(
                hm(s0b, hh), _t, pb[3 * (hh % 2)][_b:_b + 64, 320:384], ALU.add),
                reads=["TMA", f"pb{3 * (h % 2)}"], writes=[f"s0{h}"])
    for c in range(_NTU):
        for h in range(H):
            b0 = pr(h)
            DSb = SCR_B[b0:b0 + 64, 4736:4800]
            p.group("T", [lambda te, hh=h, cc=c, _b=b0: te.matmul(
                pb[3 * (hh % 2)][_b:_b + 64, 320:384], psvv(hh, cc), hm(s0b, hh), start=True, stop=True)],
                reads=[f"psv{h}_{c}", f"s0{h}"], writes=[f"pb{3 * (h % 2)}"])
            p.emit("V", lambda v, _d=DSb, _b=b0, hh=h: v.tensor_copy(
                _d, pb[3 * (hh % 2)][_b:_b + 64, 320:384]),
                   reads=[f"pb{3 * (h % 2)}"], writes=["DSb"])
            p.group("T", [lambda te, hh=h, cc=c, _d=DSb, _b=b0: te.matmul(
                pb[3 * (hh % 2) + 1][_b:_b + 64, 256:384], _d, Aview(hh, cc), start=True, stop=True)],
                reads=["DSb", f"A{h}_{c}"], writes=[f"pb{3 * (h % 2) + 1}"])
            p.emit("V", lambda v, hh=h, cc=c, _b=b0: v.tensor_tensor(
                Bview(hh, cc), Bview(hh, cc), pb[3 * (hh % 2) + 1][_b:_b + 64, 256:384], ALU.add),
                reads=[f"B{h}_{c}", f"pb{3 * (h % 2) + 1}"], writes=[f"B{h}_{c}"])

    if debug == "A2":
        p.fence()
        add_dbg("OT", [128, 16384])
        p.emit("P", lambda g: g.dma_start(out=dbg["OT"], in_=BUF_B[:, :]), dma="dbg")
        add_dbg("JT", [128, 16384])
        p.emit("P", lambda g: g.dma_start(out=dbg["JT"], in_=BUF_A[:, :]), dma="dbg")
        add_dbg("s0b", [128, 512])
        p.emit("P", lambda g: g.dma_start(out=dbg["s0b"], in_=s0b[:, :]), dma="dbg")
        add_dbg("slocb", [128, 512])
        p.emit("P", lambda g: g.dma_start(out=dbg["slocb"], in_=slocb[:, :]), dma="dbg")
        p.fence(); p.final_waits()
        return _finish(nc, p, ctx, dbg)

    p.fence()

    bob_bf = SCR_B[:, 0:1024]
    p.emit("P", lambda g: g.dma_start(out=bob_bf, in_=bob_d), writes=["bob"], dma="dp")

    def wochk(par):
        return BUF_A[:, 8192 + par * 4096: 8192 + (par + 1) * 4096]

    for chn in range(2):
        par = chn % 2
        p.emit("P", lambda g, c=chn, pa=par: g.dma_start(
            out=wochk(pa).rearrange("p (k f) -> p k f", k=DT),
            in_=wo_d[:, c * 512:(c + 1) * 512].rearrange("(k p) f -> p k f", p=128)),
            writes=[f"woch{par}"], dma=f"w{par}")
        for t in range(NT):
            bank = t % 2
            p.group("T", [
                (lambda te, kk=kk, pa=par, tg=t, b=bank: te.matmul(
                    pb[b][:, 0:512],
                    BUF_B[:, kk * TOK + tg * 128: kk * TOK + (tg + 1) * 128],
                    wochk(pa)[:, kk * 512:(kk + 1) * 512],
                    start=(kk == 0), stop=(kk == DT - 1)))
                for kk in range(DT)],
                reads=[f"woch{par}"] + [f"B{hh}_{t}" for hh in range(H)],
                writes=[f"pb{bank}"])
            p.emit("V", lambda v, tg=t, c=chn, b=bank: v.tensor_tensor(
                xv(tg)[:, c * 512:(c + 1) * 512], xv(tg)[:, c * 512:(c + 1) * 512],
                pb[b][:, 0:512], ALU.add),
                reads=[f"pb{bank}", f"x{t}"], writes=[f"x{t}"])
    for t in range(NT):
        p.emit("V", lambda v, tg=t: v.tensor_tensor(xv(tg), xv(tg), bob_bf, ALU.add),
               reads=["bob", f"x{t}"], writes=[f"x{t}"])

    if debug == "A3":
        p.fence()
        add_dbg("x1", [128, NT * D])
        p.emit("Y", lambda y: y.dma_start(out=dbg["x1"], in_=BUF_X[:, :]), dma=True)
        p.fence(); p.final_waits()
        return _finish(nc, p, ctx, dbg)

    p.fence()

    def w2t(r):
        if r < 16:
            return BUF_B[:, r * 1024:(r + 1) * 1024]
        return BUF_C[:, (r - 16) * 1024:(r - 16 + 1) * 1024]

    for r in range(32):
        p.emit("P", lambda g, rr=r: g.dma_start(
            out=w2t(rr), in_=w2_d[rr * 128:(rr + 1) * 128, :]),
            writes=[f"w2c{r}"], dma="dp")
    b2b_bf = SCR_B[:, 7168:8192]
    p.emit("P", lambda g: g.dma_start(out=b2b_bf, in_=b2b_d), writes=["b2b"], dma="dp")
    p.retoken([f"w2c{r}" for r in range(32)] + ["b2b"])

    def xn2T(dt_, j):
        return SCR_B[:, dt_ * 256 + j * 128: dt_ * 256 + (j + 1) * 128]

    xntr2 = SCR_B[:, 2048:3072]

    def w1ch(par):
        return SCR_B[:, 3072 + par * 2048: 3072 + (par + 1) * 2048]

    def h1t(ft):
        return PSV[:, ft * 256: (ft + 1) * 256]

    for hmc in range(8):
        tt0 = hmc * 2
        for j in range(2):
            t = tt0 + j
            emit_ln(t, xntr2[:, :], [f"xn2_{t}"])
            for dt_ in range(DT):
                p.group("T", [lambda te, d=dt_: te.transpose(
                    pbtA[:, 0:128], xntr2[:, d * 128:(d + 1) * 128], identb[:, :])],
                    reads=[f"xn2_{t}", "identb"], writes=["pbtA"])
                p.emit("V", lambda v, d=dt_, jj=j: v.tensor_copy(xn2T(d, jj), pbtA[:, 0:128]),
                       reads=["pbtA"], writes=[f"x2T{t}_{dt_}"])
        xnk = [f"x2T{tt0 + j}_{d}" for j in range(2) for d in range(DT)]
        for chn in range(16):
            par = chn % 2
            p.emit("P", lambda g, c=chn, pa=par: g.dma_start(
                out=w1ch(pa).rearrange("p (k f) -> p k f", k=DT),
                in_=w1_d[:, c * 256:(c + 1) * 256].rearrange("(k p) f -> p k f", p=128)),
                writes=[f"w1ch{par}"], dma=f"w{par}")
            if hmc == 0:
                for sub in range(2):
                    ft = chn * 2 + sub
                    p.group("T", [
                        (lambda te, kk=kk, s_=sub, pa=par, f=ft: te.matmul(
                            pb[4][:, (f % 2):(f % 2) + 1],
                            w1ch(pa)[:, kk * 256 + s_ * 128: kk * 256 + (s_ + 1) * 128],
                            b2lb[:, kk:kk + 1], start=(kk == 0), stop=(kk == DT - 1)))
                        for kk in range(DT)],
                        reads=[f"w1ch{par}", "b2lb"], writes=["pb4"])
                    p.emit("V", lambda v, f=ft: v.tensor_tensor(
                        b1tot[:, f:f + 1], b1c[:, f:f + 1], pb[4][:, (f % 2):(f % 2) + 1], ALU.add),
                        reads=["pb4", "b1c"], writes=["b1tot"])
            for kk in range(DT):
                p.emit("V", lambda v, k2=kk, pa=par: v.tensor_scalar(
                    w1ch(pa)[:, k2 * 256:(k2 + 1) * 256], w1ch(pa)[:, k2 * 256:(k2 + 1) * 256],
                    g2c[:, k2:k2 + 1], None, ALU.mult),
                    reads=[f"w1ch{par}", "g2c"], writes=[f"w1ch{par}"])
            for sub in range(2):
                ft = chn * 2 + sub
                bank = ft % 2
                p.group("T", [
                    (lambda te, kk=kk, s_=sub, pa=par, b=bank: te.matmul(
                        pb[b][:, 0:256],
                        w1ch(pa)[:, kk * 256 + s_ * 128: kk * 256 + (s_ + 1) * 128],
                        SCR_B[:, kk * 256:(kk + 1) * 256],
                        start=(kk == 0), stop=(kk == DT - 1)))
                    for kk in range(DT)],
                    reads=[f"w1ch{par}"] + xnk, writes=[f"pb{bank}"])
                p.emit("S", lambda s, f=ft, b=bank: s.activation(
                    h1t(f), pb[b][:, 0:256], AF.Gelu, bias=b1tot[:, f:f + 1]),
                    reads=[f"pb{bank}", "b1tot"], writes=[f"h1_{ft}"])
        for j in range(2):
            t = tt0 + j
            for dc in range(2):
                bank = 2 + (j * 2 + dc) % 2
                p.group("T", [
                    (lambda te, kk=kk, jj=j, d=dc, b=bank: te.matmul(
                        pb[b][:, 0:512], h1t(kk)[:, jj * 128:(jj + 1) * 128],
                        w2t(kk)[:, d * 512:(d + 1) * 512],
                        start=(kk == 0), stop=(kk == 31)))
                    for kk in range(32)],
                    reads=[f"h1_{kk}" for kk in range(32)] + [f"w2c{kk}" for kk in range(32)],
                    writes=[f"pb{bank}"])
                p.emit("V", lambda v, tg=t, d=dc, b=bank: v.tensor_tensor(
                    xv(tg)[:, d * 512:(d + 1) * 512], xv(tg)[:, d * 512:(d + 1) * 512],
                    pb[b][:, 0:512], ALU.add),
                    reads=[f"pb{bank}", f"x{t}"], writes=[f"x{t}"])
            p.emit("V", lambda v, tg=t: v.tensor_tensor(xv(tg), xv(tg), b2b_bf, ALU.add),
                   reads=["b2b", f"x{t}"], writes=[f"x{t}"])

    if debug == "A4":
        p.fence()
        add_dbg("x1", [128, NT * D])
        p.emit("Y", lambda y: y.dma_start(out=dbg["x1"], in_=BUF_X[:, :]), dma=True)
        p.fence(); p.final_waits()
        return _finish(nc, p, ctx, dbg)

    p.fence()

    BUF_Bf = BUF_B[:, :].bitcast(F32)
    BUF_Cf = BUF_C[:, :].bitcast(F32)

    def xorig(t):
        if t < 8:
            return BUF_Bf[:, t * D:(t + 1) * D]
        return BUF_Cf[:, (t - 8) * D:(t - 7) * D]

    p.emit("Y", lambda y: y.dma_start(
        out=BUF_Bf[:, 0:8 * D].rearrange("p (t d) -> p t d", t=8),
        in_=x_d[0:8 * 128, :].rearrange("(t p) d -> p t d", p=128)),
        writes=[f"xo{t}" for t in range(8)], dma=True)
    p.emit("Y", lambda y: y.dma_start(
        out=BUF_Cf[:, 0:8 * D].rearrange("p (t d) -> p t d", t=8),
        in_=x_d[8 * 128:16 * 128, :].rearrange("(t p) d -> p t d", p=128)),
        writes=[f"xo{t}" for t in range(8, 16)], dma=True)
    p.retoken([f"xo{t}" for t in range(NT)])
    xtr = SCR_B[:, 0:1024]
    for t in range(NT):
        p.emit("V", lambda v, tg=t: v.tensor_tensor(xv(tg), xv(tg), xorig(tg), ALU.add),
               reads=[f"x{t}", f"xo{t}"], writes=[f"x{t}"])
        emit_ln(t, xtr[:, :], [f"x2b{t}"])
        p.emit("V", lambda v, tg=t: v.tensor_scalar(
            xv(tg), xv(tg), stats[:, 0:1], stats[:, 4:5], ALU.add, ALU.mult),
            reads=[f"x{t}", "st_neg", "st_rst"], writes=[f"x{t}"])
        for dt_ in range(DT):
            p.group("T", [lambda te, d=dt_: te.transpose(
                pbtA[:, 0:128], xtr[:, d * 128:(d + 1) * 128], identb[:, :])],
                reads=[f"x2b{t}", "identb"], writes=["pbtA"])
            p.emit("V", lambda v, d=dt_, tg=t: v.tensor_copy(
                BUF_A[:, d * TOK + tg * 128: d * TOK + (tg + 1) * 128], pbtA[:, 0:128]),
                reads=["pbtA"], writes=[f"xT2_{t}_{dt_}"])

    if debug == "A5":
        p.fence()
        add_dbg("x2", [128, NT * D])
        p.emit("Y", lambda y: y.dma_start(out=dbg["x2"], in_=BUF_X[:, :]), dma=True)
        p.fence(); p.final_waits()
        return _finish(nc, p, ctx, dbg)

    p.fence()

    for t in range(NT):
        bank = t % 2
        p.group("T", [
            (lambda te, kk=kk, tg=t, b=bank: te.matmul(
                pb[b][:, 0:NE], BUF_A[:, kk * TOK + tg * 128: kk * TOK + (tg + 1) * 128],
                gwc[:, kk * NE:(kk + 1) * NE], start=(kk == 0), stop=(kk == DT - 1)))
            for kk in range(DT)],
            reads=["gwc"] + [f"xT2_{t}_{d}" for d in range(DT)], writes=[f"pb{bank}"])
        sc = SCR_F[:, 0:8]; es = SCR_F[:, 8:16]; c1 = SCR_F[:, 16:24]; c2 = SCR_F[:, 24:32]
        es2 = SCR_F[:, 32:40]
        m1 = stats[:, 36:37]; m2 = stats[:, 40:41]; rs = stats[:, 44:45]; rw = stats[:, 48:49]
        p.emit("V", lambda v, b=bank: v.tensor_tensor(sc, pb[b][:, 0:NE], gbb[:, :], ALU.add),
               reads=[f"pb{bank}", "gbb"], writes=["g_sc"])
        p.emit("V", lambda v: v.reduce_max(m1, sc, axis=AX.X, negate=True),
               reads=["g_sc"], writes=["g_m1"])
        p.emit("S", lambda s: s.activation(es, sc, AF.Exp, bias=m1),
               reads=["g_sc", "g_m1"], writes=["g_es"])
        p.emit("V", lambda v: v.reduce_sum(m2, es, axis=AX.X), reads=["g_es"], writes=["g_sm"])
        p.emit("V", lambda v: v.reciprocal(rs, m2), reads=["g_sm"], writes=["g_rs"])
        p.emit("V", lambda v: v.tensor_scalar(es, es, rs, None, ALU.mult),
               reads=["g_es", "g_rs"], writes=["g_es"])
        p.emit("V", lambda v: v.reduce_max(m1, es, axis=AX.X), reads=["g_es"], writes=["g_m1"])
        p.emit("V", lambda v: v.tensor_scalar(c1, es, m1, None, ALU.is_ge),
               reads=["g_es", "g_m1"], writes=["g_c1"])
        p.emit("V", lambda v: v.tensor_copy(c2, c1), reads=["g_c1"], writes=["g_c2"])
        p.emit("V", lambda v: v.tensor_tensor(c2[:, 1:8], c2[:, 1:8], c1[:, 0:7], ALU.add),
               reads=["g_c1", "g_c2"], writes=["g_c2"])
        p.emit("V", lambda v: v.tensor_copy(c1, c2), reads=["g_c2"], writes=["g_c1b"])
        p.emit("V", lambda v: v.tensor_tensor(c2[:, 2:8], c2[:, 2:8], c1[:, 0:6], ALU.add),
               reads=["g_c1b", "g_c2"], writes=["g_c2"])
        p.emit("V", lambda v: v.tensor_copy(c1, c2), reads=["g_c2"], writes=["g_c1b"])
        p.emit("V", lambda v: v.tensor_tensor(c2[:, 4:8], c2[:, 4:8], c1[:, 0:4], ALU.add),
               reads=["g_c1b", "g_c2"], writes=["g_c2"])
        p.emit("V", lambda v: v.tensor_scalar(c2, c2, 1.0, None, ALU.is_equal),
               reads=["g_c2"], writes=["g_c2"])
        p.emit("V", lambda v: v.tensor_scalar(c1, es, m1, None, ALU.is_ge),
               reads=["g_es", "g_m1"], writes=["g_c1"])
        p.emit("V", lambda v: v.tensor_tensor(c1, c1, c2, ALU.mult),
               reads=["g_c1", "g_c2"], writes=["g_c1"])
        p.emit("V", lambda v: v.scalar_tensor_tensor(es2, c1, -1e30, es, ALU.mult, ALU.add),
               reads=["g_c1", "g_es"], writes=["g_es2"])
        p.emit("V", lambda v: v.reduce_max(m2, es2, axis=AX.X), reads=["g_es2"], writes=["g_m2"])
        p.emit("V", lambda v: v.tensor_scalar(c2, es2, m2, None, ALU.is_ge),
               reads=["g_es2", "g_m2"], writes=["g_c2"])
        p.emit("V", lambda v: v.tensor_tensor(rs, m1, m2, ALU.add),
               reads=["g_m1", "g_m2"], writes=["g_rs"])
        p.emit("V", lambda v: v.tensor_scalar(rs, rs, 1e-20, None, ALU.add),
               reads=["g_rs"], writes=["g_rs"])
        p.emit("V", lambda v: v.reciprocal(rw, rs), reads=["g_rs"], writes=["g_rw"])
        p.emit("V", lambda v: v.tensor_scalar(c1, c1, m1, None, ALU.mult),
               reads=["g_c1", "g_m1"], writes=["g_c1"])
        p.emit("V", lambda v: v.tensor_scalar(c2, c2, m2, None, ALU.mult),
               reads=["g_c2", "g_m2"], writes=["g_c2"])
        p.emit("V", lambda v: v.tensor_tensor(c1, c1, c2, ALU.add),
               reads=["g_c1", "g_c2"], writes=["g_c1"])
        p.emit("V", lambda v, tg=t: v.tensor_scalar(
            comb[:, tg * 8:(tg + 1) * 8], c1, rw, None, ALU.mult),
            reads=["g_c1", "g_rw"], writes=[f"comb{t}"])

    if debug == "B1":
        p.fence()
        add_dbg("comb", [128, 128])
        p.emit("Y", lambda y: y.dma_start(out=dbg["comb"], in_=comb[:, :]), dma=True)
        p.final_waits()
        return _finish(nc, p, ctx, dbg)

    def moebuf(par):
        return BUF_C if par == 0 else SCR_B

    def dwc(par, k):
        i = par * 4 + k
        return PSV[:, i * 1024:(i + 1) * 1024]

    def hreg(ft, sub):
        return BUF_B[:, ft * 2048 + sub * 128: ft * 2048 + (sub + 1) * 128]

    PRD = [SCR_F[:, 0:512], SCR_F[:, 0:512]]
    dbbt = BUF_B[:, 8192:9216]
    UPS = [BUF_C[:, 8192 + i * 512: 8192 + (i + 1) * 512] for i in range(2)]

    for e in range(NE):
        ep = e % 2
        p.emit("P", lambda g, ee=e, pa=ep: g.dma_start(
            out=upbt[:, pa * FT:(pa + 1) * FT], in_=upb_d[ee]),
            writes=[f"upb{ep}"], dma=f"eb{ep}")
        p.emit("P", lambda g, ee=e, pa=ep: g.dma_start(
            out=gpbt[:, pa * FT:(pa + 1) * FT], in_=gpb_d[ee]),
            writes=[f"gpb{ep}"], dma=f"eb{ep}")
        p.retoken([f"upb{ep}", f"gpb{ep}"])
        p.emit("P", lambda g, ee=e: g.dma_start(out=dbbt[:, :], in_=dbb_d[ee]),
            writes=["dbb"], dma="dbbs")
        for ci, (f0, fw) in enumerate(MOE_CHUNKS):
            par = ci % 2
            nk = fw // 128
            p.emit("P", lambda g, ee=e, a=f0, w=fw, pa=par: g.dma_start(
                out=moebuf(pa)[:, 0:DT * w].rearrange("p (k f) -> p k f", k=DT),
                in_=upw_d[ee, :, a:a + w].rearrange("(k p) f -> p k f", p=128)),
                writes=[f"upw{par}"], dma=f"m{par}")
            p.emit("P", lambda g, ee=e, a=f0, w=fw, pa=par: g.dma_start(
                out=moebuf(pa)[:, 4096:4096 + DT * w].rearrange("p (k f) -> p k f", k=DT),
                in_=gpw_d[ee, :, a:a + w].rearrange("(k p) f -> p k f", p=128)),
                writes=[f"gpw{par}"], dma=f"m{par}")
            for k2 in range(nk):
                p.emit("P", lambda g, ee=e, a=f0, kk=k2, pa=par: g.dma_start(
                    out=dwc(pa, kk), in_=dwt_d[ee, a + kk * 128:a + (kk + 1) * 128, :]),
                    writes=[f"dw{par}_{k2}"], dma=f"m{par}")
            p.retoken([f"upw{par}", f"gpw{par}"] + [f"dw{par}_{k2}" for k2 in range(nk)])
            for ft in range(nk):
                ftg = f0 // 128 + ft
                for mc in range(4):
                    bank = mc % 2
                    bank2 = 2 + mc % 2
                    p.group("T", [
                        (lambda te, kk=kk, f=ft, m=mc, b=bank, pa=par, w=fw: te.matmul(
                            pb[b][:, 0:512], moebuf(pa)[:, kk * w + f * 128: kk * w + (f + 1) * 128],
                            BUF_A[:, kk * TOK + m * 512:kk * TOK + (m + 1) * 512],
                            start=(kk == 0), stop=(kk == DT - 1)))
                        for kk in range(DT)],
                        reads=[f"upw{par}"], writes=[f"pb{bank}"])
                    p.group("T", [
                        (lambda te, kk=kk, f=ft, m=mc, b=bank2, pa=par, w=fw: te.matmul(
                            pb[b][:, 0:512],
                            moebuf(pa)[:, 4096 + kk * w + f * 128: 4096 + kk * w + (f + 1) * 128],
                            BUF_A[:, kk * TOK + m * 512:kk * TOK + (m + 1) * 512],
                            start=(kk == 0), stop=(kk == DT - 1)))
                        for kk in range(DT)],
                        reads=[f"gpw{par}"], writes=[f"pb{bank2}"])
                    ub = UPS[mc % 2]
                    p.emit("S", lambda s, _u=ub, b=bank, f=ftg, pa=ep: s.activation(
                        _u, pb[b][:, 0:512], AF.Identity, bias=upbt[:, pa * FT + f:pa * FT + f + 1]),
                        reads=[f"pb{bank}", f"upb{ep}"], writes=[f"UPS{mc % 2}"])
                    p.emit("V", lambda v, _u=ub, b=bank2, f=ftg, pa=ep, m=mc: v.scalar_tensor_tensor(
                        PRD[m % 2], pb[b][:, 0:512], gpbt[:, pa * FT + f:pa * FT + f + 1], _u,
                        ALU.add, ALU.mult),
                        reads=[f"pb{bank2}", f"gpb{ep}", f"UPS{mc % 2}"], writes=[f"PRD{mc % 2}"])
                    p.emit("S", lambda s, f=ft, m=mc: s.activation(
                        BUF_B[:, f * 2048 + m * 512: f * 2048 + (m + 1) * 512],
                        PRD[m % 2], AF.Silu),
                        reads=[f"PRD{mc % 2}"], writes=[f"h{ft}_{mc}"])
            hkeys = [f"h{ft}_{m}" for ft in range(nk) for m in range(4)]
            for sub in range(NT):
                for dc in range(2):
                    bank = 4 + (sub * 2 + dc) % 2
                    p.group("T", [
                        (lambda te, kk=kk, s_=sub, d=dc, b=bank, pa=par: te.matmul(
                            pb[b][:, 0:512], hreg(kk, s_), dwc(pa, kk)[:, d * 512:(d + 1) * 512],
                            start=(kk == 0), stop=(kk == nk - 1)))
                        for kk in range(nk)],
                        reads=hkeys + [f"dw{par}_{kk}" for kk in range(nk)],
                        writes=[f"pb{bank}"])
                    p.emit("V", lambda v, s_=sub, d=dc, b=bank, ee=e: v.scalar_tensor_tensor(
                        xv(s_)[:, d * 512:(d + 1) * 512], pb[b][:, 0:512],
                        comb[:, s_ * 8 + ee: s_ * 8 + ee + 1],
                        xv(s_)[:, d * 512:(d + 1) * 512], ALU.mult, ALU.add),
                        reads=[f"pb{bank}", f"comb{sub}", f"x{sub}"], writes=[f"x{sub}"])
        for sub in range(NT):
            p.emit("V", lambda v, s_=sub, ee=e: v.scalar_tensor_tensor(
                xv(s_), dbbt[:, :],
                comb[:, s_ * 8 + ee: s_ * 8 + ee + 1], xv(s_), ALU.mult, ALU.add),
                reads=["dbb", f"comb{sub}", f"x{sub}"], writes=[f"x{sub}"])

    if debug == "B2":
        p.fence()
        add_dbg("acc", [128, NT * D])
        p.emit("Y", lambda y: y.dma_start(out=dbg["acc"], in_=BUF_X[:, :]), dma=True)
        p.final_waits()
        return _finish(nc, p, ctx, dbg)

    p.fence()

    for t in range(NT):
        emit_ln(t, SCR_B[:, 0:1024], [f"fln{t}"])
        p.emit("V", lambda v, tg=t: v.tensor_scalar(
            xv(tg), xv(tg), stats[:, 0:1], stats[:, 4:5], ALU.add, ALU.mult),
            reads=[f"x{t}", "st_neg", "st_rst"], writes=[f"x{t}"])
    p.emit("Y", lambda y: y.dma_start(
        out=out_d.rearrange("(t p) d -> p t d", p=128),
        in_=BUF_X[:, :].rearrange("p (t d) -> p t d", t=NT)),
        reads=[f"x{t}" for t in range(NT)], dma=True)
    p.final_waits()
    return _finish(nc, p, ctx, dbg)


def _finish(nc, p, ctx, dbg):
    with nc.Block() as block:
        p.run(block)
    for t in reversed(ctx):
        t.__exit__(None, None, None)
    return nc, sorted(dbg.keys())


def prep_inputs(inputs, core):
    ii = {}
    b, q = core // 4, core % 4
    ii["x"] = np.ascontiguousarray(inputs["L0_attn_x"][b, q * TOK:(q + 1) * TOK, :],
                                   dtype=np.float32)
    ii["wq"] = inputs["L0_attn_Wq"]; ii["wk"] = inputs["L0_attn_Wk"]; ii["wv"] = inputs["L0_attn_Wv"]
    ii["wb"] = inputs["L0_attn_Wb"]; ii["wg"] = inputs["L0_attn_Wg"]; ii["wo"] = inputs["L0_attn_Wo"]
    ii["w1"] = inputs["L0_attn_w1"]; ii["w2"] = inputs["L0_attn_w2"]
    ii["gwt"] = np.ascontiguousarray(inputs["L0_ffn_gate_w"].T)
    upw = np.zeros((NE, D, FP), dtype=np.float32)
    upw[:, :, :F_RAW] = inputs["L0_ffn_up_w"].transpose(0, 2, 1)
    gpw = np.zeros((NE, D, FP), dtype=np.float32)
    gpw[:, :, :F_RAW] = inputs["L0_ffn_gp_w"].transpose(0, 2, 1)
    dwt = np.zeros((NE, FP, D), dtype=np.float32)
    dwt[:, :F_RAW, :] = inputs["L0_ffn_down_w"].transpose(0, 2, 1)
    ii["upw"] = upw; ii["gpw"] = gpw; ii["dwt"] = dwt
    ub = np.zeros((NE, FP), dtype=np.float32); ub[:, :F_RAW] = inputs["L0_ffn_up_b"]
    gb = np.zeros((NE, FP), dtype=np.float32); gb[:, :F_RAW] = inputs["L0_ffn_gp_b"]
    ii["upb"] = np.ascontiguousarray(ub.reshape(NE, FT, 128).transpose(0, 2, 1))
    ii["gpb"] = np.ascontiguousarray(gb.reshape(NE, FT, 128).transpose(0, 2, 1))
    ii["dbb"] = np.ascontiguousarray(
        np.broadcast_to(inputs["L0_ffn_down_b"][:, None, :], (NE, 128, D)))
    ii["bob"] = np.ascontiguousarray(np.broadcast_to(inputs["L0_attn_bo"][None, :], (128, D)))
    ii["b2b"] = np.ascontiguousarray(np.broadcast_to(inputs["L0_attn_b2"][None, :], (128, D)))
    ii["b1c"] = np.ascontiguousarray(inputs["L0_attn_b1"].reshape(FFN // 128, 128).T)
    ii["g1c"] = np.ascontiguousarray(inputs["L0_attn_ln1_g"].reshape(DT, 128).T)
    ii["b1lc"] = np.ascontiguousarray(inputs["L0_attn_ln1_b"].reshape(DT, 128).T)
    ii["g2c"] = np.ascontiguousarray(inputs["L0_attn_ln2_g"].reshape(DT, 128).T)
    ii["b2lc"] = np.ascontiguousarray(inputs["L0_attn_ln2_b"].reshape(DT, 128).T)
    ii["gbb"] = np.ascontiguousarray(np.broadcast_to(inputs["L0_ffn_gate_b"][None, :], (128, NE)))
    ii["identf"] = np.eye(128, dtype=np.float32)
    ii["identb"] = np.eye(128).astype(BFD)
    ii["miu"] = np.triu(np.ones((128, 128), dtype=np.float32))
    ii["ones128"] = np.ones((128, 128), dtype=np.float32)
    ii["msu"] = np.triu(np.ones((128, 128)), 1).astype(BFD)
    ii["miub"] = np.triu(np.ones((128, 128))).astype(BFD)
    ii["onesb"] = np.ones((128, 128)).astype(BFD)
    cm = np.zeros((128, 8), dtype=np.float32)
    for j in range(4):
        cm[:, j] = 1.0 if j < q else 0.0
        cm[:, 4 + j] = 0.0 if j < q else 1.0
    ii["cmask"] = cm
    for k in ii:
        if ii[k].dtype == np.float64:
            ii[k] = ii[k].astype(np.float32)
        ii[k] = np.ascontiguousarray(ii[k])
    return ii


_CACHE = {}


def kernel(**inputs):
    if "nc" not in _CACHE:
        _CACHE["nc"] = build_nc()
    nc, _ = _CACHE["nc"]
    in_maps = [prep_inputs(inputs, core) for core in range(8)]
    res = run_bass_kernel_spmd(nc, in_maps, core_ids=list(range(8)))
    outs = [np.asarray(res.results[core]["out"]) for core in range(8)]
    full = np.stack([np.concatenate(outs[0:4], 0), np.concatenate(outs[4:8], 0)], 0)
    return full.astype(np.float32)


# version 09
# speedup vs baseline: 1.1825x; 1.1825x over previous
import numpy as np
import ml_dtypes
import concourse.bass as bass
import concourse.mybir as mybir
from concourse.bass_utils import run_bass_kernel_spmd

F32 = mybir.dt.float32
BF16 = mybir.dt.bfloat16
AF = mybir.ActivationFunctionType
ALU = mybir.AluOpType
AX = mybir.AxisListType
BFD = ml_dtypes.bfloat16

TOK = 2048
NT = 16
D = 1024
DT = 8
H = 16
DH = 64
FFN = 4096
NE = 8
F_RAW = 3264
FP = 3328
FT = FP // 128
NMAC = 4
EPS = 1e-5
MOE_CHUNKS = [(i * 512, 512) for i in range(6)] + [(3072, 256)]


class Prog:
    ENGS = ("T", "V", "S", "P", "Y")

    DMA_STREAMS = ("dy", "dp", "w0", "w1", "m0", "m1", "eb0", "eb1", "dbbs", "dbg")

    def __init__(self):
        self.instrs = {e: [] for e in Prog.ENGS}
        self.cnt = {k: 0 for k in (("T", "V", "S", "P", "Y", "CC") + Prog.DMA_STREAMS)}
        self.seen = {e: {} for e in Prog.ENGS}
        self.last_w = {}
        self.readers = {}
        self.sems = {}

    def _need(self, need, tok):
        k, v = tok
        if need.get(k, 0) < v:
            need[k] = v

    def _deps(self, reads, writes):
        need = {}
        for r in reads:
            t = self.last_w.get(r)
            if t:
                self._need(need, t)
        for w in writes:
            t = self.last_w.get(w)
            if t:
                self._need(need, t)
            for t in self.readers.get(w, ()):
                self._need(need, t)
        return need

    SELF_EDGE_SKIP = {("T", "T")}

    def _emit_waits(self, eng, need, self_key=None):
        for k, v in sorted(need.items()):
            if (eng, k) in Prog.SELF_EDGE_SKIP:
                continue
            if k == eng and self.cnt[k] < v:
                continue
            if self.seen[eng].get(k, 0) >= v:
                continue
            self.seen[eng][k] = v
            self.instrs[eng].append(("w", k, v))

    def _record(self, tok, reads, writes):
        for w in writes:
            self.last_w[w] = tok
            self.readers[w] = []
        for r in reads:
            self.readers.setdefault(r, []).append(tok)

    def emit(self, eng, fn, reads=(), writes=(), dma=False, collective=False):
        if getattr(self, "cap", None) is not None:
            self.cap.append(("e", eng, fn, reads, writes, dma, collective))
            return None
        need = self._deps(reads, writes)
        if collective:
            key = "CC"
        elif dma:
            key = dma if isinstance(dma, str) else "dy"
        else:
            key = eng
        self._emit_waits(eng, need)
        amt = 16 if dma else 1
        self.cnt[key] += amt
        tok = (key, self.cnt[key])
        self.instrs[eng].append(("i", fn, key, amt))
        self._record(tok, reads, writes)
        return tok

    def replay_rr(self, lists, block=4):
        idx = [0] * len(lists)
        while True:
            done = True
            for li, l in enumerate(lists):
                for _ in range(block):
                    if idx[li] >= len(l):
                        break
                    done = False
                    it = l[idx[li]]
                    idx[li] += 1
                    if it[0] == "e":
                        _, eng, fn, reads, writes, dma, coll = it
                        self.emit(eng, fn, reads, writes, dma, coll)
                    else:
                        _, eng, fns, reads, writes = it
                        self.group(eng, fns, reads, writes)
            if done:
                break

    def retoken(self, keys):
        toks = [self.last_w[k] for k in keys if k in self.last_w]
        if not toks:
            return
        best = max(toks, key=lambda t: t[1])
        for k in keys:
            if k in self.last_w:
                self.last_w[k] = best

    def group(self, eng, fns, reads=(), writes=()):
        if getattr(self, "cap", None) is not None:
            self.cap.append(("g", eng, fns, reads, writes))
            return None
        need = self._deps(reads, writes)
        self._emit_waits(eng, need)
        for fn in fns[:-1]:
            self.instrs[eng].append(("q", fn))
        self.cnt[eng] += 1
        tok = (eng, self.cnt[eng])
        self.instrs[eng].append(("i", fns[-1], eng, 1))
        self._record(tok, reads, writes)
        return tok

    def fence(self):
        snap = {k: v for k, v in self.cnt.items() if v > 0}
        for e in Prog.ENGS:
            self._emit_waits(e, snap)
        self.last_w = {}
        self.readers = {}

    def final_waits(self):
        snap = {k: v for k, v in self.cnt.items() if v > 0}
        for k, v in sorted(snap.items()):
            if k == "Y":
                continue
            self.instrs["Y"].append(("w", k, v))

    def run(self, block):
        sems = self.sems

        def runner(eng_obj, lst):
            for it in lst:
                if it[0] == "w":
                    eng_obj.wait_ge(sems[it[1]], it[2])
                elif it[0] == "q":
                    it[1](eng_obj)
                else:
                    _, fn, key, amt = it
                    fn(eng_obj).then_inc(sems[key], amt)

        @block.tensor
        def _(t):
            runner(t, self.instrs["T"])

        @block.vector
        def _(v):
            runner(v, self.instrs["V"])

        @block.scalar
        def _(s):
            runner(s, self.instrs["S"])

        @block.gpsimd
        def _(g):
            runner(g, self.instrs["P"])

        @block.sync
        def _(y):
            runner(y, self.instrs["Y"])


def build_nc(debug=None):
    nc = bass.Bass("TRN2", target_bir_lowering=False)
    p = Prog()

    def din(name, shape, dt=F32):
        return nc.dram_tensor(name, shape, dt, kind="ExternalInput").ap()

    x_d = din("x", [TOK, D])
    wq_d = din("wq", [D, D]); wk_d = din("wk", [D, D]); wv_d = din("wv", [D, D])
    wb_d = din("wb", [D, H]); wg_d = din("wg", [D, H]); wo_d = din("wo", [D, D])
    w1_d = din("w1", [D, FFN]); w2_d = din("w2", [FFN, D])
    gwt_d = din("gwt", [D, NE])
    upw_d = din("upw", [NE, D, FP]); gpw_d = din("gpw", [NE, D, FP])
    dwt_d = din("dwt", [NE, FP, D])
    upb_d = din("upb", [NE, 128, FT]); gpb_d = din("gpb", [NE, 128, FT])
    dbb_d = din("dbb", [NE, 128, D])
    bob_d = din("bob", [128, D]); b2b_d = din("b2b", [128, D])
    b1c_d = din("b1c", [128, FFN // 128])
    g1c_d = din("g1c", [128, DT]); b1lc_d = din("b1lc", [128, DT])
    g2c_d = din("g2c", [128, DT]); b2lc_d = din("b2lc", [128, DT])
    gbb_d = din("gbb", [128, NE])
    identf_d = din("identf", [128, 128])
    identb_d = din("identb", [128, 128], BF16)
    miu_d = din("miu", [128, 128])
    ones128_d = din("ones128", [128, 128])
    msu_d = din("msu", [128, 128], BF16)
    miub_d = din("miub", [128, 128], BF16)
    onesb_d = din("onesb", [128, 128], BF16)
    cmask_d = din("cmask", [128, 8])
    out_d = nc.dram_tensor("out", [TOK, D], F32, kind="ExternalOutput").ap()
    cc_in_d = nc.dram_tensor("cc_in", [32, DH * DH], F32).ap()
    cc_out_d = nc.dram_tensor("cc_out", [128, DH * DH], F32).ap()

    dbg = {}

    def add_dbg(name, shape, dt=F32):
        dbg[name] = nc.dram_tensor("dbg_" + name, shape, dt, kind="ExternalOutput").ap()
        return dbg[name]

    ctx = []

    def sb(name, shape, dt):
        t = nc.sbuf_tensor("sb_" + name, shape, dt)
        ctx.append(t)
        return t.__enter__()

    def psmt(name, shape, dt):
        t = nc.psum_tensor(name, shape, dt)
        ctx.append(t)
        return t.__enter__()

    BUF_X = sb("BUF_X", [128, NT * D], F32)
    BUF_A = sb("BUF_A", [128, 16384], BF16)
    BUF_B = sb("BUF_B", [128, 16384], BF16)
    BUF_C = sb("BUF_C", [128, 16384], BF16)
    SCR_B = sb("SCR_B", [128, 8192], BF16)
    PSV = sb("PSV", [128, 8192], BF16)
    SCR_F = sb("SCR_F", [128, 896], F32)
    lgcol = sb("lgcol", [128, 256], F32)
    lgcc = sb("lgcc", [128, 256], F32)
    bcol = sb("bcol", [128, 256], F32)
    slocb = sb("slocb", [128, 512], BF16)
    pib = sb("pib", [128, 512], BF16)
    s0b = sb("s0b", [128, 512], BF16)
    comb = sb("comb", [128, 128], F32)
    identf = sb("identf", [128, 128], F32)
    identb = sb("identb", [128, 128], BF16)
    miu_f = sb("miu_f", [128, 128], F32)
    ones128 = sb("ones128", [128, 128], F32)
    msu = sb("msu", [128, 128], BF16)
    miu_b = sb("miu_b", [128, 128], BF16)
    onesb = sb("onesb", [128, 128], BF16)
    cmask = sb("cmask", [128, 8], F32)
    b1c = sb("b1c", [128, FFN // 128], F32)
    b1tot = sb("b1tot", [128, FFN // 128], F32)
    g1c = sb("g1c", [128, DT], F32); b1lc = sb("b1lc", [128, DT], F32)
    g2c = sb("g2c", [128, DT], F32); b2lc = sb("b2lc", [128, DT], F32)
    gbb = sb("gbb", [128, NE], F32)
    upbt = sb("upbt", [128, 2 * FT], F32)
    gpbt = sb("gpbt", [128, 2 * FT], F32)
    gwc = sb("gwc", [128, DT * NE], BF16)
    stats = sb("stats", [128, 64], F32)
    bqkv = sb("bqkv", [128, 2 * DT], F32)
    zbias = sb("zbias", [128, 2 * H], F32)
    onec = sb("onec", [128, 1], F32)
    b1lb = sb("b1lb", [128, DT], BF16)
    b2lb = sb("b2lb", [128, DT], BF16)

    pb = [psmt(f"pb{i}", [128, 512], F32) for i in range(6)]
    pbtA = psmt("pbtA", [128, 1024], BF16)
    pbtB = psmt("pbtB", [128, 1024], BF16)

    for k in ("T", "V", "S", "P", "Y", "CC") + Prog.DMA_STREAMS:
        h_ = nc.semaphore("sem_" + k)
        ctx.append(h_)
        p.sems[k] = h_.__enter__()

    def xv(t):
        return BUF_X[:, t * D:(t + 1) * D]

    def Aview(h, c):
        return BUF_A[(h % 2) * 64:(h % 2) * 64 + 64,
                     (h // 2) * TOK + c * 128:(h // 2) * TOK + (c + 1) * 128]

    def Bview(h, c):
        return BUF_B[(h % 2) * 64:(h % 2) * 64 + 64,
                     (h // 2) * TOK + c * 128:(h // 2) * TOK + (c + 1) * 128]

    def vview(h, c):
        return BUF_C[:, c * D + h * 64: c * D + (h + 1) * 64]

    def hm(buf, h):
        return buf[(h % 2) * 64:(h % 2) * 64 + 64, (h // 2) * 64:(h // 2) * 64 + 64]

    def psvv(h, c):
        return PSV[(h % 2) * 64:(h % 2) * 64 + 64,
                   ((h // 2) * 16 + c) * 64:((h // 2) * 16 + c) * 64 + 64]

    def cs(buf, c, h):
        return buf[:, c * H + h: c * H + h + 1]

    def pr(h):
        return (h % 2) * 64

    for dst, src, key in [
        (identf[:, :], identf_d, "identf"), (identb[:, :], identb_d, "identb"),
        (miu_f[:, :], miu_d, "miu_f"), (ones128[:, :], ones128_d, "ones128"),
        (msu[:, :], msu_d, "msu"), (miu_b[:, :], miub_d, "miu_b"),
        (onesb[:, :], onesb_d, "onesb"), (cmask[:, :], cmask_d, "cmask"),
        (b1c[:, :], b1c_d, "b1c"), (g1c[:, :], g1c_d, "g1c"),
        (b1lc[:, :], b1lc_d, "b1lc"), (g2c[:, :], g2c_d, "g2c"),
        (b2lc[:, :], b2lc_d, "b2lc"), (gbb[:, :], gbb_d, "gbb"),
    ]:
        p.emit("Y", lambda y, d=dst, s=src: y.dma_start(out=d, in_=s), writes=[key], dma=True)
    p.emit("P", lambda g: g.dma_start(out=gwc[:, :].rearrange("p (k f) -> p k f", k=DT),
                                      in_=gwt_d.rearrange("(k p) f -> p k f", p=128)),
           writes=["gwc"], dma="dp")
    p.emit("Y", lambda y: y.dma_start(
        out=BUF_X[:, :].rearrange("p (t d) -> p t d", t=NT),
        in_=x_d.rearrange("(t p) d -> p t d", p=128)),
        writes=[f"x{t}" for t in range(NT)], dma=True)
    p.retoken(["identf", "identb", "miu_f", "ones128", "msu", "miu_b", "onesb", "cmask",
               "b1c", "g1c", "b1lc", "g2c", "b2lc", "gbb"] + [f"x{t}" for t in range(NT)])
    p.emit("V", lambda v: v.memset(onec[:, :], 1.0), writes=["onec"])
    p.emit("V", lambda v: v.tensor_copy(b1lb[:, :], b1lc[:, :]), reads=["b1lc"], writes=["b1lb"])
    p.emit("V", lambda v: v.tensor_copy(b2lb[:, :], b2lc[:, :]), reads=["b2lc"], writes=["b2lb"])

    def emit_ln(t, out_ap, out_keys):
        neg = stats[:, 0:1]; rst = stats[:, 4:5]
        sa = stats[:, 12:13]; sbb = stats[:, 16:17]; tmp = stats[:, 8:9]
        p.emit("V", lambda v: v.reduce_sum(neg, xv(t), axis=AX.X, negate=True),
               reads=[f"x{t}"], writes=["st_neg"])
        p.emit("V", lambda v: v.tensor_scalar(neg, neg, 1.0 / D, None, ALU.mult),
               reads=["st_neg"], writes=["st_neg"])
        p.emit("S", lambda s: s.activation(pb[5][:, 0:512], xv(t)[:, 0:512], AF.Square,
                                           accum_out=sa),
               reads=[f"x{t}"], writes=["pb5", "st_sa"])
        p.emit("S", lambda s: s.activation(pb[5][:, 0:512], xv(t)[:, 512:1024], AF.Square,
                                           accum_out=sbb),
               reads=[f"x{t}"], writes=["pb5", "st_sb"])
        p.emit("V", lambda v: v.tensor_tensor(tmp, sa, sbb, ALU.add),
               reads=["st_sa", "st_sb"], writes=["st_tmp"])
        p.emit("V", lambda v: v.tensor_scalar(tmp, tmp, 1.0 / D, None, ALU.mult),
               reads=["st_tmp"], writes=["st_tmp"])
        p.emit("V", lambda v: v.scalar_tensor_tensor(tmp, neg, neg, tmp, ALU.mult,
                                                     ALU.subtract),
               reads=["st_neg", "st_tmp"], writes=["st_tmp2"])
        p.emit("V", lambda v: v.tensor_scalar(tmp, tmp, -1.0, EPS, ALU.mult, ALU.add),
               reads=["st_tmp2"], writes=["st_tmp"])
        p.emit("V", lambda v: v.tensor_scalar(tmp, tmp, 1e-12, None, ALU.max),
               reads=["st_tmp"], writes=["st_tmp"])
        p.emit("S", lambda s: s.activation(rst, tmp, AF.Ln), reads=["st_tmp"], writes=["st_rst"])
        p.emit("S", lambda s: s.activation(rst, rst, AF.Exp, scale=-0.5),
               reads=["st_rst"], writes=["st_rst"])
        p.emit("V", lambda v: v.tensor_scalar(out_ap, xv(t), neg, rst, ALU.add, ALU.mult),
               reads=[f"x{t}", "st_neg", "st_rst"], writes=list(out_keys))

    def xnT(dt_, j):
        return SCR_B[:, dt_ * 512 + j * 128: dt_ * 512 + (j + 1) * 128]

    def wch(par):
        return SCR_B[:, 4096 + par * 2048: 4096 + (par + 1) * 2048]

    xntr = PSV[:, 1024:2048]
    vbias = PSV[:, 0:1024]
    GPAT = PSV[:, 2560:4608]
    GPATZ = PSV[:, 4608:4736]
    for kk in range(DT):
        for hh2 in range(2):
            p.emit("V", lambda v, k2=kk, j2=hh2: v.tensor_scalar(
                GPAT[:, k2 * 256 + j2 * 128:k2 * 256 + (j2 + 1) * 128], onesb[:, 0:128],
                g1c[:, k2:k2 + 1], None, ALU.mult),
                reads=["g1c", "onesb"], writes=["GPAT"])
        p.emit("V", lambda v, k2=kk: v.tensor_scalar(
            GPATZ[:, k2 * H:(k2 + 1) * H], onesb[:, 0:H],
            g1c[:, k2:k2 + 1], None, ALU.mult),
            reads=["g1c", "onesb"], writes=["GPATZ"])
    zcol_b = SCR_F[:, 320:384]
    zcol_g = SCR_F[:, 384:448]
    rowtmp = SCR_F[0:1, 448:576]

    for mac in range(NMAC):
        tt0 = mac * 4
        for j in range(4):
            t = tt0 + j
            emit_ln(t, xntr[:, :], [f"xn{t}"])
            for dt_ in range(DT):
                p.group("T", [lambda te, d=dt_: te.transpose(
                    pbtA[:, 0:128], xntr[:, d * 128:(d + 1) * 128], identb[:, :])],
                    reads=[f"xn{t}", "identb"], writes=["pbtA"])
                p.emit("V", lambda v, d=dt_, jj=j: v.tensor_copy(xnT(d, jj), pbtA[:, 0:128]),
                       reads=["pbtA"], writes=[f"xnT{t}_{dt_}"])
        xnk = [f"xnT{tt0 + j}_{d}" for j in range(4) for d in range(DT)]

        for wi, (wname, wdram) in enumerate((("q", wq_d), ("k", wk_d))):
            for chn in range(4):
                par = chn % 2
                p.emit("P", lambda g, w=wdram, c=chn, pa=par: g.dma_start(
                    out=wch(pa).rearrange("p (k f) -> p k f", k=DT),
                    in_=w[:, c * 256:(c + 1) * 256].rearrange("(k p) f -> p k f", p=128)),
                    writes=[f"wch{par}"], dma=f"w{par}")
                if mac == 0:
                    for sub in range(2):
                        do = chn * 2 + sub
                        p.group("T", [
                            (lambda te, kk=kk, s_=sub, pa=par, d_o=do: te.matmul(
                                pb[4][:, (d_o % 2):(d_o % 2) + 1],
                                wch(pa)[:, kk * 256 + s_ * 128: kk * 256 + (s_ + 1) * 128],
                                b1lb[:, kk:kk + 1], start=(kk == 0), stop=(kk == DT - 1)))
                            for kk in range(DT)],
                            reads=[f"wch{par}", "b1lb"], writes=["pb4"])
                        p.emit("V", lambda v, w=wi, d_o=do: v.tensor_copy(
                            bqkv[:, w * DT + d_o: w * DT + d_o + 1], pb[4][:, (d_o % 2):(d_o % 2) + 1]),
                            reads=["pb4"], writes=[f"bq{wi}"])
                p.emit("V", lambda v, pa=par: v.tensor_tensor(
                    wch(pa)[:, :], wch(pa)[:, :], GPAT[:, :], ALU.mult),
                    reads=[f"wch{par}", "GPAT"], writes=[f"wch{par}"])
                for sub in range(2):
                    do = chn * 2 + sub
                    bank = do % 2
                    p.group("T", [
                        (lambda te, kk=kk, s_=sub, pa=par, b=bank: te.matmul(
                            pb[b][:, 0:512],
                            wch(pa)[:, kk * 256 + s_ * 128: kk * 256 + (s_ + 1) * 128],
                            SCR_B[:, kk * 512:(kk + 1) * 512],
                            start=(kk == 0), stop=(kk == DT - 1)))
                        for kk in range(DT)],
                        reads=[f"wch{par}"] + xnk, writes=[f"pb{bank}"])
                    dst = (BUF_A if wname == "q" else BUF_B)
                    wk_keys = [f"{'A' if wname == 'q' else 'B'}{2 * do + i}_{tt0 + j}"
                               for i in range(2) for j in range(4)]
                    p.emit("S", lambda s, d_o=do, b=bank, w=wi, _d=dst, _t0=tt0: s.activation(
                        _d[:, d_o * TOK + _t0 * 128: d_o * TOK + _t0 * 128 + 512],
                        pb[b][:, 0:512], AF.Identity,
                        bias=bqkv[:, w * DT + d_o: w * DT + d_o + 1]),
                        reads=[f"pb{bank}", f"bq{wi}"], writes=wk_keys)

        for chn in range(4):
            par = chn % 2
            p.emit("P", lambda g, c=chn, pa=par: g.dma_start(
                out=wch(pa).rearrange("p (k f) -> p k f", k=DT),
                in_=wv_d[:, c * 256:(c + 1) * 256].rearrange("(k p) f -> p k f", p=128)),
                writes=[f"wch{par}"], dma=f"w{par}")
            if mac == 0:
                p.group("T", [
                    (lambda te, kk=kk, pa=par: te.matmul(
                        pb[4][0:1, 0:256], b1lb[:, kk:kk + 1], wch(pa)[:, kk * 256:(kk + 1) * 256],
                        start=(kk == 0), stop=(kk == DT - 1)))
                    for kk in range(DT)],
                    reads=[f"wch{par}", "b1lb"], writes=["pb4"])
                p.emit("V", lambda v: v.tensor_copy(PSV[0:1, 2048:2176], pb[4][0:1, 0:128]),
                       reads=["pb4"], writes=["vbr"])
                p.emit("V", lambda v: v.tensor_copy(PSV[0:1, 2176:2304], pb[4][0:1, 128:256]),
                       reads=["pb4"], writes=["vbr2"])
                p.group("T", [lambda te: te.matmul(
                    pb[5][:, 256:384], onesb[0:1, 0:128], PSV[0:1, 2048:2176], start=True, stop=True)],
                    reads=["vbr", "onesb"], writes=["pb5"])
                p.emit("V", lambda v, c=chn: v.tensor_copy(vbias[:, c * 256:c * 256 + 128],
                                                           pb[5][:, 256:384]),
                       reads=["pb5"], writes=["vbias"])
                p.group("T", [lambda te: te.matmul(
                    pb[5][:, 384:512], onesb[0:1, 0:128], PSV[0:1, 2176:2304], start=True, stop=True)],
                    reads=["vbr2", "onesb"], writes=["pb5"])
                p.emit("V", lambda v, c=chn: v.tensor_copy(vbias[:, c * 256 + 128:(c + 1) * 256],
                                                           pb[5][:, 384:512]),
                       reads=["pb5"], writes=["vbias"])
            p.emit("V", lambda v, pa=par: v.tensor_tensor(
                wch(pa)[:, :], wch(pa)[:, :], GPAT[:, :], ALU.mult),
                reads=[f"wch{par}", "GPAT"], writes=[f"wch{par}"])
            for j in range(4):
                t = tt0 + j
                bank = 2 + j % 2
                p.group("T", [
                    (lambda te, kk=kk, pa=par, jj=j, b=bank: te.matmul(
                        pb[b][:, 0:256], xnT(kk, jj), wch(pa)[:, kk * 256:(kk + 1) * 256],
                        start=(kk == 0), stop=(kk == DT - 1)))
                    for kk in range(DT)],
                    reads=[f"wch{par}"] + xnk, writes=[f"pb{bank}"])
                p.emit("V", lambda v, tg=t, c=chn, b=bank: v.tensor_tensor(
                    BUF_C[:, tg * D + c * 256: tg * D + (c + 1) * 256],
                    pb[b][:, 0:256], vbias[:, c * 256:(c + 1) * 256], ALU.add),
                    reads=[f"pb{bank}", "vbias"], writes=[f"C{t}_{chn}"])

        for zi, (zdram, zc) in enumerate(((wb_d, zcol_b), (wg_d, zcol_g))):
            par = zi
            p.emit("P", lambda g, w=zdram, pa=par: g.dma_start(
                out=wch(pa)[:, 0:DT * H].rearrange("p (k f) -> p k f", k=DT),
                in_=w.rearrange("(k p) f -> p k f", p=128)),
                writes=[f"wch{par}"], dma=f"w{par}")
            if mac == 0:
                p.group("T", [
                    (lambda te, kk=kk, pa=par: te.matmul(
                        pb[4][0:1, 256:256 + H], b1lb[:, kk:kk + 1], wch(pa)[:, kk * H:(kk + 1) * H],
                        start=(kk == 0), stop=(kk == DT - 1)))
                    for kk in range(DT)],
                    reads=[f"wch{par}", "b1lb"], writes=["pb4"])
                p.emit("V", lambda v: v.tensor_copy(PSV[0:1, 2304:2304 + H], pb[4][0:1, 256:256 + H]),
                       reads=["pb4"], writes=["zbr"])
                p.group("T", [lambda te: te.matmul(
                    pb[5][:, 256:256 + H], onesb[0:1, 0:128], PSV[0:1, 2304:2304 + H], start=True, stop=True)],
                    reads=["zbr", "onesb"], writes=["pb5"])
                p.emit("V", lambda v, z=zi: v.tensor_copy(zbias[:, z * H:(z + 1) * H],
                                                          pb[5][:, 256:256 + H]),
                       reads=["pb5"], writes=[f"zbias{zi}"])
            p.emit("V", lambda v, pa=par: v.tensor_tensor(
                wch(pa)[:, 0:DT * H], wch(pa)[:, 0:DT * H], GPATZ[:, :], ALU.mult),
                reads=[f"wch{par}", "GPATZ"], writes=[f"wch{par}"])
            for j in range(4):
                bank = 4 + zi
                p.group("T", [
                    (lambda te, kk=kk, pa=par, jj=j, b=bank: te.matmul(
                        pb[b][:, 0:H], xnT(kk, jj), wch(pa)[:, kk * H:(kk + 1) * H],
                        start=(kk == 0), stop=(kk == DT - 1)))
                    for kk in range(DT)],
                    reads=[f"wch{par}"] + xnk, writes=[f"pb{bank}"])
                p.emit("V", lambda v, jj=j, b=bank, z=zi, _zc=zc: v.tensor_tensor(
                    _zc[:, jj * H:(jj + 1) * H], pb[b][:, 0:H], zbias[:, z * H:(z + 1) * H], ALU.add),
                    reads=[f"pb{bank}", f"zbias{zi}"], writes=[f"z{zi}_{j}"])
        for zi, zc in ((0, zcol_b), (1, zcol_g)):
            a_ = SCR_F[:, 0:64]; e_ = SCR_F[:, 64:128]; l_ = SCR_F[:, 128:192]
            zk = [f"z{zi}_{j}" for j in range(4)]
            p.emit("S", lambda s, _zc=zc: s.activation(a_, _zc[:, :], AF.Abs),
                   reads=zk, writes=["ls_a"])
            p.emit("S", lambda s: s.activation(e_, a_, AF.Exp, scale=-1.0),
                   reads=["ls_a"], writes=["ls_e"])
            p.emit("S", lambda s: s.activation(l_, e_, AF.Ln, bias=onec[:, 0:1]),
                   reads=["ls_e", "onec"], writes=["ls_l"])
            p.emit("V", lambda v, _zc=zc: v.tensor_scalar(a_, _zc[:, :], 0.0, None, ALU.min),
                   reads=zk, writes=["ls_a"])
            p.emit("V", lambda v, _zc=zc: v.tensor_tensor(_zc[:, :], a_, l_, ALU.subtract),
                   reads=["ls_a", "ls_l"], writes=zk)
        p.emit("S", lambda s, m=mac: s.activation(bcol[:, m * 64:(m + 1) * 64], zcol_b[:, :], AF.Exp),
               reads=[f"z0_{j}" for j in range(4)], writes=[f"bcol{mac}"])
        for j in range(4):
            t = tt0 + j
            p.group("T", [lambda te, jj=j: te.matmul(
                pb[5][:, 0:H], miu_f[:, :], zcol_g[:, jj * H:(jj + 1) * H], start=True, stop=True)],
                reads=[f"z1_{j}", "miu_f"], writes=["pb5"])
            p.emit("V", lambda v, tg=t: v.tensor_copy(lgcol[:, tg * H:(tg + 1) * H], pb[5][:, 0:H]),
                   reads=["pb5"], writes=[f"lgcol{t}"])
            p.group("T", [lambda te, jj=j: te.matmul(
                pb[5][:, H:2 * H], ones128[:, :], zcol_g[:, jj * H:(jj + 1) * H], start=True, stop=True)],
                reads=[f"z1_{j}", "ones128"], writes=["pb5"])
            p.emit("V", lambda v, tg=t: v.tensor_copy(lgcc[:, tg * H:(tg + 1) * H], pb[5][:, H:2 * H]),
                   reads=["pb5"], writes=[f"lgcc{t}"])

    if debug == "A1":
        p.fence()
        for nm, buf in (("qT", BUF_A), ("kT", BUF_B), ("v", BUF_C)):
            dd = add_dbg(nm, [128, 16384])
            p.emit("P", lambda g, d=dd, b=buf: g.dma_start(out=d, in_=b[:, :]), dma="dbg")
        for nm, buf in (("lgcol", lgcol), ("lgcc", lgcc)):
            dd = add_dbg(nm, [128, 256])
            p.emit("Y", lambda y, d=dd, b=buf: y.dma_start(out=d, in_=b[:, :]), dma=True)
        add_dbg("bcol", [128, 256])
        p.emit("P", lambda g: g.dma_start(out=dbg["bcol"], in_=bcol[:, :]), dma="dbg")
        add_dbg("scrb", [128, 8192])
        p.emit("P", lambda g: g.dma_start(out=dbg["scrb"], in_=SCR_B[:, :]), dma="dbg")
        add_dbg("stats", [128, 64])
        p.emit("Y", lambda y: y.dma_start(out=dbg["stats"], in_=stats[:, :]), dma=True)
        add_dbg("xbuf", [128, NT * D])
        p.emit("Y", lambda y: y.dma_start(out=dbg["xbuf"], in_=BUF_X[:, :]), dma=True)
        p.fence(); p.final_waits()
        return _finish(nc, p, ctx, dbg)

    p.fence()

    def scrb(par, a, b):
        return SCR_B[:, par * 2048 + a: par * 2048 + b]

    p.emit("V", lambda v: v.memset(slocb[:, :], 0.0), writes=[f"sl{h}" for h in range(H)])
    for h in range(H):
        p.emit("V", lambda v, hh=h: v.tensor_copy(
            hm(pib, hh), identf[pr(hh):pr(hh) + 64, pr(hh):pr(hh) + 64]),
            reads=["identf"], writes=[f"pi{h}"])

    import os as _os
    _NTU = int(_os.environ.get("KUNITS", NT))

    def emit_unit(c, h):
            par = (c * H + h) % 2
            b0 = pr(h)
            TMP1 = SCR_F[:, 0:128] if par == 0 else SCR_F[:, 768:896]
            pbA = pb[3 * par]
            pbB = pb[3 * par + 1]
            pbC = pb[3 * par + 2]
            kA = f"pb{3 * par}"; kB = f"pb{3 * par + 1}"; kC = f"pb{3 * par + 2}"
            DMP = SCR_F[:, 128 + par * 128:256 + par * 128]
            DM = DMP
            XF = SCR_F[:, 384 + par * 128:512 + par * 128]
            RN = SCR_F[b0:b0 + 1, 640:768]
            POW = [scrb(par, i * 128, (i + 1) * 128) for i in range(7)]
            TT = scrb(par, 896, 1024)
            khT = SCR_B[b0:b0 + 64, par * 2048 + 1024: par * 2048 + 1152]
            QtT = SCR_B[b0:b0 + 64, par * 2048 + 1152: par * 2048 + 1280]
            Kc = scrb(par, 1280, 1344)
            KG = scrb(par, 1344, 1408)
            BXv = scrb(par, 1408, 1472)
            BW = scrb(par, 1472, 1536)
            BWT = SCR_B[b0:b0 + 64, par * 2048 + 1536: par * 2048 + 1664]
            Rt = scrb(par, 1664, 1728)
            PT = scrb(par, 1728, 1856)
            Gb64 = SCR_B[b0:b0 + 64, par * 2048 + 1856: par * 2048 + 1984]
            K2 = SCR_B[b0:b0 + 64, 4096:4224]
            XB = SCR_B[:, 4224 + par * 128: 4352 + par * 128]
            RNb = SCR_B[b0:b0 + 1, 4480:4608]
            AcT = SCR_B[b0:b0 + 64, 4608:4672]
            u = f"{par}"
            kq = f"A{h}_{c}"; kk_ = f"B{h}_{c}"; kv_ = f"C{c}_{h // 4}"
            lgc_ = cs(lgcol, c, h); bc_ = cs(bcol, c, h); lgcc_ = cs(lgcc, c, h)
            gcc_ = stats[b0:b0 + 64, 52 + 4 * par:53 + 4 * par]

            p.emit("V", lambda v, l=lgc_: v.tensor_scalar(TMP1, ones128[:, :], l, None, ALU.mult),
                   reads=[f"lgcol{c}", "ones128"], writes=[f"TMP1{par}"])
            p.group("T", [lambda te, _pa=pbA: te.transpose(_pa[:, 0:128], TMP1, identf[:, :])],
                    reads=[f"TMP1{par}", "identf"], writes=[kA])
            p.emit("V", lambda v, l=lgc_, _d=DMP, _pa=pbA: v.tensor_scalar(
                _d, _pa[:, 0:128], l, 0.0, ALU.subtract, ALU.min),
                reads=[kA, f"lgcol{c}"], writes=[f"DMP{u}"])
            p.emit("S", lambda s, _d=DMP: s.activation(_d, _d, AF.Exp),
                   reads=[f"DMP{u}"], writes=[f"DM{u}"])
            p.emit("S", lambda s, _g=Gb64, _b=b0, _pa=pbA: s.activation(_g, _pa[_b:_b + 64, 0:128], AF.Exp),
                   reads=[kA], writes=[f"Gb{u}"])
            p.emit("V", lambda v, _q=QtT, _g=Gb64, ap=Aview(h, c): v.scalar_tensor_tensor(
                _q, ap, 0.125, _g, ALU.mult, ALU.mult),
                reads=[kq, f"Gb{u}"], writes=[f"QtT{u}"])
            p.emit("V", lambda v, _k=K2, ap=Bview(h, c): v.tensor_tensor(_k, ap, ap, ALU.mult),
                   reads=[kk_], writes=[f"K2{u}"])
            p.group("T", [lambda te, _k=K2, _b=b0, _pa=pbA: te.matmul(
                _pa[_b:_b + 1, 128:256], onesb[_b:_b + 64, 0:1], _k, start=True, stop=True)],
                reads=[f"K2{u}", "onesb"], writes=[kA])
            p.emit("S", lambda s, _r=RN, _b=b0, _pa=pbA: s.activation(_r, _pa[_b:_b + 1, 128:256], AF.Ln),
                   reads=[kA], writes=[f"RN{u}"])
            p.emit("S", lambda s, _r=RN: s.activation(_r, _r, AF.Exp, scale=0.5),
                   reads=[f"RN{u}"], writes=[f"RN{u}"])
            p.emit("V", lambda v, _r=RN: v.tensor_scalar(_r, _r, 1e-6, None, ALU.add),
                   reads=[f"RN{u}"], writes=[f"RN{u}"])
            def _recip(v, _r=RN, _o=RNb):
                with nc.allow_low_precision(reason="bf16 k-norm reciprocal row"):
                    return v.reciprocal(_o, _r)
            p.emit("V", _recip, reads=[f"RN{u}"], writes=[f"RNb{u}"])
            p.group("T", [lambda te, _o=RNb, _b=b0, _pa=pbA: te.matmul(
                _pa[_b:_b + 64, 256:384], onesb[_b:_b + 1, 0:64], _o, start=True, stop=True)],
                reads=[f"RNb{u}", "onesb"], writes=[kA])
            p.emit("V", lambda v, _k=khT, ap=Bview(h, c), _b=b0, _pa=pbA: v.tensor_tensor(
                _k, ap, _pa[_b:_b + 64, 256:384], ALU.mult),
                reads=[kk_, kA], writes=[f"khT{u}"])
            p.group("T", [lambda te, _k=khT, _pb=pbB: te.matmul(_pb[:, 0:128], _k, _k, start=True, stop=True)],
                    reads=[f"khT{u}"], writes=[kB])
            p.emit("V", lambda v, _d=DM, _pb=pbB: v.tensor_tensor(TMP1, _pb[:, 0:128], _d, ALU.mult),
                   reads=[kB, f"DM{u}", f"TMP1{par}"], writes=[f"TMP1{par}"])
            p.emit("V", lambda v: v.tensor_tensor(TMP1, TMP1, msu[:, :], ALU.mult),
                   reads=[f"TMP1{par}", "msu"], writes=[f"TMP1{par}"])
            p.emit("V", lambda v, _p=POW[0], b=bc_: v.tensor_scalar(
                _p, TMP1, b, -1.0, ALU.mult, ALU.mult),
                reads=[f"TMP1{par}", f"bcol{c // 4}"], writes=[f"POW0{u}"])
            DPW = [SCR_B[:, 4800 + par * 768 + i * 128: 4800 + par * 768 + (i + 1) * 128]
                   for i in range(6)]
            p.group("T", [lambda te, _p=POW[0]: te.transpose(pbtA[:, 0:128], _p, identb[:, :])],
                    reads=[f"POW0{u}", "identb"], writes=["pbtA"])
            p.emit("V", lambda v, _d=DPW[0]: v.tensor_copy(_d, pbtA[:, 0:128]),
                   reads=["pbtA"], writes=[f"DP0{u}"])
            for jj in range(6):
                p.group("T", [lambda te, _d=DPW[jj], _p=POW[jj], _pc=pbC: te.matmul(
                    _pc[:, 0:128], _d, _p, start=True, stop=True)],
                    reads=[f"DP{jj}{u}", f"POW{jj}{u}"], writes=[kC])
                p.emit("V", lambda v, _p=POW[jj + 1], _pc=pbC: v.tensor_copy(_p, _pc[:, 0:128]),
                       reads=[kC], writes=[f"POW{jj + 1}{u}"])
                if jj < 5:
                    p.group("T", [lambda te, _d=DPW[jj], _p=POW[jj], _pb=pbB: te.matmul(
                        _pb[:, 384:512], _p, _d, start=True, stop=True)],
                        reads=[f"DP{jj}{u}", f"POW{jj}{u}"], writes=[kB])
                    p.emit("V", lambda v, _d=DPW[jj + 1], _pb=pbB: v.tensor_copy(_d, _pb[:, 384:512]),
                           reads=[kB], writes=[f"DP{jj + 1}{u}"])
            p.group("T", [lambda te, _k=khT, _b=b0: te.transpose(
                pbtA[:, 128:192], _k, identb[_b:_b + 64, _b:_b + 64])],
                    reads=[f"khT{u}", "identb"], writes=["pbtA"])
            p.emit("V", lambda v, _k=Kc: v.tensor_copy(_k, pbtA[:, 128:192]),
                   reads=["pbtA"], writes=[f"Kc{u}"])
            gcol_ = stats[:, 20 + 4 * par:21 + 4 * par]
            p.emit("S", lambda s, l=lgc_, _g=gcol_: s.activation(_g, l, AF.Exp),
                   reads=[f"lgcol{c}"], writes=[f"gcol{u}"])
            p.emit("V", lambda v, _b=XB, ap=vview(h, c): v.tensor_copy(_b[:, 0:64], ap),
                   reads=[kv_], writes=[f"XB{u}"])
            p.emit("V", lambda v, _b=XB, _k=Kc, _g=gcol_: v.tensor_scalar(
                _b[:, 64:128], _k, _g, None, ALU.mult),
                reads=[f"Kc{u}", f"gcol{u}"], writes=[f"XB{u}"])
            regs = [(128, 256), (256, 384)]
            p.group("T", [
                (lambda te, _b=XB, _pc=pbC: te.matmul(
                    _pc[:, 128:256], identb[:, :], _b, start=True, stop=False)),
                (lambda te, _p=POW[6], _b=XB, _pc=pbC: te.matmul(
                    _pc[:, 128:256], _p, _b, start=False, stop=True))],
                reads=[f"XB{u}", f"POW6{u}", "identb"], writes=[kC])
            cur = 0
            for jj in range(5, -1, -1):
                r0 = regs[cur]
                r1 = regs[1 - cur]
                p.emit("V", lambda v, _b=XB, _pc=pbC, _r=r0: v.tensor_copy(_b, _pc[:, _r[0]:_r[1]]),
                       reads=[kC], writes=[f"XB{u}"])
                p.group("T", [
                    (lambda te, _b=XB, _pc=pbC, _r=r1: te.matmul(
                        _pc[:, _r[0]:_r[1]], identb[:, :], _b, start=True, stop=False)),
                    (lambda te, _p=POW[jj], _b=XB, _pc=pbC, _r=r1: te.matmul(
                        _pc[:, _r[0]:_r[1]], _p, _b, start=False, stop=True))],
                    reads=[f"POW{jj}{u}", f"XB{u}", "identb"], writes=[kC])
                cur = 1 - cur
            xf0 = regs[cur][0]
            p.emit("V", lambda v, _o=BXv, b=bc_, _pc=pbC, _x=xf0: v.tensor_scalar(
                _o, _pc[:, _x:_x + 64], b, None, ALU.mult),
                reads=[kC, f"bcol{c // 4}"], writes=[f"BXv{u}"])
            p.emit("V", lambda v, _o=BW, b=bc_, _pc=pbC, _x=xf0: v.tensor_scalar(
                _o, _pc[:, _x + 64:_x + 128], b, -1.0, ALU.mult, ALU.mult),
                reads=[kC, f"bcol{c // 4}"], writes=[f"BW{u}"])
            p.group("T", [lambda te, _k=khT, ap=Aview(h, c), _pb=pbB: te.matmul(
                _pb[:, 128:256], _k, ap, start=True, stop=True)],
                reads=[f"khT{u}", kq], writes=[kB])
            p.emit("V", lambda v, _d=DM, _pb=pbB: v.tensor_tensor(TMP1, _pb[:, 128:256], _d, ALU.mult),
                   reads=[kB, f"DM{u}", f"TMP1{par}"], writes=[f"TMP1{par}"])
            p.emit("V", lambda v, _p=PT: v.scalar_tensor_tensor(
                _p, TMP1, 0.125, miu_b[:, :], ALU.mult, ALU.mult),
                reads=[f"TMP1{par}", "miu_b"], writes=[f"PT{u}"])
            p.group("T", [lambda te, _w=BW, _p=PT, _b=b0, _pb=pbB: te.matmul(
                _pb[_b:_b + 64, 256:384], _w, _p, start=True, stop=True)],
                reads=[f"BW{u}", f"PT{u}"], writes=[kB])
            p.emit("V", lambda v, _q=QtT, ap=Aview(h, c), _b=b0, _pb=pbB: v.tensor_tensor(
                ap, _q, _pb[_b:_b + 64, 256:384], ALU.add),
                reads=[f"QtT{u}", kB], writes=[kq])
            p.group("T", [lambda te, _w=BW, _b=b0: te.transpose(
                pbtA[_b:_b + 64, 192:320], _w, identb[:, :])],
                reads=[f"BW{u}", "identb"], writes=["pbtA"])
            p.emit("V", lambda v, _w=BWT, _b=b0: v.tensor_copy(_w, pbtA[_b:_b + 64, 192:320]),
                   reads=["pbtA"], writes=[f"BWT{u}"])
            p.group("T", [lambda te, _w=BWT, hh=h, _pb=pbB: te.matmul(
                _pb[:, 384:448], _w, hm(slocb, hh), start=True, stop=True)],
                reads=[f"BWT{u}", f"sl{h}"], writes=[kB])
            p.emit("V", lambda v, _r=Rt, _x=BXv, _pb=pbB: v.tensor_tensor(_r, _x, _pb[:, 384:448], ALU.add),
                   reads=[f"BXv{u}", kB], writes=[f"Rt{u}"])
            p.group("T", [
                (lambda te, _r=Rt, _p=PT, _b=b0, _pb=pbB: te.matmul(
                    _pb[_b:_b + 64, 0:128], _r, _p, start=True, stop=False)),
                (lambda te, hh=h, _q=QtT, _b=b0, _pb=pbB: te.matmul(
                    _pb[_b:_b + 64, 0:128], hm(slocb, hh), _q, start=False, stop=True))],
                reads=[f"Rt{u}", f"PT{u}", f"sl{h}", f"QtT{u}"], writes=[kB])
            p.emit("S", lambda s, ap=Bview(h, c), _b=b0, _pb=pbB: s.copy(ap, _pb[_b:_b + 64, 0:128]),
                   reads=[kB], writes=[kk_])
            grc = stats[:, 28 + 4 * par:29 + 4 * par]
            p.emit("S", lambda s, l=lgc_, lc=lgcc_, _g=grc: s.activation(
                _g, l, AF.Exp, scale=-1.0, bias=lc),
                reads=[f"lgcol{c}", f"lgcc{c}"], writes=[f"grc{u}"])
            p.emit("V", lambda v, _k=Kc, _o=KG, _g=grc: v.tensor_scalar(_o, _k, _g, None, ALU.mult),
                   reads=[f"Kc{u}", f"grc{u}"], writes=[f"KG{u}"])
            p.group("T", [lambda te, _w=BW, _k=KG, _b=b0, _pa=pbA: te.matmul(
                _pa[_b:_b + 64, 384:448], _w, _k, start=True, stop=True)],
                reads=[f"BW{u}", f"KG{u}"], writes=[kA])
            p.emit("S", lambda s, lc=lgcc_, g=stats[:, 52 + 4 * par:53 + 4 * par]: s.activation(
                g, lc, AF.Exp),
                reads=[f"lgcc{c}"], writes=[f"gccs{u}"])
            p.emit("V", lambda v, _a=AcT, g=gcc_, _b=b0, _pa=pbA: v.scalar_tensor_tensor(
                _a, identf[_b:_b + 64, _b:_b + 64], g, _pa[_b:_b + 64, 384:448],
                ALU.mult, ALU.add),
                reads=[kA, f"gccs{u}", "identf"], writes=[f"AcT{u}"])
            p.group("T", [lambda te, hh=h, _b=b0: te.transpose(
                pbtB[_b:_b + 64, 0:64], hm(pib, hh), identb[_b:_b + 64, _b:_b + 64])],
                reads=[f"pi{h}", "identb"], writes=["pbtB"])
            p.emit("V", lambda v, hh=h, cc=c, _b=b0: v.tensor_copy(psvv(hh, cc), pbtB[_b:_b + 64, 0:64]),
                   reads=["pbtB"], writes=[f"psv{h}_{c}"])
            p.group("T", [lambda te, _a=AcT, hh=h, _b=b0, _pa=pbA: te.matmul(
                _pa[_b:_b + 64, 192:256], _a, hm(pib, hh), start=True, stop=True)],
                reads=[f"AcT{u}", f"pi{h}"], writes=[kA])
            p.emit("V", lambda v, hh=h, _b=b0, _pa=pbA: v.tensor_copy(hm(pib, hh), _pa[_b:_b + 64, 192:256]),
                   reads=[kA], writes=[f"pi{h}"])
            p.group("T", [
                (lambda te, _k=KG, _x=BXv, _b=b0, _pa=pbA: te.matmul(
                    _pa[_b:_b + 64, 448:512], _k, _x, start=True, stop=False)),
                (lambda te, _a=AcT, hh=h, _b=b0, _pa=pbA: te.matmul(
                    _pa[_b:_b + 64, 448:512], _a, hm(slocb, hh), start=False, stop=True))],
                reads=[f"KG{u}", f"BXv{u}", f"AcT{u}", f"sl{h}"], writes=[kA])
            p.emit("V", lambda v, hh=h, _b=b0, _pa=pbA: v.tensor_copy(hm(slocb, hh), _pa[_b:_b + 64, 448:512]),
                   reads=[kA], writes=[f"sl{h}"])

    for c in range(_NTU):
        for hp in range(0, H, 2):
            p.cap = []
            emit_unit(c, hp)
            l0 = p.cap
            p.cap = []
            emit_unit(c, hp + 1)
            l1 = p.cap
            p.cap = None
            p.replay_rr([l0, l1])

    for h in range(H):
        b0 = pr(h)
        p.group("T", [lambda te, hh=h, _b=b0: te.transpose(
            pbtB[_b:_b + 64, 64:128], hm(pib, hh), identb[_b:_b + 64, _b:_b + 64])],
            reads=[f"pi{h}", "identb"], writes=["pbtB"])
        p.emit("V", lambda v, hh=h, _b=b0: v.tensor_copy(
            BUF_C[_b:_b + 64, (hh // 2) * 64:(hh // 2) * 64 + 64],
            pbtB[_b:_b + 64, 64:128]),
            reads=["pbtB"], writes=[f"pif{h}"])
    for pa in range(2):
        p.emit("P", lambda g, _p=pa: g.dma_start(
            out=cc_in_d[_p * 8:(_p + 1) * 8, :].rearrange("m (a b) -> a m b", a=64),
            in_=BUF_C[_p * 64:_p * 64 + 64, 0:512].rearrange("a (m b) -> a m b", m=8)),
            reads=[f"pif{h}" for h in range(H)], writes=["ccin"], dma="dp")
        p.emit("P", lambda g, _p=pa: g.dma_start(
            out=cc_in_d[16 + _p * 8:16 + (_p + 1) * 8, :].rearrange("m (a b) -> a m b", a=64),
            in_=slocb[_p * 64:_p * 64 + 64, :].rearrange("a (m b) -> a m b", m=8)),
            reads=[f"sl{h}" for h in range(H)], writes=["ccin"], dma="dp")
    import os as _os
    if _os.environ.get("KSKIP_CC"):
        for _r in range(4):
            p.emit("P", lambda g, rr=_r: g.dma_start(out=cc_out_d[rr * 32:(rr + 1) * 32, :], in_=cc_in_d),
                   reads=["ccin"], writes=["ccout"], dma="dp")
    else:
        p.emit("P", lambda g: g.collective_compute(
            "AllGather", ALU.bypass, replica_groups=[[0, 1, 2, 3], [4, 5, 6, 7]],
            ins=[cc_in_d.opt()], outs=[cc_out_d.opt()]),
            reads=["ccin"], writes=["ccout"], collective=True)
    GB0 = 512
    for j4 in range(4):
        for kind in range(2):
            for pa in range(2):
                p.emit("P", lambda g, jj=j4, kd=kind, _p=pa: g.dma_start(
                    out=BUF_C[_p * 64:_p * 64 + 64,
                              GB0 + (jj * 16 + kd * 8) * 64:GB0 + (jj * 16 + kd * 8 + 8) * 64
                              ].rearrange("a (m b) -> a m b", m=8),
                    in_=cc_out_d[jj * 32 + kd * 16 + _p * 8: jj * 32 + kd * 16 + (_p + 1) * 8, :
                                 ].rearrange("m (a b) -> a m b", a=64)),
                    reads=["ccout"], writes=[f"G{j4}_{kind}_{pa}"], dma="dp")

    p.retoken([f"G{j}_{k}_{pa}" for j in range(4) for k in range(2) for pa in range(2)])

    def gmat(j4, kind, h):
        base = GB0 + (j4 * 16 + kind * 8 + h // 2) * 64
        return BUF_C[pr(h):pr(h) + 64, base:base + 64]

    for h in range(H):
        b0 = pr(h)
        TMA = SCR_F[b0:b0 + 64, 0:64]
        AeffT = SCR_B[b0:b0 + 64, 4672:4736]
        p.emit("V", lambda v, hh=h: v.memset(hm(s0b, hh), 0.0), writes=[f"s0{h}"])
        gks = [f"G{j}_{k}_{h % 2}" for j in range(4) for k in range(2)]
        for j4 in range(4):
            p.emit("V", lambda v, jj=j4, hh=h, _t=TMA, _b=b0: v.tensor_scalar(
                _t, gmat(jj, 0, hh), cmask[_b:_b + 64, jj:jj + 1], None, ALU.mult),
                reads=gks + ["cmask"], writes=["TMA"])
            p.emit("V", lambda v, jj=j4, _t=TMA, _a=AeffT, _b=b0: v.scalar_tensor_tensor(
                _a, identf[_b:_b + 64, _b:_b + 64], cmask[_b:_b + 64, 4 + jj:5 + jj], _t,
                ALU.mult, ALU.add),
                reads=["TMA", "cmask", "identf"], writes=["AeffT"])
            p.group("T", [lambda te, hh=h, _a=AeffT, _b=b0: te.matmul(
                pb[3 * (hh % 2)][_b:_b + 64, 320:384], _a, hm(s0b, hh), start=True, stop=True)],
                reads=["AeffT", f"s0{h}"], writes=[f"pb{3 * (h % 2)}"])
            p.emit("V", lambda v, jj=j4, hh=h, _t=TMA, _b=b0: v.tensor_scalar(
                _t, gmat(jj, 1, hh), cmask[_b:_b + 64, jj:jj + 1], None, ALU.mult),
                reads=gks + ["cmask"], writes=["TMA"])
            p.emit("V", lambda v, hh=h, _t=TMA, _b=b0: v.tensor_tensor(
                hm(s0b, hh), _t, pb[3 * (hh % 2)][_b:_b + 64, 320:384], ALU.add),
                reads=["TMA", f"pb{3 * (h % 2)}"], writes=[f"s0{h}"])
    for c in range(_NTU):
        for h in range(H):
            b0 = pr(h)
            DSb = SCR_B[b0:b0 + 64, 4736:4800]
            p.group("T", [lambda te, hh=h, cc=c, _b=b0: te.matmul(
                pb[3 * (hh % 2)][_b:_b + 64, 320:384], psvv(hh, cc), hm(s0b, hh), start=True, stop=True)],
                reads=[f"psv{h}_{c}", f"s0{h}"], writes=[f"pb{3 * (h % 2)}"])
            p.emit("V", lambda v, _d=DSb, _b=b0, hh=h: v.tensor_copy(
                _d, pb[3 * (hh % 2)][_b:_b + 64, 320:384]),
                   reads=[f"pb{3 * (h % 2)}"], writes=["DSb"])
            p.group("T", [lambda te, hh=h, cc=c, _d=DSb, _b=b0: te.matmul(
                pb[3 * (hh % 2) + 1][_b:_b + 64, 256:384], _d, Aview(hh, cc), start=True, stop=True)],
                reads=["DSb", f"A{h}_{c}"], writes=[f"pb{3 * (h % 2) + 1}"])
            p.emit("V", lambda v, hh=h, cc=c, _b=b0: v.tensor_tensor(
                Bview(hh, cc), Bview(hh, cc), pb[3 * (hh % 2) + 1][_b:_b + 64, 256:384], ALU.add),
                reads=[f"B{h}_{c}", f"pb{3 * (h % 2) + 1}"], writes=[f"B{h}_{c}"])

    if debug == "A2":
        p.fence()
        add_dbg("OT", [128, 16384])
        p.emit("P", lambda g: g.dma_start(out=dbg["OT"], in_=BUF_B[:, :]), dma="dbg")
        add_dbg("JT", [128, 16384])
        p.emit("P", lambda g: g.dma_start(out=dbg["JT"], in_=BUF_A[:, :]), dma="dbg")
        add_dbg("s0b", [128, 512])
        p.emit("P", lambda g: g.dma_start(out=dbg["s0b"], in_=s0b[:, :]), dma="dbg")
        add_dbg("slocb", [128, 512])
        p.emit("P", lambda g: g.dma_start(out=dbg["slocb"], in_=slocb[:, :]), dma="dbg")
        p.fence(); p.final_waits()
        return _finish(nc, p, ctx, dbg)

    p.fence()

    bob_bf = SCR_B[:, 0:1024]
    p.emit("P", lambda g: g.dma_start(out=bob_bf, in_=bob_d), writes=["bob"], dma="dp")

    def wochk(par):
        return BUF_A[:, 8192 + par * 4096: 8192 + (par + 1) * 4096]

    for chn in range(2):
        par = chn % 2
        p.emit("P", lambda g, c=chn, pa=par: g.dma_start(
            out=wochk(pa).rearrange("p (k f) -> p k f", k=DT),
            in_=wo_d[:, c * 512:(c + 1) * 512].rearrange("(k p) f -> p k f", p=128)),
            writes=[f"woch{par}"], dma=f"w{par}")
        for t in range(NT):
            bank = t % 2
            p.group("T", [
                (lambda te, kk=kk, pa=par, tg=t, b=bank: te.matmul(
                    pb[b][:, 0:512],
                    BUF_B[:, kk * TOK + tg * 128: kk * TOK + (tg + 1) * 128],
                    wochk(pa)[:, kk * 512:(kk + 1) * 512],
                    start=(kk == 0), stop=(kk == DT - 1)))
                for kk in range(DT)],
                reads=[f"woch{par}"] + [f"B{hh}_{t}" for hh in range(H)],
                writes=[f"pb{bank}"])
            p.emit("V", lambda v, tg=t, c=chn, b=bank: v.tensor_tensor(
                xv(tg)[:, c * 512:(c + 1) * 512], xv(tg)[:, c * 512:(c + 1) * 512],
                pb[b][:, 0:512], ALU.add),
                reads=[f"pb{bank}", f"x{t}"], writes=[f"x{t}"])
    for t in range(NT):
        p.emit("V", lambda v, tg=t: v.tensor_tensor(xv(tg), xv(tg), bob_bf, ALU.add),
               reads=["bob", f"x{t}"], writes=[f"x{t}"])

    if debug == "A3":
        p.fence()
        add_dbg("x1", [128, NT * D])
        p.emit("Y", lambda y: y.dma_start(out=dbg["x1"], in_=BUF_X[:, :]), dma=True)
        p.fence(); p.final_waits()
        return _finish(nc, p, ctx, dbg)

    p.fence()

    def w2t(r):
        if r < 16:
            return BUF_B[:, r * 1024:(r + 1) * 1024]
        return BUF_C[:, (r - 16) * 1024:(r - 16 + 1) * 1024]

    for r in range(32):
        p.emit("P", lambda g, rr=r: g.dma_start(
            out=w2t(rr), in_=w2_d[rr * 128:(rr + 1) * 128, :]),
            writes=[f"w2c{r}"], dma="dp")
    b2b_bf = SCR_B[:, 7168:8192]
    p.emit("P", lambda g: g.dma_start(out=b2b_bf, in_=b2b_d), writes=["b2b"], dma="dp")
    p.retoken([f"w2c{r}" for r in range(32)] + ["b2b"])

    def xn2T(dt_, j):
        return SCR_B[:, dt_ * 256 + j * 128: dt_ * 256 + (j + 1) * 128]

    xntr2 = SCR_B[:, 2048:3072]

    def w1ch(par):
        return SCR_B[:, 3072 + par * 2048: 3072 + (par + 1) * 2048]

    def h1t(ft):
        return PSV[:, ft * 256: (ft + 1) * 256]

    for hmc in range(8):
        tt0 = hmc * 2
        for j in range(2):
            t = tt0 + j
            emit_ln(t, xntr2[:, :], [f"xn2_{t}"])
            for dt_ in range(DT):
                p.group("T", [lambda te, d=dt_: te.transpose(
                    pbtA[:, 0:128], xntr2[:, d * 128:(d + 1) * 128], identb[:, :])],
                    reads=[f"xn2_{t}", "identb"], writes=["pbtA"])
                p.emit("V", lambda v, d=dt_, jj=j: v.tensor_copy(xn2T(d, jj), pbtA[:, 0:128]),
                       reads=["pbtA"], writes=[f"x2T{t}_{dt_}"])
        xnk = [f"x2T{tt0 + j}_{d}" for j in range(2) for d in range(DT)]
        for chn in range(16):
            par = chn % 2
            p.emit("P", lambda g, c=chn, pa=par: g.dma_start(
                out=w1ch(pa).rearrange("p (k f) -> p k f", k=DT),
                in_=w1_d[:, c * 256:(c + 1) * 256].rearrange("(k p) f -> p k f", p=128)),
                writes=[f"w1ch{par}"], dma=f"w{par}")
            if hmc == 0:
                for sub in range(2):
                    ft = chn * 2 + sub
                    p.group("T", [
                        (lambda te, kk=kk, s_=sub, pa=par, f=ft: te.matmul(
                            pb[4][:, (f % 2):(f % 2) + 1],
                            w1ch(pa)[:, kk * 256 + s_ * 128: kk * 256 + (s_ + 1) * 128],
                            b2lb[:, kk:kk + 1], start=(kk == 0), stop=(kk == DT - 1)))
                        for kk in range(DT)],
                        reads=[f"w1ch{par}", "b2lb"], writes=["pb4"])
                    p.emit("V", lambda v, f=ft: v.tensor_tensor(
                        b1tot[:, f:f + 1], b1c[:, f:f + 1], pb[4][:, (f % 2):(f % 2) + 1], ALU.add),
                        reads=["pb4", "b1c"], writes=["b1tot"])
            for kk in range(DT):
                p.emit("V", lambda v, k2=kk, pa=par: v.tensor_scalar(
                    w1ch(pa)[:, k2 * 256:(k2 + 1) * 256], w1ch(pa)[:, k2 * 256:(k2 + 1) * 256],
                    g2c[:, k2:k2 + 1], None, ALU.mult),
                    reads=[f"w1ch{par}", "g2c"], writes=[f"w1ch{par}"])
            for sub in range(2):
                ft = chn * 2 + sub
                bank = ft % 2
                p.group("T", [
                    (lambda te, kk=kk, s_=sub, pa=par, b=bank: te.matmul(
                        pb[b][:, 0:256],
                        w1ch(pa)[:, kk * 256 + s_ * 128: kk * 256 + (s_ + 1) * 128],
                        SCR_B[:, kk * 256:(kk + 1) * 256],
                        start=(kk == 0), stop=(kk == DT - 1)))
                    for kk in range(DT)],
                    reads=[f"w1ch{par}"] + xnk, writes=[f"pb{bank}"])
                p.emit("S", lambda s, f=ft, b=bank: s.activation(
                    h1t(f), pb[b][:, 0:256], AF.Gelu, bias=b1tot[:, f:f + 1]),
                    reads=[f"pb{bank}", "b1tot"], writes=[f"h1_{ft}"])
        for j in range(2):
            t = tt0 + j
            for dc in range(2):
                bank = 2 + (j * 2 + dc) % 2
                p.group("T", [
                    (lambda te, kk=kk, jj=j, d=dc, b=bank: te.matmul(
                        pb[b][:, 0:512], h1t(kk)[:, jj * 128:(jj + 1) * 128],
                        w2t(kk)[:, d * 512:(d + 1) * 512],
                        start=(kk == 0), stop=(kk == 31)))
                    for kk in range(32)],
                    reads=[f"h1_{kk}" for kk in range(32)] + [f"w2c{kk}" for kk in range(32)],
                    writes=[f"pb{bank}"])
                p.emit("V", lambda v, tg=t, d=dc, b=bank: v.tensor_tensor(
                    xv(tg)[:, d * 512:(d + 1) * 512], xv(tg)[:, d * 512:(d + 1) * 512],
                    pb[b][:, 0:512], ALU.add),
                    reads=[f"pb{bank}", f"x{t}"], writes=[f"x{t}"])
            p.emit("V", lambda v, tg=t: v.tensor_tensor(xv(tg), xv(tg), b2b_bf, ALU.add),
                   reads=["b2b", f"x{t}"], writes=[f"x{t}"])

    if debug == "A4":
        p.fence()
        add_dbg("x1", [128, NT * D])
        p.emit("Y", lambda y: y.dma_start(out=dbg["x1"], in_=BUF_X[:, :]), dma=True)
        p.fence(); p.final_waits()
        return _finish(nc, p, ctx, dbg)

    p.fence()

    BUF_Bf = BUF_B[:, :].bitcast(F32)
    BUF_Cf = BUF_C[:, :].bitcast(F32)

    def xorig(t):
        if t < 8:
            return BUF_Bf[:, t * D:(t + 1) * D]
        return BUF_Cf[:, (t - 8) * D:(t - 7) * D]

    p.emit("Y", lambda y: y.dma_start(
        out=BUF_Bf[:, 0:8 * D].rearrange("p (t d) -> p t d", t=8),
        in_=x_d[0:8 * 128, :].rearrange("(t p) d -> p t d", p=128)),
        writes=[f"xo{t}" for t in range(8)], dma=True)
    p.emit("Y", lambda y: y.dma_start(
        out=BUF_Cf[:, 0:8 * D].rearrange("p (t d) -> p t d", t=8),
        in_=x_d[8 * 128:16 * 128, :].rearrange("(t p) d -> p t d", p=128)),
        writes=[f"xo{t}" for t in range(8, 16)], dma=True)
    p.retoken([f"xo{t}" for t in range(NT)])
    xtr = SCR_B[:, 0:1024]
    for t in range(NT):
        p.emit("V", lambda v, tg=t: v.tensor_tensor(xv(tg), xv(tg), xorig(tg), ALU.add),
               reads=[f"x{t}", f"xo{t}"], writes=[f"x{t}"])
        emit_ln(t, xtr[:, :], [f"x2b{t}"])
        p.emit("V", lambda v, tg=t: v.tensor_scalar(
            xv(tg), xv(tg), stats[:, 0:1], stats[:, 4:5], ALU.add, ALU.mult),
            reads=[f"x{t}", "st_neg", "st_rst"], writes=[f"x{t}"])
        for dt_ in range(DT):
            p.group("T", [lambda te, d=dt_: te.transpose(
                pbtA[:, 0:128], xtr[:, d * 128:(d + 1) * 128], identb[:, :])],
                reads=[f"x2b{t}", "identb"], writes=["pbtA"])
            p.emit("V", lambda v, d=dt_, tg=t: v.tensor_copy(
                BUF_A[:, d * TOK + tg * 128: d * TOK + (tg + 1) * 128], pbtA[:, 0:128]),
                reads=["pbtA"], writes=[f"xT2_{t}_{dt_}"])

    if debug == "A5":
        p.fence()
        add_dbg("x2", [128, NT * D])
        p.emit("Y", lambda y: y.dma_start(out=dbg["x2"], in_=BUF_X[:, :]), dma=True)
        p.fence(); p.final_waits()
        return _finish(nc, p, ctx, dbg)

    p.fence()

    for t in range(NT):
        bank = t % 2
        p.group("T", [
            (lambda te, kk=kk, tg=t, b=bank: te.matmul(
                pb[b][:, 0:NE], BUF_A[:, kk * TOK + tg * 128: kk * TOK + (tg + 1) * 128],
                gwc[:, kk * NE:(kk + 1) * NE], start=(kk == 0), stop=(kk == DT - 1)))
            for kk in range(DT)],
            reads=["gwc"] + [f"xT2_{t}_{d}" for d in range(DT)], writes=[f"pb{bank}"])
        sc = SCR_F[:, 0:8]; es = SCR_F[:, 8:16]; c1 = SCR_F[:, 16:24]; c2 = SCR_F[:, 24:32]
        es2 = SCR_F[:, 32:40]
        m1 = stats[:, 36:37]; m2 = stats[:, 40:41]; rs = stats[:, 44:45]; rw = stats[:, 48:49]
        p.emit("V", lambda v, b=bank: v.tensor_tensor(sc, pb[b][:, 0:NE], gbb[:, :], ALU.add),
               reads=[f"pb{bank}", "gbb"], writes=["g_sc"])
        p.emit("V", lambda v: v.reduce_max(m1, sc, axis=AX.X, negate=True),
               reads=["g_sc"], writes=["g_m1"])
        p.emit("S", lambda s: s.activation(es, sc, AF.Exp, bias=m1),
               reads=["g_sc", "g_m1"], writes=["g_es"])
        p.emit("V", lambda v: v.reduce_sum(m2, es, axis=AX.X), reads=["g_es"], writes=["g_sm"])
        p.emit("V", lambda v: v.reciprocal(rs, m2), reads=["g_sm"], writes=["g_rs"])
        p.emit("V", lambda v: v.tensor_scalar(es, es, rs, None, ALU.mult),
               reads=["g_es", "g_rs"], writes=["g_es"])
        p.emit("V", lambda v: v.reduce_max(m1, es, axis=AX.X), reads=["g_es"], writes=["g_m1"])
        p.emit("V", lambda v: v.tensor_scalar(c1, es, m1, None, ALU.is_ge),
               reads=["g_es", "g_m1"], writes=["g_c1"])
        p.emit("V", lambda v: v.tensor_copy(c2, c1), reads=["g_c1"], writes=["g_c2"])
        p.emit("V", lambda v: v.tensor_tensor(c2[:, 1:8], c2[:, 1:8], c1[:, 0:7], ALU.add),
               reads=["g_c1", "g_c2"], writes=["g_c2"])
        p.emit("V", lambda v: v.tensor_copy(c1, c2), reads=["g_c2"], writes=["g_c1b"])
        p.emit("V", lambda v: v.tensor_tensor(c2[:, 2:8], c2[:, 2:8], c1[:, 0:6], ALU.add),
               reads=["g_c1b", "g_c2"], writes=["g_c2"])
        p.emit("V", lambda v: v.tensor_copy(c1, c2), reads=["g_c2"], writes=["g_c1b"])
        p.emit("V", lambda v: v.tensor_tensor(c2[:, 4:8], c2[:, 4:8], c1[:, 0:4], ALU.add),
               reads=["g_c1b", "g_c2"], writes=["g_c2"])
        p.emit("V", lambda v: v.tensor_scalar(c2, c2, 1.0, None, ALU.is_equal),
               reads=["g_c2"], writes=["g_c2"])
        p.emit("V", lambda v: v.tensor_scalar(c1, es, m1, None, ALU.is_ge),
               reads=["g_es", "g_m1"], writes=["g_c1"])
        p.emit("V", lambda v: v.tensor_tensor(c1, c1, c2, ALU.mult),
               reads=["g_c1", "g_c2"], writes=["g_c1"])
        p.emit("V", lambda v: v.scalar_tensor_tensor(es2, c1, -1e30, es, ALU.mult, ALU.add),
               reads=["g_c1", "g_es"], writes=["g_es2"])
        p.emit("V", lambda v: v.reduce_max(m2, es2, axis=AX.X), reads=["g_es2"], writes=["g_m2"])
        p.emit("V", lambda v: v.tensor_scalar(c2, es2, m2, None, ALU.is_ge),
               reads=["g_es2", "g_m2"], writes=["g_c2"])
        p.emit("V", lambda v: v.tensor_tensor(rs, m1, m2, ALU.add),
               reads=["g_m1", "g_m2"], writes=["g_rs"])
        p.emit("V", lambda v: v.tensor_scalar(rs, rs, 1e-20, None, ALU.add),
               reads=["g_rs"], writes=["g_rs"])
        p.emit("V", lambda v: v.reciprocal(rw, rs), reads=["g_rs"], writes=["g_rw"])
        p.emit("V", lambda v: v.tensor_scalar(c1, c1, m1, None, ALU.mult),
               reads=["g_c1", "g_m1"], writes=["g_c1"])
        p.emit("V", lambda v: v.tensor_scalar(c2, c2, m2, None, ALU.mult),
               reads=["g_c2", "g_m2"], writes=["g_c2"])
        p.emit("V", lambda v: v.tensor_tensor(c1, c1, c2, ALU.add),
               reads=["g_c1", "g_c2"], writes=["g_c1"])
        p.emit("V", lambda v, tg=t: v.tensor_scalar(
            comb[:, tg * 8:(tg + 1) * 8], c1, rw, None, ALU.mult),
            reads=["g_c1", "g_rw"], writes=[f"comb{t}"])

    if debug == "B1":
        p.fence()
        add_dbg("comb", [128, 128])
        p.emit("Y", lambda y: y.dma_start(out=dbg["comb"], in_=comb[:, :]), dma=True)
        p.final_waits()
        return _finish(nc, p, ctx, dbg)

    def moebuf(par):
        return BUF_C if par == 0 else SCR_B

    def dwc(par, k):
        i = par * 4 + k
        return PSV[:, i * 1024:(i + 1) * 1024]

    def hreg(ft, sub):
        return BUF_B[:, ft * 2048 + sub * 128: ft * 2048 + (sub + 1) * 128]

    PRD = [SCR_F[:, 0:512], SCR_F[:, 0:512]]
    dbbt = BUF_B[:, 8192:9216]
    UPS = [BUF_C[:, 8192 + i * 512: 8192 + (i + 1) * 512] for i in range(2)]

    for e in range(NE):
        ep = e % 2
        p.emit("P", lambda g, ee=e, pa=ep: g.dma_start(
            out=upbt[:, pa * FT:(pa + 1) * FT], in_=upb_d[ee]),
            writes=[f"upb{ep}"], dma=f"eb{ep}")
        p.emit("P", lambda g, ee=e, pa=ep: g.dma_start(
            out=gpbt[:, pa * FT:(pa + 1) * FT], in_=gpb_d[ee]),
            writes=[f"gpb{ep}"], dma=f"eb{ep}")
        p.retoken([f"upb{ep}", f"gpb{ep}"])
        p.emit("P", lambda g, ee=e: g.dma_start(out=dbbt[:, :], in_=dbb_d[ee]),
            writes=["dbb"], dma="dbbs")
        for ci, (f0, fw) in enumerate(MOE_CHUNKS):
            par = ci % 2
            nk = fw // 128
            p.emit("P", lambda g, ee=e, a=f0, w=fw, pa=par: g.dma_start(
                out=moebuf(pa)[:, 0:DT * w].rearrange("p (k f) -> p k f", k=DT),
                in_=upw_d[ee, :, a:a + w].rearrange("(k p) f -> p k f", p=128)),
                writes=[f"upw{par}"], dma=f"m{par}")
            p.emit("P", lambda g, ee=e, a=f0, w=fw, pa=par: g.dma_start(
                out=moebuf(pa)[:, 4096:4096 + DT * w].rearrange("p (k f) -> p k f", k=DT),
                in_=gpw_d[ee, :, a:a + w].rearrange("(k p) f -> p k f", p=128)),
                writes=[f"gpw{par}"], dma=f"m{par}")
            for k2 in range(nk):
                p.emit("P", lambda g, ee=e, a=f0, kk=k2, pa=par: g.dma_start(
                    out=dwc(pa, kk), in_=dwt_d[ee, a + kk * 128:a + (kk + 1) * 128, :]),
                    writes=[f"dw{par}_{k2}"], dma=f"m{par}")
            p.retoken([f"upw{par}", f"gpw{par}"] + [f"dw{par}_{k2}" for k2 in range(nk)])
            for ft in range(nk):
                ftg = f0 // 128 + ft
                for mc in range(4):
                    bank = mc % 2
                    bank2 = 2 + mc % 2
                    p.group("T", [
                        (lambda te, kk=kk, f=ft, m=mc, b=bank, pa=par, w=fw: te.matmul(
                            pb[b][:, 0:512], moebuf(pa)[:, kk * w + f * 128: kk * w + (f + 1) * 128],
                            BUF_A[:, kk * TOK + m * 512:kk * TOK + (m + 1) * 512],
                            start=(kk == 0), stop=(kk == DT - 1)))
                        for kk in range(DT)],
                        reads=[f"upw{par}"], writes=[f"pb{bank}"])
                    p.group("T", [
                        (lambda te, kk=kk, f=ft, m=mc, b=bank2, pa=par, w=fw: te.matmul(
                            pb[b][:, 0:512],
                            moebuf(pa)[:, 4096 + kk * w + f * 128: 4096 + kk * w + (f + 1) * 128],
                            BUF_A[:, kk * TOK + m * 512:kk * TOK + (m + 1) * 512],
                            start=(kk == 0), stop=(kk == DT - 1)))
                        for kk in range(DT)],
                        reads=[f"gpw{par}"], writes=[f"pb{bank2}"])
                    ub = UPS[mc % 2]
                    p.emit("S", lambda s, _u=ub, b=bank, f=ftg, pa=ep: s.activation(
                        _u, pb[b][:, 0:512], AF.Identity, bias=upbt[:, pa * FT + f:pa * FT + f + 1]),
                        reads=[f"pb{bank}", f"upb{ep}"], writes=[f"UPS{mc % 2}"])
                    p.emit("V", lambda v, _u=ub, b=bank2, f=ftg, pa=ep, m=mc: v.scalar_tensor_tensor(
                        PRD[m % 2], pb[b][:, 0:512], gpbt[:, pa * FT + f:pa * FT + f + 1], _u,
                        ALU.add, ALU.mult),
                        reads=[f"pb{bank2}", f"gpb{ep}", f"UPS{mc % 2}"], writes=[f"PRD{mc % 2}"])
                    p.emit("S", lambda s, f=ft, m=mc: s.activation(
                        BUF_B[:, f * 2048 + m * 512: f * 2048 + (m + 1) * 512],
                        PRD[m % 2], AF.Silu),
                        reads=[f"PRD{mc % 2}"], writes=[f"h{ft}_{mc}"])
            hkeys = [f"h{ft}_{m}" for ft in range(nk) for m in range(4)]
            for sub in range(NT):
                for dc in range(2):
                    bank = 4 + (sub * 2 + dc) % 2
                    p.group("T", [
                        (lambda te, kk=kk, s_=sub, d=dc, b=bank, pa=par: te.matmul(
                            pb[b][:, 0:512], hreg(kk, s_), dwc(pa, kk)[:, d * 512:(d + 1) * 512],
                            start=(kk == 0), stop=(kk == nk - 1)))
                        for kk in range(nk)],
                        reads=hkeys + [f"dw{par}_{kk}" for kk in range(nk)],
                        writes=[f"pb{bank}"])
                    p.emit("V", lambda v, s_=sub, d=dc, b=bank, ee=e: v.scalar_tensor_tensor(
                        xv(s_)[:, d * 512:(d + 1) * 512], pb[b][:, 0:512],
                        comb[:, s_ * 8 + ee: s_ * 8 + ee + 1],
                        xv(s_)[:, d * 512:(d + 1) * 512], ALU.mult, ALU.add),
                        reads=[f"pb{bank}", f"comb{sub}", f"x{sub}"], writes=[f"x{sub}"])
        for sub in range(NT):
            p.emit("V", lambda v, s_=sub, ee=e: v.scalar_tensor_tensor(
                xv(s_), dbbt[:, :],
                comb[:, s_ * 8 + ee: s_ * 8 + ee + 1], xv(s_), ALU.mult, ALU.add),
                reads=["dbb", f"comb{sub}", f"x{sub}"], writes=[f"x{sub}"])

    if debug == "B2":
        p.fence()
        add_dbg("acc", [128, NT * D])
        p.emit("Y", lambda y: y.dma_start(out=dbg["acc"], in_=BUF_X[:, :]), dma=True)
        p.final_waits()
        return _finish(nc, p, ctx, dbg)

    p.fence()

    for t in range(NT):
        emit_ln(t, SCR_B[:, 0:1024], [f"fln{t}"])
        p.emit("V", lambda v, tg=t: v.tensor_scalar(
            xv(tg), xv(tg), stats[:, 0:1], stats[:, 4:5], ALU.add, ALU.mult),
            reads=[f"x{t}", "st_neg", "st_rst"], writes=[f"x{t}"])
    p.emit("Y", lambda y: y.dma_start(
        out=out_d.rearrange("(t p) d -> p t d", p=128),
        in_=BUF_X[:, :].rearrange("p (t d) -> p t d", t=NT)),
        reads=[f"x{t}" for t in range(NT)], dma=True)
    p.final_waits()
    return _finish(nc, p, ctx, dbg)


def _finish(nc, p, ctx, dbg):
    with nc.Block() as block:
        p.run(block)
    for t in reversed(ctx):
        t.__exit__(None, None, None)
    return nc, sorted(dbg.keys())


def prep_inputs(inputs, core):
    ii = {}
    b, q = core // 4, core % 4
    ii["x"] = np.ascontiguousarray(inputs["L0_attn_x"][b, q * TOK:(q + 1) * TOK, :],
                                   dtype=np.float32)
    ii["wq"] = inputs["L0_attn_Wq"]; ii["wk"] = inputs["L0_attn_Wk"]; ii["wv"] = inputs["L0_attn_Wv"]
    ii["wb"] = inputs["L0_attn_Wb"]; ii["wg"] = inputs["L0_attn_Wg"]; ii["wo"] = inputs["L0_attn_Wo"]
    ii["w1"] = inputs["L0_attn_w1"]; ii["w2"] = inputs["L0_attn_w2"]
    ii["gwt"] = np.ascontiguousarray(inputs["L0_ffn_gate_w"].T)
    upw = np.zeros((NE, D, FP), dtype=np.float32)
    upw[:, :, :F_RAW] = inputs["L0_ffn_up_w"].transpose(0, 2, 1)
    gpw = np.zeros((NE, D, FP), dtype=np.float32)
    gpw[:, :, :F_RAW] = inputs["L0_ffn_gp_w"].transpose(0, 2, 1)
    dwt = np.zeros((NE, FP, D), dtype=np.float32)
    dwt[:, :F_RAW, :] = inputs["L0_ffn_down_w"].transpose(0, 2, 1)
    ii["upw"] = upw; ii["gpw"] = gpw; ii["dwt"] = dwt
    ub = np.zeros((NE, FP), dtype=np.float32); ub[:, :F_RAW] = inputs["L0_ffn_up_b"]
    gb = np.zeros((NE, FP), dtype=np.float32); gb[:, :F_RAW] = inputs["L0_ffn_gp_b"]
    ii["upb"] = np.ascontiguousarray(ub.reshape(NE, FT, 128).transpose(0, 2, 1))
    ii["gpb"] = np.ascontiguousarray(gb.reshape(NE, FT, 128).transpose(0, 2, 1))
    ii["dbb"] = np.ascontiguousarray(
        np.broadcast_to(inputs["L0_ffn_down_b"][:, None, :], (NE, 128, D)))
    ii["bob"] = np.ascontiguousarray(np.broadcast_to(inputs["L0_attn_bo"][None, :], (128, D)))
    ii["b2b"] = np.ascontiguousarray(np.broadcast_to(inputs["L0_attn_b2"][None, :], (128, D)))
    ii["b1c"] = np.ascontiguousarray(inputs["L0_attn_b1"].reshape(FFN // 128, 128).T)
    ii["g1c"] = np.ascontiguousarray(inputs["L0_attn_ln1_g"].reshape(DT, 128).T)
    ii["b1lc"] = np.ascontiguousarray(inputs["L0_attn_ln1_b"].reshape(DT, 128).T)
    ii["g2c"] = np.ascontiguousarray(inputs["L0_attn_ln2_g"].reshape(DT, 128).T)
    ii["b2lc"] = np.ascontiguousarray(inputs["L0_attn_ln2_b"].reshape(DT, 128).T)
    ii["gbb"] = np.ascontiguousarray(np.broadcast_to(inputs["L0_ffn_gate_b"][None, :], (128, NE)))
    ii["identf"] = np.eye(128, dtype=np.float32)
    ii["identb"] = np.eye(128).astype(BFD)
    ii["miu"] = np.triu(np.ones((128, 128), dtype=np.float32))
    ii["ones128"] = np.ones((128, 128), dtype=np.float32)
    ii["msu"] = np.triu(np.ones((128, 128)), 1).astype(BFD)
    ii["miub"] = np.triu(np.ones((128, 128))).astype(BFD)
    ii["onesb"] = np.ones((128, 128)).astype(BFD)
    cm = np.zeros((128, 8), dtype=np.float32)
    for j in range(4):
        cm[:, j] = 1.0 if j < q else 0.0
        cm[:, 4 + j] = 0.0 if j < q else 1.0
    ii["cmask"] = cm
    for k in ii:
        if ii[k].dtype == np.float64:
            ii[k] = ii[k].astype(np.float32)
        ii[k] = np.ascontiguousarray(ii[k])
    return ii


_CACHE = {}


def kernel(**inputs):
    if "nc" not in _CACHE:
        _CACHE["nc"] = build_nc()
    nc, _ = _CACHE["nc"]
    in_maps = [prep_inputs(inputs, core) for core in range(8)]
    res = run_bass_kernel_spmd(nc, in_maps, core_ids=list(range(8)))
    outs = [np.asarray(res.results[core]["out"]) for core in range(8)]
    full = np.stack([np.concatenate(outs[0:4], 0), np.concatenate(outs[4:8], 0)], 0)
    return full.astype(np.float32)


# version 10
# speedup vs baseline: 1.2361x; 1.0453x over previous
import numpy as np
import ml_dtypes
import concourse.bass as bass
import concourse.mybir as mybir
from concourse.bass_utils import run_bass_kernel_spmd

F32 = mybir.dt.float32
BF16 = mybir.dt.bfloat16
AF = mybir.ActivationFunctionType
ALU = mybir.AluOpType
AX = mybir.AxisListType
BFD = ml_dtypes.bfloat16

TOK = 2048
NT = 16
D = 1024
DT = 8
H = 16
DH = 64
FFN = 4096
NE = 8
F_RAW = 3264
FP = 3328
FT = FP // 128
NMAC = 4
EPS = 1e-5
MOE_CHUNKS = [(i * 512, 512) for i in range(6)] + [(3072, 256)]


class Prog:
    ENGS = ("T", "V", "S", "P", "Y")

    DMA_STREAMS = ("dy", "dp", "w0", "w1", "m0", "m1", "eb0", "eb1", "dbbs", "dbg")

    def __init__(self):
        self.instrs = {e: [] for e in Prog.ENGS}
        self.cnt = {k: 0 for k in (("T", "V", "S", "P", "Y", "CC") + Prog.DMA_STREAMS)}
        self.seen = {e: {} for e in Prog.ENGS}
        self.last_w = {}
        self.readers = {}
        self.sems = {}

    def _need(self, need, tok):
        k, v = tok
        if need.get(k, 0) < v:
            need[k] = v

    def _deps(self, reads, writes):
        need = {}
        for r in reads:
            t = self.last_w.get(r)
            if t:
                self._need(need, t)
        for w in writes:
            t = self.last_w.get(w)
            if t:
                self._need(need, t)
            for t in self.readers.get(w, ()):
                self._need(need, t)
        return need

    SELF_EDGE_SKIP = {("T", "T")}

    def _emit_waits(self, eng, need, self_key=None):
        for k, v in sorted(need.items()):
            if (eng, k) in Prog.SELF_EDGE_SKIP:
                continue
            if k == eng and self.cnt[k] < v:
                continue
            if self.seen[eng].get(k, 0) >= v:
                continue
            self.seen[eng][k] = v
            self.instrs[eng].append(("w", k, v))

    def _record(self, tok, reads, writes):
        for w in writes:
            self.last_w[w] = tok
            self.readers[w] = []
        for r in reads:
            self.readers.setdefault(r, []).append(tok)

    def emit(self, eng, fn, reads=(), writes=(), dma=False, collective=False):
        if getattr(self, "cap", None) is not None:
            self.cap.append(("e", eng, fn, reads, writes, dma, collective))
            return None
        need = self._deps(reads, writes)
        if collective:
            key = "CC"
        elif dma:
            key = dma if isinstance(dma, str) else "dy"
        else:
            key = eng
        self._emit_waits(eng, need)
        amt = 16 if dma else 1
        self.cnt[key] += amt
        tok = (key, self.cnt[key])
        self.instrs[eng].append(("i", fn, key, amt))
        self._record(tok, reads, writes)
        return tok

    def replay_rr(self, lists, block=2):
        idx = [0] * len(lists)
        while True:
            done = True
            for li, l in enumerate(lists):
                for _ in range(block):
                    if idx[li] >= len(l):
                        break
                    done = False
                    it = l[idx[li]]
                    idx[li] += 1
                    if it[0] == "e":
                        _, eng, fn, reads, writes, dma, coll = it
                        self.emit(eng, fn, reads, writes, dma, coll)
                    else:
                        _, eng, fns, reads, writes = it
                        self.group(eng, fns, reads, writes)
            if done:
                break

    def retoken(self, keys):
        toks = [self.last_w[k] for k in keys if k in self.last_w]
        if not toks:
            return
        best = max(toks, key=lambda t: t[1])
        for k in keys:
            if k in self.last_w:
                self.last_w[k] = best

    def group(self, eng, fns, reads=(), writes=()):
        if getattr(self, "cap", None) is not None:
            self.cap.append(("g", eng, fns, reads, writes))
            return None
        need = self._deps(reads, writes)
        self._emit_waits(eng, need)
        for fn in fns[:-1]:
            self.instrs[eng].append(("q", fn))
        self.cnt[eng] += 1
        tok = (eng, self.cnt[eng])
        self.instrs[eng].append(("i", fns[-1], eng, 1))
        self._record(tok, reads, writes)
        return tok

    def fence(self):
        snap = {k: v for k, v in self.cnt.items() if v > 0}
        for e in Prog.ENGS:
            self._emit_waits(e, snap)
        self.last_w = {}
        self.readers = {}

    def final_waits(self):
        snap = {k: v for k, v in self.cnt.items() if v > 0}
        for k, v in sorted(snap.items()):
            if k == "Y":
                continue
            self.instrs["Y"].append(("w", k, v))

    def run(self, block):
        sems = self.sems

        def runner(eng_obj, lst):
            for it in lst:
                if it[0] == "w":
                    eng_obj.wait_ge(sems[it[1]], it[2])
                elif it[0] == "q":
                    it[1](eng_obj)
                else:
                    _, fn, key, amt = it
                    fn(eng_obj).then_inc(sems[key], amt)

        @block.tensor
        def _(t):
            runner(t, self.instrs["T"])

        @block.vector
        def _(v):
            runner(v, self.instrs["V"])

        @block.scalar
        def _(s):
            runner(s, self.instrs["S"])

        @block.gpsimd
        def _(g):
            runner(g, self.instrs["P"])

        @block.sync
        def _(y):
            runner(y, self.instrs["Y"])


def build_nc(debug=None):
    nc = bass.Bass("TRN2", target_bir_lowering=False)
    p = Prog()

    def din(name, shape, dt=F32):
        return nc.dram_tensor(name, shape, dt, kind="ExternalInput").ap()

    x_d = din("x", [TOK, D])
    wq_d = din("wq", [D, D]); wk_d = din("wk", [D, D]); wv_d = din("wv", [D, D])
    wb_d = din("wb", [D, H]); wg_d = din("wg", [D, H]); wo_d = din("wo", [D, D])
    w1_d = din("w1", [D, FFN]); w2_d = din("w2", [FFN, D])
    gwt_d = din("gwt", [D, NE])
    upw_d = din("upw", [NE, D, FP]); gpw_d = din("gpw", [NE, D, FP])
    dwt_d = din("dwt", [NE, FP, D])
    upb_d = din("upb", [NE, 128, FT]); gpb_d = din("gpb", [NE, 128, FT])
    dbb_d = din("dbb", [NE, 128, D])
    bob_d = din("bob", [128, D]); b2b_d = din("b2b", [128, D])
    b1c_d = din("b1c", [128, FFN // 128])
    g1c_d = din("g1c", [128, DT]); b1lc_d = din("b1lc", [128, DT])
    g2c_d = din("g2c", [128, DT]); b2lc_d = din("b2lc", [128, DT])
    gbb_d = din("gbb", [128, NE])
    identf_d = din("identf", [128, 128])
    identb_d = din("identb", [128, 128], BF16)
    miu_d = din("miu", [128, 128])
    ones128_d = din("ones128", [128, 128])
    msu_d = din("msu", [128, 128], BF16)
    miub_d = din("miub", [128, 128], BF16)
    onesb_d = din("onesb", [128, 128], BF16)
    cmask_d = din("cmask", [128, 8])
    out_d = nc.dram_tensor("out", [TOK, D], F32, kind="ExternalOutput").ap()
    cc_in_d = nc.dram_tensor("cc_in", [32, DH * DH], F32).ap()
    cc_out_d = nc.dram_tensor("cc_out", [128, DH * DH], F32).ap()

    dbg = {}

    def add_dbg(name, shape, dt=F32):
        dbg[name] = nc.dram_tensor("dbg_" + name, shape, dt, kind="ExternalOutput").ap()
        return dbg[name]

    ctx = []

    def sb(name, shape, dt):
        t = nc.sbuf_tensor("sb_" + name, shape, dt)
        ctx.append(t)
        return t.__enter__()

    def psmt(name, shape, dt):
        t = nc.psum_tensor(name, shape, dt)
        ctx.append(t)
        return t.__enter__()

    BUF_X = sb("BUF_X", [128, NT * D], F32)
    BUF_A = sb("BUF_A", [128, 16384], BF16)
    BUF_B = sb("BUF_B", [128, 16384], BF16)
    BUF_C = sb("BUF_C", [128, 16384], BF16)
    SCR_B = sb("SCR_B", [128, 8192], BF16)
    PSV = sb("PSV", [128, 8192], BF16)
    SCR_F = sb("SCR_F", [128, 896], F32)
    lgcol = sb("lgcol", [128, 256], F32)
    lgcc = sb("lgcc", [128, 256], F32)
    bcol = sb("bcol", [128, 256], F32)
    slocb = sb("slocb", [128, 512], BF16)
    pib = sb("pib", [128, 512], BF16)
    s0b = sb("s0b", [128, 512], BF16)
    comb = sb("comb", [128, 128], F32)
    identf = sb("identf", [128, 128], F32)
    identb = sb("identb", [128, 128], BF16)
    miu_f = sb("miu_f", [128, 128], F32)
    ones128 = sb("ones128", [128, 128], F32)
    msu = sb("msu", [128, 128], BF16)
    miu_b = sb("miu_b", [128, 128], BF16)
    onesb = sb("onesb", [128, 128], BF16)
    cmask = sb("cmask", [128, 8], F32)
    b1c = sb("b1c", [128, FFN // 128], F32)
    b1tot = sb("b1tot", [128, FFN // 128], F32)
    g1c = sb("g1c", [128, DT], F32); b1lc = sb("b1lc", [128, DT], F32)
    g2c = sb("g2c", [128, DT], F32); b2lc = sb("b2lc", [128, DT], F32)
    gbb = sb("gbb", [128, NE], F32)
    upbt = sb("upbt", [128, 2 * FT], F32)
    gpbt = sb("gpbt", [128, 2 * FT], F32)
    gwc = sb("gwc", [128, DT * NE], BF16)
    stats = sb("stats", [128, 64], F32)
    bqkv = sb("bqkv", [128, 2 * DT], F32)
    zbias = sb("zbias", [128, 2 * H], F32)
    onec = sb("onec", [128, 1], F32)
    b1lb = sb("b1lb", [128, DT], BF16)
    b2lb = sb("b2lb", [128, DT], BF16)

    pb = [psmt(f"pb{i}", [128, 512], F32) for i in range(6)]
    pbtA = psmt("pbtA", [128, 1024], BF16)
    pbtB = psmt("pbtB", [128, 1024], BF16)

    for k in ("T", "V", "S", "P", "Y", "CC") + Prog.DMA_STREAMS:
        h_ = nc.semaphore("sem_" + k)
        ctx.append(h_)
        p.sems[k] = h_.__enter__()

    def xv(t):
        return BUF_X[:, t * D:(t + 1) * D]

    def Aview(h, c):
        return BUF_A[(h % 2) * 64:(h % 2) * 64 + 64,
                     (h // 2) * TOK + c * 128:(h // 2) * TOK + (c + 1) * 128]

    def Bview(h, c):
        return BUF_B[(h % 2) * 64:(h % 2) * 64 + 64,
                     (h // 2) * TOK + c * 128:(h // 2) * TOK + (c + 1) * 128]

    def vview(h, c):
        return BUF_C[:, c * D + h * 64: c * D + (h + 1) * 64]

    def hm(buf, h):
        return buf[(h % 2) * 64:(h % 2) * 64 + 64, (h // 2) * 64:(h // 2) * 64 + 64]

    def psvv(h, c):
        return PSV[(h % 2) * 64:(h % 2) * 64 + 64,
                   ((h // 2) * 16 + c) * 64:((h // 2) * 16 + c) * 64 + 64]

    def cs(buf, c, h):
        return buf[:, c * H + h: c * H + h + 1]

    def pr(h):
        return (h % 2) * 64

    for dst, src, key in [
        (identf[:, :], identf_d, "identf"), (identb[:, :], identb_d, "identb"),
        (miu_f[:, :], miu_d, "miu_f"), (ones128[:, :], ones128_d, "ones128"),
        (msu[:, :], msu_d, "msu"), (miu_b[:, :], miub_d, "miu_b"),
        (onesb[:, :], onesb_d, "onesb"), (cmask[:, :], cmask_d, "cmask"),
        (b1c[:, :], b1c_d, "b1c"), (g1c[:, :], g1c_d, "g1c"),
        (b1lc[:, :], b1lc_d, "b1lc"), (g2c[:, :], g2c_d, "g2c"),
        (b2lc[:, :], b2lc_d, "b2lc"), (gbb[:, :], gbb_d, "gbb"),
    ]:
        p.emit("Y", lambda y, d=dst, s=src: y.dma_start(out=d, in_=s), writes=[key], dma=True)
    p.emit("P", lambda g: g.dma_start(out=gwc[:, :].rearrange("p (k f) -> p k f", k=DT),
                                      in_=gwt_d.rearrange("(k p) f -> p k f", p=128)),
           writes=["gwc"], dma="dp")
    p.emit("Y", lambda y: y.dma_start(
        out=BUF_X[:, :].rearrange("p (t d) -> p t d", t=NT),
        in_=x_d.rearrange("(t p) d -> p t d", p=128)),
        writes=[f"x{t}" for t in range(NT)], dma=True)
    p.retoken(["identf", "identb", "miu_f", "ones128", "msu", "miu_b", "onesb", "cmask",
               "b1c", "g1c", "b1lc", "g2c", "b2lc", "gbb"] + [f"x{t}" for t in range(NT)])
    p.emit("V", lambda v: v.memset(onec[:, :], 1.0), writes=["onec"])
    p.emit("V", lambda v: v.tensor_copy(b1lb[:, :], b1lc[:, :]), reads=["b1lc"], writes=["b1lb"])
    p.emit("V", lambda v: v.tensor_copy(b2lb[:, :], b2lc[:, :]), reads=["b2lc"], writes=["b2lb"])

    def emit_ln(t, out_ap, out_keys):
        neg = stats[:, 0:1]; rst = stats[:, 4:5]
        sa = stats[:, 12:13]; sbb = stats[:, 16:17]; tmp = stats[:, 8:9]
        p.emit("V", lambda v: v.reduce_sum(neg, xv(t), axis=AX.X, negate=True),
               reads=[f"x{t}"], writes=["st_neg"])
        p.emit("V", lambda v: v.tensor_scalar(neg, neg, 1.0 / D, None, ALU.mult),
               reads=["st_neg"], writes=["st_neg"])
        p.emit("S", lambda s: s.activation(pb[5][:, 0:512], xv(t)[:, 0:512], AF.Square,
                                           accum_out=sa),
               reads=[f"x{t}"], writes=["pb5", "st_sa"])
        p.emit("S", lambda s: s.activation(pb[5][:, 0:512], xv(t)[:, 512:1024], AF.Square,
                                           accum_out=sbb),
               reads=[f"x{t}"], writes=["pb5", "st_sb"])
        p.emit("V", lambda v: v.tensor_tensor(tmp, sa, sbb, ALU.add),
               reads=["st_sa", "st_sb"], writes=["st_tmp"])
        p.emit("V", lambda v: v.tensor_scalar(tmp, tmp, 1.0 / D, None, ALU.mult),
               reads=["st_tmp"], writes=["st_tmp"])
        p.emit("V", lambda v: v.scalar_tensor_tensor(tmp, neg, neg, tmp, ALU.mult,
                                                     ALU.subtract),
               reads=["st_neg", "st_tmp"], writes=["st_tmp2"])
        p.emit("V", lambda v: v.tensor_scalar(tmp, tmp, -1.0, EPS, ALU.mult, ALU.add),
               reads=["st_tmp2"], writes=["st_tmp"])
        p.emit("V", lambda v: v.tensor_scalar(tmp, tmp, 1e-12, None, ALU.max),
               reads=["st_tmp"], writes=["st_tmp"])
        p.emit("S", lambda s: s.activation(rst, tmp, AF.Ln), reads=["st_tmp"], writes=["st_rst"])
        p.emit("S", lambda s: s.activation(rst, rst, AF.Exp, scale=-0.5),
               reads=["st_rst"], writes=["st_rst"])
        p.emit("V", lambda v: v.tensor_scalar(out_ap, xv(t), neg, rst, ALU.add, ALU.mult),
               reads=[f"x{t}", "st_neg", "st_rst"], writes=list(out_keys))

    def xnT(dt_, j):
        return SCR_B[:, dt_ * 512 + j * 128: dt_ * 512 + (j + 1) * 128]

    def wch(par):
        return SCR_B[:, 4096 + par * 2048: 4096 + (par + 1) * 2048]

    xntr = PSV[:, 1024:2048]
    vbias = PSV[:, 0:1024]
    GPAT = PSV[:, 2560:4608]
    GPATZ = PSV[:, 4608:4736]
    for kk in range(DT):
        for hh2 in range(2):
            p.emit("V", lambda v, k2=kk, j2=hh2: v.tensor_scalar(
                GPAT[:, k2 * 256 + j2 * 128:k2 * 256 + (j2 + 1) * 128], onesb[:, 0:128],
                g1c[:, k2:k2 + 1], None, ALU.mult),
                reads=["g1c", "onesb"], writes=["GPAT"])
        p.emit("V", lambda v, k2=kk: v.tensor_scalar(
            GPATZ[:, k2 * H:(k2 + 1) * H], onesb[:, 0:H],
            g1c[:, k2:k2 + 1], None, ALU.mult),
            reads=["g1c", "onesb"], writes=["GPATZ"])
    zcol_b = SCR_F[:, 320:384]
    zcol_g = SCR_F[:, 384:448]
    rowtmp = SCR_F[0:1, 448:576]

    for mac in range(NMAC):
        tt0 = mac * 4
        for j in range(4):
            t = tt0 + j
            emit_ln(t, xntr[:, :], [f"xn{t}"])
            for dt_ in range(DT):
                p.group("T", [lambda te, d=dt_: te.transpose(
                    pbtA[:, 0:128], xntr[:, d * 128:(d + 1) * 128], identb[:, :])],
                    reads=[f"xn{t}", "identb"], writes=["pbtA"])
                p.emit("V", lambda v, d=dt_, jj=j: v.tensor_copy(xnT(d, jj), pbtA[:, 0:128]),
                       reads=["pbtA"], writes=[f"xnT{t}_{dt_}"])
        xnk = [f"xnT{tt0 + j}_{d}" for j in range(4) for d in range(DT)]

        for wi, (wname, wdram) in enumerate((("q", wq_d), ("k", wk_d))):
            for chn in range(4):
                par = chn % 2
                p.emit("P", lambda g, w=wdram, c=chn, pa=par: g.dma_start(
                    out=wch(pa).rearrange("p (k f) -> p k f", k=DT),
                    in_=w[:, c * 256:(c + 1) * 256].rearrange("(k p) f -> p k f", p=128)),
                    writes=[f"wch{par}"], dma=f"w{par}")
                if mac == 0:
                    for sub in range(2):
                        do = chn * 2 + sub
                        p.group("T", [
                            (lambda te, kk=kk, s_=sub, pa=par, d_o=do: te.matmul(
                                pb[4][:, (d_o % 2):(d_o % 2) + 1],
                                wch(pa)[:, kk * 256 + s_ * 128: kk * 256 + (s_ + 1) * 128],
                                b1lb[:, kk:kk + 1], start=(kk == 0), stop=(kk == DT - 1)))
                            for kk in range(DT)],
                            reads=[f"wch{par}", "b1lb"], writes=["pb4"])
                        p.emit("V", lambda v, w=wi, d_o=do: v.tensor_copy(
                            bqkv[:, w * DT + d_o: w * DT + d_o + 1], pb[4][:, (d_o % 2):(d_o % 2) + 1]),
                            reads=["pb4"], writes=[f"bq{wi}"])
                p.emit("V", lambda v, pa=par: v.tensor_tensor(
                    wch(pa)[:, :], wch(pa)[:, :], GPAT[:, :], ALU.mult),
                    reads=[f"wch{par}", "GPAT"], writes=[f"wch{par}"])
                for sub in range(2):
                    do = chn * 2 + sub
                    bank = do % 2
                    p.group("T", [
                        (lambda te, kk=kk, s_=sub, pa=par, b=bank: te.matmul(
                            pb[b][:, 0:512],
                            wch(pa)[:, kk * 256 + s_ * 128: kk * 256 + (s_ + 1) * 128],
                            SCR_B[:, kk * 512:(kk + 1) * 512],
                            start=(kk == 0), stop=(kk == DT - 1)))
                        for kk in range(DT)],
                        reads=[f"wch{par}"] + xnk, writes=[f"pb{bank}"])
                    dst = (BUF_A if wname == "q" else BUF_B)
                    wk_keys = [f"{'A' if wname == 'q' else 'B'}{2 * do + i}_{tt0 + j}"
                               for i in range(2) for j in range(4)]
                    p.emit("S", lambda s, d_o=do, b=bank, w=wi, _d=dst, _t0=tt0: s.activation(
                        _d[:, d_o * TOK + _t0 * 128: d_o * TOK + _t0 * 128 + 512],
                        pb[b][:, 0:512], AF.Identity,
                        bias=bqkv[:, w * DT + d_o: w * DT + d_o + 1]),
                        reads=[f"pb{bank}", f"bq{wi}"], writes=wk_keys)

        for chn in range(4):
            par = chn % 2
            p.emit("P", lambda g, c=chn, pa=par: g.dma_start(
                out=wch(pa).rearrange("p (k f) -> p k f", k=DT),
                in_=wv_d[:, c * 256:(c + 1) * 256].rearrange("(k p) f -> p k f", p=128)),
                writes=[f"wch{par}"], dma=f"w{par}")
            if mac == 0:
                p.group("T", [
                    (lambda te, kk=kk, pa=par: te.matmul(
                        pb[4][0:1, 0:256], b1lb[:, kk:kk + 1], wch(pa)[:, kk * 256:(kk + 1) * 256],
                        start=(kk == 0), stop=(kk == DT - 1)))
                    for kk in range(DT)],
                    reads=[f"wch{par}", "b1lb"], writes=["pb4"])
                p.emit("V", lambda v: v.tensor_copy(PSV[0:1, 2048:2176], pb[4][0:1, 0:128]),
                       reads=["pb4"], writes=["vbr"])
                p.emit("V", lambda v: v.tensor_copy(PSV[0:1, 2176:2304], pb[4][0:1, 128:256]),
                       reads=["pb4"], writes=["vbr2"])
                p.group("T", [lambda te: te.matmul(
                    pb[5][:, 256:384], onesb[0:1, 0:128], PSV[0:1, 2048:2176], start=True, stop=True)],
                    reads=["vbr", "onesb"], writes=["pb5"])
                p.emit("V", lambda v, c=chn: v.tensor_copy(vbias[:, c * 256:c * 256 + 128],
                                                           pb[5][:, 256:384]),
                       reads=["pb5"], writes=["vbias"])
                p.group("T", [lambda te: te.matmul(
                    pb[5][:, 384:512], onesb[0:1, 0:128], PSV[0:1, 2176:2304], start=True, stop=True)],
                    reads=["vbr2", "onesb"], writes=["pb5"])
                p.emit("V", lambda v, c=chn: v.tensor_copy(vbias[:, c * 256 + 128:(c + 1) * 256],
                                                           pb[5][:, 384:512]),
                       reads=["pb5"], writes=["vbias"])
            p.emit("V", lambda v, pa=par: v.tensor_tensor(
                wch(pa)[:, :], wch(pa)[:, :], GPAT[:, :], ALU.mult),
                reads=[f"wch{par}", "GPAT"], writes=[f"wch{par}"])
            for j in range(4):
                t = tt0 + j
                bank = 2 + j % 2
                p.group("T", [
                    (lambda te, kk=kk, pa=par, jj=j, b=bank: te.matmul(
                        pb[b][:, 0:256], xnT(kk, jj), wch(pa)[:, kk * 256:(kk + 1) * 256],
                        start=(kk == 0), stop=(kk == DT - 1)))
                    for kk in range(DT)],
                    reads=[f"wch{par}"] + xnk, writes=[f"pb{bank}"])
                p.emit("V", lambda v, tg=t, c=chn, b=bank: v.tensor_tensor(
                    BUF_C[:, tg * D + c * 256: tg * D + (c + 1) * 256],
                    pb[b][:, 0:256], vbias[:, c * 256:(c + 1) * 256], ALU.add),
                    reads=[f"pb{bank}", "vbias"], writes=[f"C{t}_{chn}"])

        for zi, (zdram, zc) in enumerate(((wb_d, zcol_b), (wg_d, zcol_g))):
            par = zi
            p.emit("P", lambda g, w=zdram, pa=par: g.dma_start(
                out=wch(pa)[:, 0:DT * H].rearrange("p (k f) -> p k f", k=DT),
                in_=w.rearrange("(k p) f -> p k f", p=128)),
                writes=[f"wch{par}"], dma=f"w{par}")
            if mac == 0:
                p.group("T", [
                    (lambda te, kk=kk, pa=par: te.matmul(
                        pb[4][0:1, 256:256 + H], b1lb[:, kk:kk + 1], wch(pa)[:, kk * H:(kk + 1) * H],
                        start=(kk == 0), stop=(kk == DT - 1)))
                    for kk in range(DT)],
                    reads=[f"wch{par}", "b1lb"], writes=["pb4"])
                p.emit("V", lambda v: v.tensor_copy(PSV[0:1, 2304:2304 + H], pb[4][0:1, 256:256 + H]),
                       reads=["pb4"], writes=["zbr"])
                p.group("T", [lambda te: te.matmul(
                    pb[5][:, 256:256 + H], onesb[0:1, 0:128], PSV[0:1, 2304:2304 + H], start=True, stop=True)],
                    reads=["zbr", "onesb"], writes=["pb5"])
                p.emit("V", lambda v, z=zi: v.tensor_copy(zbias[:, z * H:(z + 1) * H],
                                                          pb[5][:, 256:256 + H]),
                       reads=["pb5"], writes=[f"zbias{zi}"])
            p.emit("V", lambda v, pa=par: v.tensor_tensor(
                wch(pa)[:, 0:DT * H], wch(pa)[:, 0:DT * H], GPATZ[:, :], ALU.mult),
                reads=[f"wch{par}", "GPATZ"], writes=[f"wch{par}"])
            for j in range(4):
                bank = 4 + zi
                p.group("T", [
                    (lambda te, kk=kk, pa=par, jj=j, b=bank: te.matmul(
                        pb[b][:, 0:H], xnT(kk, jj), wch(pa)[:, kk * H:(kk + 1) * H],
                        start=(kk == 0), stop=(kk == DT - 1)))
                    for kk in range(DT)],
                    reads=[f"wch{par}"] + xnk, writes=[f"pb{bank}"])
                p.emit("V", lambda v, jj=j, b=bank, z=zi, _zc=zc: v.tensor_tensor(
                    _zc[:, jj * H:(jj + 1) * H], pb[b][:, 0:H], zbias[:, z * H:(z + 1) * H], ALU.add),
                    reads=[f"pb{bank}", f"zbias{zi}"], writes=[f"z{zi}_{j}"])
        for zi, zc in ((0, zcol_b), (1, zcol_g)):
            a_ = SCR_F[:, 0:64]; e_ = SCR_F[:, 64:128]; l_ = SCR_F[:, 128:192]
            zk = [f"z{zi}_{j}" for j in range(4)]
            p.emit("S", lambda s, _zc=zc: s.activation(a_, _zc[:, :], AF.Abs),
                   reads=zk, writes=["ls_a"])
            p.emit("S", lambda s: s.activation(e_, a_, AF.Exp, scale=-1.0),
                   reads=["ls_a"], writes=["ls_e"])
            p.emit("S", lambda s: s.activation(l_, e_, AF.Ln, bias=onec[:, 0:1]),
                   reads=["ls_e", "onec"], writes=["ls_l"])
            p.emit("V", lambda v, _zc=zc: v.tensor_scalar(a_, _zc[:, :], 0.0, None, ALU.min),
                   reads=zk, writes=["ls_a"])
            p.emit("V", lambda v, _zc=zc: v.tensor_tensor(_zc[:, :], a_, l_, ALU.subtract),
                   reads=["ls_a", "ls_l"], writes=zk)
        p.emit("S", lambda s, m=mac: s.activation(bcol[:, m * 64:(m + 1) * 64], zcol_b[:, :], AF.Exp),
               reads=[f"z0_{j}" for j in range(4)], writes=[f"bcol{mac}"])
        for j in range(4):
            t = tt0 + j
            p.group("T", [lambda te, jj=j: te.matmul(
                pb[5][:, 0:H], miu_f[:, :], zcol_g[:, jj * H:(jj + 1) * H], start=True, stop=True)],
                reads=[f"z1_{j}", "miu_f"], writes=["pb5"])
            p.emit("V", lambda v, tg=t: v.tensor_copy(lgcol[:, tg * H:(tg + 1) * H], pb[5][:, 0:H]),
                   reads=["pb5"], writes=[f"lgcol{t}"])
            p.group("T", [lambda te, jj=j: te.matmul(
                pb[5][:, H:2 * H], ones128[:, :], zcol_g[:, jj * H:(jj + 1) * H], start=True, stop=True)],
                reads=[f"z1_{j}", "ones128"], writes=["pb5"])
            p.emit("V", lambda v, tg=t: v.tensor_copy(lgcc[:, tg * H:(tg + 1) * H], pb[5][:, H:2 * H]),
                   reads=["pb5"], writes=[f"lgcc{t}"])

    if debug == "A1":
        p.fence()
        for nm, buf in (("qT", BUF_A), ("kT", BUF_B), ("v", BUF_C)):
            dd = add_dbg(nm, [128, 16384])
            p.emit("P", lambda g, d=dd, b=buf: g.dma_start(out=d, in_=b[:, :]), dma="dbg")
        for nm, buf in (("lgcol", lgcol), ("lgcc", lgcc)):
            dd = add_dbg(nm, [128, 256])
            p.emit("Y", lambda y, d=dd, b=buf: y.dma_start(out=d, in_=b[:, :]), dma=True)
        add_dbg("bcol", [128, 256])
        p.emit("P", lambda g: g.dma_start(out=dbg["bcol"], in_=bcol[:, :]), dma="dbg")
        add_dbg("scrb", [128, 8192])
        p.emit("P", lambda g: g.dma_start(out=dbg["scrb"], in_=SCR_B[:, :]), dma="dbg")
        add_dbg("stats", [128, 64])
        p.emit("Y", lambda y: y.dma_start(out=dbg["stats"], in_=stats[:, :]), dma=True)
        add_dbg("xbuf", [128, NT * D])
        p.emit("Y", lambda y: y.dma_start(out=dbg["xbuf"], in_=BUF_X[:, :]), dma=True)
        p.fence(); p.final_waits()
        return _finish(nc, p, ctx, dbg)

    p.fence()

    def scrb(par, a, b):
        return SCR_B[:, par * 2048 + a: par * 2048 + b]

    p.emit("V", lambda v: v.memset(slocb[:, :], 0.0), writes=[f"sl{h}" for h in range(H)])
    for h in range(H):
        p.emit("V", lambda v, hh=h: v.tensor_copy(
            hm(pib, hh), identf[pr(hh):pr(hh) + 64, pr(hh):pr(hh) + 64]),
            reads=["identf"], writes=[f"pi{h}"])

    import os as _os
    _NTU = int(_os.environ.get("KUNITS", NT))

    def emit_unit(c, h):
            par = (c * H + h) % 2
            b0 = pr(h)
            TMP1 = SCR_F[:, 0:128] if par == 0 else SCR_F[:, 768:896]
            pbA = pb[3 * par]
            pbB = pb[3 * par + 1]
            pbC = pb[3 * par + 2]
            kA = f"pb{3 * par}"; kB = f"pb{3 * par + 1}"; kC = f"pb{3 * par + 2}"
            DMP = SCR_F[:, 128 + par * 128:256 + par * 128]
            DM = DMP
            XF = SCR_F[:, 384 + par * 128:512 + par * 128]
            RN = SCR_F[b0:b0 + 1, 640:768]
            POW = [scrb(par, i * 128, (i + 1) * 128) for i in range(7)]
            TT = scrb(par, 896, 1024)
            khT = SCR_B[b0:b0 + 64, par * 2048 + 1024: par * 2048 + 1152]
            QtT = SCR_B[b0:b0 + 64, par * 2048 + 1152: par * 2048 + 1280]
            Kc = scrb(par, 1280, 1344)
            KG = scrb(par, 1344, 1408)
            BXv = scrb(par, 1408, 1472)
            BW = scrb(par, 1472, 1536)
            BWT = SCR_B[b0:b0 + 64, par * 2048 + 1536: par * 2048 + 1664]
            Rt = scrb(par, 1664, 1728)
            PT = scrb(par, 1728, 1856)
            Gb64 = SCR_B[b0:b0 + 64, par * 2048 + 1856: par * 2048 + 1984]
            K2 = SCR_B[b0:b0 + 64, 4096:4224]
            XB = SCR_B[:, 4224 + par * 128: 4352 + par * 128]
            RNb = SCR_B[b0:b0 + 1, 4480:4608]
            AcT = SCR_B[b0:b0 + 64, 4608:4672]
            u = f"{par}"
            kq = f"A{h}_{c}"; kk_ = f"B{h}_{c}"; kv_ = f"C{c}_{h // 4}"
            lgc_ = cs(lgcol, c, h); bc_ = cs(bcol, c, h); lgcc_ = cs(lgcc, c, h)
            gcc_ = stats[b0:b0 + 64, 52 + 4 * par:53 + 4 * par]

            p.emit("V", lambda v, l=lgc_: v.tensor_scalar(TMP1, ones128[:, :], l, None, ALU.mult),
                   reads=[f"lgcol{c}", "ones128"], writes=[f"TMP1{par}"])
            p.group("T", [lambda te, _pa=pbA: te.transpose(_pa[:, 0:128], TMP1, identf[:, :])],
                    reads=[f"TMP1{par}", "identf"], writes=[kA])
            p.emit("V", lambda v, l=lgc_, _d=DMP, _pa=pbA: v.tensor_scalar(
                _d, _pa[:, 0:128], l, 0.0, ALU.subtract, ALU.min),
                reads=[kA, f"lgcol{c}"], writes=[f"DMP{u}"])
            p.emit("S", lambda s, _d=DMP: s.activation(_d, _d, AF.Exp),
                   reads=[f"DMP{u}"], writes=[f"DM{u}"])
            p.emit("S", lambda s, _g=Gb64, _b=b0, _pa=pbA: s.activation(_g, _pa[_b:_b + 64, 0:128], AF.Exp),
                   reads=[kA], writes=[f"Gb{u}"])
            p.emit("V", lambda v, _q=QtT, _g=Gb64, ap=Aview(h, c): v.scalar_tensor_tensor(
                _q, ap, 0.125, _g, ALU.mult, ALU.mult),
                reads=[kq, f"Gb{u}"], writes=[f"QtT{u}"])
            p.emit("V", lambda v, _k=K2, ap=Bview(h, c): v.tensor_tensor(_k, ap, ap, ALU.mult),
                   reads=[kk_], writes=[f"K2{u}"])
            p.group("T", [lambda te, _k=K2, _b=b0, _pa=pbA: te.matmul(
                _pa[_b:_b + 1, 128:256], onesb[_b:_b + 64, 0:1], _k, start=True, stop=True)],
                reads=[f"K2{u}", "onesb"], writes=[kA])
            p.emit("S", lambda s, _r=RN, _b=b0, _pa=pbA: s.activation(_r, _pa[_b:_b + 1, 128:256], AF.Ln),
                   reads=[kA], writes=[f"RN{u}"])
            p.emit("S", lambda s, _r=RN: s.activation(_r, _r, AF.Exp, scale=0.5),
                   reads=[f"RN{u}"], writes=[f"RN{u}"])
            p.emit("V", lambda v, _r=RN: v.tensor_scalar(_r, _r, 1e-6, None, ALU.add),
                   reads=[f"RN{u}"], writes=[f"RN{u}"])
            def _recip(v, _r=RN, _o=RNb):
                with nc.allow_low_precision(reason="bf16 k-norm reciprocal row"):
                    return v.reciprocal(_o, _r)
            p.emit("V", _recip, reads=[f"RN{u}"], writes=[f"RNb{u}"])
            p.group("T", [lambda te, _o=RNb, _b=b0, _pa=pbA: te.matmul(
                _pa[_b:_b + 64, 256:384], onesb[_b:_b + 1, 0:64], _o, start=True, stop=True)],
                reads=[f"RNb{u}", "onesb"], writes=[kA])
            p.emit("V", lambda v, _k=khT, ap=Bview(h, c), _b=b0, _pa=pbA: v.tensor_tensor(
                _k, ap, _pa[_b:_b + 64, 256:384], ALU.mult),
                reads=[kk_, kA], writes=[f"khT{u}"])
            p.group("T", [lambda te, _k=khT, _pb=pbB: te.matmul(_pb[:, 0:128], _k, _k, start=True, stop=True)],
                    reads=[f"khT{u}"], writes=[kB])
            p.emit("V", lambda v, _d=DM, _pb=pbB: v.tensor_tensor(TMP1, _pb[:, 0:128], _d, ALU.mult),
                   reads=[kB, f"DM{u}", f"TMP1{par}"], writes=[f"TMP1{par}"])
            p.emit("V", lambda v: v.tensor_tensor(TMP1, TMP1, msu[:, :], ALU.mult),
                   reads=[f"TMP1{par}", "msu"], writes=[f"TMP1{par}"])
            p.emit("V", lambda v, _p=POW[0], b=bc_: v.tensor_scalar(
                _p, TMP1, b, -1.0, ALU.mult, ALU.mult),
                reads=[f"TMP1{par}", f"bcol{c // 4}"], writes=[f"POW0{u}"])
            DPW = [SCR_B[:, 4800 + par * 768 + i * 128: 4800 + par * 768 + (i + 1) * 128]
                   for i in range(6)]
            p.group("T", [lambda te, _p=POW[0]: te.transpose(pbtA[:, 0:128], _p, identb[:, :])],
                    reads=[f"POW0{u}", "identb"], writes=["pbtA"])
            p.emit("V", lambda v, _d=DPW[0]: v.tensor_copy(_d, pbtA[:, 0:128]),
                   reads=["pbtA"], writes=[f"DP0{u}"])
            for jj in range(6):
                p.group("T", [lambda te, _d=DPW[jj], _p=POW[jj], _pc=pbC: te.matmul(
                    _pc[:, 0:128], _d, _p, start=True, stop=True)],
                    reads=[f"DP{jj}{u}", f"POW{jj}{u}"], writes=[kC])
                p.emit("V", lambda v, _p=POW[jj + 1], _pc=pbC: v.tensor_copy(_p, _pc[:, 0:128]),
                       reads=[kC], writes=[f"POW{jj + 1}{u}"])
                if jj < 5:
                    p.group("T", [lambda te, _d=DPW[jj], _p=POW[jj], _pb=pbB: te.matmul(
                        _pb[:, 384:512], _p, _d, start=True, stop=True)],
                        reads=[f"DP{jj}{u}", f"POW{jj}{u}"], writes=[kB])
                    p.emit("V", lambda v, _d=DPW[jj + 1], _pb=pbB: v.tensor_copy(_d, _pb[:, 384:512]),
                           reads=[kB], writes=[f"DP{jj + 1}{u}"])
            p.group("T", [lambda te, _k=khT, _b=b0: te.transpose(
                pbtA[:, 128:192], _k, identb[_b:_b + 64, _b:_b + 64])],
                    reads=[f"khT{u}", "identb"], writes=["pbtA"])
            p.emit("V", lambda v, _k=Kc: v.tensor_copy(_k, pbtA[:, 128:192]),
                   reads=["pbtA"], writes=[f"Kc{u}"])
            gcol_ = stats[:, 20 + 4 * par:21 + 4 * par]
            p.emit("S", lambda s, l=lgc_, _g=gcol_: s.activation(_g, l, AF.Exp),
                   reads=[f"lgcol{c}"], writes=[f"gcol{u}"])
            p.emit("V", lambda v, _b=XB, ap=vview(h, c): v.tensor_copy(_b[:, 0:64], ap),
                   reads=[kv_], writes=[f"XB{u}"])
            p.emit("V", lambda v, _b=XB, _k=Kc, _g=gcol_: v.tensor_scalar(
                _b[:, 64:128], _k, _g, None, ALU.mult),
                reads=[f"Kc{u}", f"gcol{u}"], writes=[f"XB{u}"])
            regs = [(128, 256), (256, 384)]
            p.group("T", [
                (lambda te, _b=XB, _pc=pbC: te.matmul(
                    _pc[:, 128:256], identb[:, :], _b, start=True, stop=False)),
                (lambda te, _p=POW[6], _b=XB, _pc=pbC: te.matmul(
                    _pc[:, 128:256], _p, _b, start=False, stop=True))],
                reads=[f"XB{u}", f"POW6{u}", "identb"], writes=[kC])
            cur = 0
            for jj in range(5, -1, -1):
                r0 = regs[cur]
                r1 = regs[1 - cur]
                p.emit("V", lambda v, _b=XB, _pc=pbC, _r=r0: v.tensor_copy(_b, _pc[:, _r[0]:_r[1]]),
                       reads=[kC], writes=[f"XB{u}"])
                p.group("T", [
                    (lambda te, _b=XB, _pc=pbC, _r=r1: te.matmul(
                        _pc[:, _r[0]:_r[1]], identb[:, :], _b, start=True, stop=False)),
                    (lambda te, _p=POW[jj], _b=XB, _pc=pbC, _r=r1: te.matmul(
                        _pc[:, _r[0]:_r[1]], _p, _b, start=False, stop=True))],
                    reads=[f"POW{jj}{u}", f"XB{u}", "identb"], writes=[kC])
                cur = 1 - cur
            xf0 = regs[cur][0]
            p.emit("V", lambda v, _o=BXv, b=bc_, _pc=pbC, _x=xf0: v.tensor_scalar(
                _o, _pc[:, _x:_x + 64], b, None, ALU.mult),
                reads=[kC, f"bcol{c // 4}"], writes=[f"BXv{u}"])
            p.emit("V", lambda v, _o=BW, b=bc_, _pc=pbC, _x=xf0: v.tensor_scalar(
                _o, _pc[:, _x + 64:_x + 128], b, -1.0, ALU.mult, ALU.mult),
                reads=[kC, f"bcol{c // 4}"], writes=[f"BW{u}"])
            p.group("T", [lambda te, _k=khT, ap=Aview(h, c), _pb=pbB: te.matmul(
                _pb[:, 128:256], _k, ap, start=True, stop=True)],
                reads=[f"khT{u}", kq], writes=[kB])
            p.emit("V", lambda v, _d=DM, _pb=pbB: v.tensor_tensor(TMP1, _pb[:, 128:256], _d, ALU.mult),
                   reads=[kB, f"DM{u}", f"TMP1{par}"], writes=[f"TMP1{par}"])
            p.emit("V", lambda v, _p=PT: v.scalar_tensor_tensor(
                _p, TMP1, 0.125, miu_b[:, :], ALU.mult, ALU.mult),
                reads=[f"TMP1{par}", "miu_b"], writes=[f"PT{u}"])
            p.group("T", [lambda te, _w=BW, _p=PT, _b=b0, _pb=pbB: te.matmul(
                _pb[_b:_b + 64, 256:384], _w, _p, start=True, stop=True)],
                reads=[f"BW{u}", f"PT{u}"], writes=[kB])
            p.emit("V", lambda v, _q=QtT, ap=Aview(h, c), _b=b0, _pb=pbB: v.tensor_tensor(
                ap, _q, _pb[_b:_b + 64, 256:384], ALU.add),
                reads=[f"QtT{u}", kB], writes=[kq])
            p.group("T", [lambda te, _w=BW, _b=b0: te.transpose(
                pbtA[_b:_b + 64, 192:320], _w, identb[:, :])],
                reads=[f"BW{u}", "identb"], writes=["pbtA"])
            p.emit("V", lambda v, _w=BWT, _b=b0: v.tensor_copy(_w, pbtA[_b:_b + 64, 192:320]),
                   reads=["pbtA"], writes=[f"BWT{u}"])
            p.group("T", [lambda te, _w=BWT, hh=h, _pb=pbB: te.matmul(
                _pb[:, 384:448], _w, hm(slocb, hh), start=True, stop=True)],
                reads=[f"BWT{u}", f"sl{h}"], writes=[kB])
            p.emit("V", lambda v, _r=Rt, _x=BXv, _pb=pbB: v.tensor_tensor(_r, _x, _pb[:, 384:448], ALU.add),
                   reads=[f"BXv{u}", kB], writes=[f"Rt{u}"])
            p.group("T", [
                (lambda te, _r=Rt, _p=PT, _b=b0, _pb=pbB: te.matmul(
                    _pb[_b:_b + 64, 0:128], _r, _p, start=True, stop=False)),
                (lambda te, hh=h, _q=QtT, _b=b0, _pb=pbB: te.matmul(
                    _pb[_b:_b + 64, 0:128], hm(slocb, hh), _q, start=False, stop=True))],
                reads=[f"Rt{u}", f"PT{u}", f"sl{h}", f"QtT{u}"], writes=[kB])
            p.emit("S", lambda s, ap=Bview(h, c), _b=b0, _pb=pbB: s.copy(ap, _pb[_b:_b + 64, 0:128]),
                   reads=[kB], writes=[kk_])
            grc = stats[:, 28 + 4 * par:29 + 4 * par]
            p.emit("S", lambda s, l=lgc_, lc=lgcc_, _g=grc: s.activation(
                _g, l, AF.Exp, scale=-1.0, bias=lc),
                reads=[f"lgcol{c}", f"lgcc{c}"], writes=[f"grc{u}"])
            p.emit("V", lambda v, _k=Kc, _o=KG, _g=grc: v.tensor_scalar(_o, _k, _g, None, ALU.mult),
                   reads=[f"Kc{u}", f"grc{u}"], writes=[f"KG{u}"])
            p.group("T", [lambda te, _w=BW, _k=KG, _b=b0, _pa=pbA: te.matmul(
                _pa[_b:_b + 64, 384:448], _w, _k, start=True, stop=True)],
                reads=[f"BW{u}", f"KG{u}"], writes=[kA])
            p.emit("S", lambda s, lc=lgcc_, g=stats[:, 52 + 4 * par:53 + 4 * par]: s.activation(
                g, lc, AF.Exp),
                reads=[f"lgcc{c}"], writes=[f"gccs{u}"])
            p.emit("V", lambda v, _a=AcT, g=gcc_, _b=b0, _pa=pbA: v.scalar_tensor_tensor(
                _a, identf[_b:_b + 64, _b:_b + 64], g, _pa[_b:_b + 64, 384:448],
                ALU.mult, ALU.add),
                reads=[kA, f"gccs{u}", "identf"], writes=[f"AcT{u}"])
            p.group("T", [lambda te, hh=h, _b=b0: te.transpose(
                pbtB[_b:_b + 64, 0:64], hm(pib, hh), identb[_b:_b + 64, _b:_b + 64])],
                reads=[f"pi{h}", "identb"], writes=["pbtB"])
            p.emit("V", lambda v, hh=h, cc=c, _b=b0: v.tensor_copy(psvv(hh, cc), pbtB[_b:_b + 64, 0:64]),
                   reads=["pbtB"], writes=[f"psv{h}_{c}"])
            p.group("T", [lambda te, _a=AcT, hh=h, _b=b0, _pa=pbA: te.matmul(
                _pa[_b:_b + 64, 192:256], _a, hm(pib, hh), start=True, stop=True)],
                reads=[f"AcT{u}", f"pi{h}"], writes=[kA])
            p.emit("V", lambda v, hh=h, _b=b0, _pa=pbA: v.tensor_copy(hm(pib, hh), _pa[_b:_b + 64, 192:256]),
                   reads=[kA], writes=[f"pi{h}"])
            p.group("T", [
                (lambda te, _k=KG, _x=BXv, _b=b0, _pa=pbA: te.matmul(
                    _pa[_b:_b + 64, 448:512], _k, _x, start=True, stop=False)),
                (lambda te, _a=AcT, hh=h, _b=b0, _pa=pbA: te.matmul(
                    _pa[_b:_b + 64, 448:512], _a, hm(slocb, hh), start=False, stop=True))],
                reads=[f"KG{u}", f"BXv{u}", f"AcT{u}", f"sl{h}"], writes=[kA])
            p.emit("V", lambda v, hh=h, _b=b0, _pa=pbA: v.tensor_copy(hm(slocb, hh), _pa[_b:_b + 64, 448:512]),
                   reads=[kA], writes=[f"sl{h}"])

    for c in range(_NTU):
        for hp in range(0, H, 2):
            p.cap = []
            emit_unit(c, hp)
            l0 = p.cap
            p.cap = []
            emit_unit(c, hp + 1)
            l1 = p.cap
            p.cap = None
            p.replay_rr([l0, l1])

    for h in range(H):
        b0 = pr(h)
        p.group("T", [lambda te, hh=h, _b=b0: te.transpose(
            pbtB[_b:_b + 64, 64:128], hm(pib, hh), identb[_b:_b + 64, _b:_b + 64])],
            reads=[f"pi{h}", "identb"], writes=["pbtB"])
        p.emit("V", lambda v, hh=h, _b=b0: v.tensor_copy(
            BUF_C[_b:_b + 64, (hh // 2) * 64:(hh // 2) * 64 + 64],
            pbtB[_b:_b + 64, 64:128]),
            reads=["pbtB"], writes=[f"pif{h}"])
    for pa in range(2):
        p.emit("P", lambda g, _p=pa: g.dma_start(
            out=cc_in_d[_p * 8:(_p + 1) * 8, :].rearrange("m (a b) -> a m b", a=64),
            in_=BUF_C[_p * 64:_p * 64 + 64, 0:512].rearrange("a (m b) -> a m b", m=8)),
            reads=[f"pif{h}" for h in range(H)], writes=["ccin"], dma="dp")
        p.emit("P", lambda g, _p=pa: g.dma_start(
            out=cc_in_d[16 + _p * 8:16 + (_p + 1) * 8, :].rearrange("m (a b) -> a m b", a=64),
            in_=slocb[_p * 64:_p * 64 + 64, :].rearrange("a (m b) -> a m b", m=8)),
            reads=[f"sl{h}" for h in range(H)], writes=["ccin"], dma="dp")
    import os as _os
    if _os.environ.get("KSKIP_CC"):
        for _r in range(4):
            p.emit("P", lambda g, rr=_r: g.dma_start(out=cc_out_d[rr * 32:(rr + 1) * 32, :], in_=cc_in_d),
                   reads=["ccin"], writes=["ccout"], dma="dp")
    else:
        p.emit("P", lambda g: g.collective_compute(
            "AllGather", ALU.bypass, replica_groups=[[0, 1, 2, 3], [4, 5, 6, 7]],
            ins=[cc_in_d.opt()], outs=[cc_out_d.opt()]),
            reads=["ccin"], writes=["ccout"], collective=True)
    GB0 = 512
    for j4 in range(4):
        for kind in range(2):
            for pa in range(2):
                p.emit("P", lambda g, jj=j4, kd=kind, _p=pa: g.dma_start(
                    out=BUF_C[_p * 64:_p * 64 + 64,
                              GB0 + (jj * 16 + kd * 8) * 64:GB0 + (jj * 16 + kd * 8 + 8) * 64
                              ].rearrange("a (m b) -> a m b", m=8),
                    in_=cc_out_d[jj * 32 + kd * 16 + _p * 8: jj * 32 + kd * 16 + (_p + 1) * 8, :
                                 ].rearrange("m (a b) -> a m b", a=64)),
                    reads=["ccout"], writes=[f"G{j4}_{kind}_{pa}"], dma="dp")

    p.retoken([f"G{j}_{k}_{pa}" for j in range(4) for k in range(2) for pa in range(2)])

    def gmat(j4, kind, h):
        base = GB0 + (j4 * 16 + kind * 8 + h // 2) * 64
        return BUF_C[pr(h):pr(h) + 64, base:base + 64]

    for h in range(H):
        b0 = pr(h)
        TMA = SCR_F[b0:b0 + 64, 0:64]
        AeffT = SCR_B[b0:b0 + 64, 4672:4736]
        p.emit("V", lambda v, hh=h: v.memset(hm(s0b, hh), 0.0), writes=[f"s0{h}"])
        gks = [f"G{j}_{k}_{h % 2}" for j in range(4) for k in range(2)]
        for j4 in range(4):
            p.emit("V", lambda v, jj=j4, hh=h, _t=TMA, _b=b0: v.tensor_scalar(
                _t, gmat(jj, 0, hh), cmask[_b:_b + 64, jj:jj + 1], None, ALU.mult),
                reads=gks + ["cmask"], writes=["TMA"])
            p.emit("V", lambda v, jj=j4, _t=TMA, _a=AeffT, _b=b0: v.scalar_tensor_tensor(
                _a, identf[_b:_b + 64, _b:_b + 64], cmask[_b:_b + 64, 4 + jj:5 + jj], _t,
                ALU.mult, ALU.add),
                reads=["TMA", "cmask", "identf"], writes=["AeffT"])
            p.group("T", [lambda te, hh=h, _a=AeffT, _b=b0: te.matmul(
                pb[3 * (hh % 2)][_b:_b + 64, 320:384], _a, hm(s0b, hh), start=True, stop=True)],
                reads=["AeffT", f"s0{h}"], writes=[f"pb{3 * (h % 2)}"])
            p.emit("V", lambda v, jj=j4, hh=h, _t=TMA, _b=b0: v.tensor_scalar(
                _t, gmat(jj, 1, hh), cmask[_b:_b + 64, jj:jj + 1], None, ALU.mult),
                reads=gks + ["cmask"], writes=["TMA"])
            p.emit("V", lambda v, hh=h, _t=TMA, _b=b0: v.tensor_tensor(
                hm(s0b, hh), _t, pb[3 * (hh % 2)][_b:_b + 64, 320:384], ALU.add),
                reads=["TMA", f"pb{3 * (h % 2)}"], writes=[f"s0{h}"])
    for c in range(_NTU):
        for h in range(H):
            b0 = pr(h)
            DSb = SCR_B[b0:b0 + 64, 4736:4800]
            p.group("T", [lambda te, hh=h, cc=c, _b=b0: te.matmul(
                pb[3 * (hh % 2)][_b:_b + 64, 320:384], psvv(hh, cc), hm(s0b, hh), start=True, stop=True)],
                reads=[f"psv{h}_{c}", f"s0{h}"], writes=[f"pb{3 * (h % 2)}"])
            p.emit("V", lambda v, _d=DSb, _b=b0, hh=h: v.tensor_copy(
                _d, pb[3 * (hh % 2)][_b:_b + 64, 320:384]),
                   reads=[f"pb{3 * (h % 2)}"], writes=["DSb"])
            p.group("T", [lambda te, hh=h, cc=c, _d=DSb, _b=b0: te.matmul(
                pb[3 * (hh % 2) + 1][_b:_b + 64, 256:384], _d, Aview(hh, cc), start=True, stop=True)],
                reads=["DSb", f"A{h}_{c}"], writes=[f"pb{3 * (h % 2) + 1}"])
            p.emit("V", lambda v, hh=h, cc=c, _b=b0: v.tensor_tensor(
                Bview(hh, cc), Bview(hh, cc), pb[3 * (hh % 2) + 1][_b:_b + 64, 256:384], ALU.add),
                reads=[f"B{h}_{c}", f"pb{3 * (h % 2) + 1}"], writes=[f"B{h}_{c}"])

    if debug == "A2":
        p.fence()
        add_dbg("OT", [128, 16384])
        p.emit("P", lambda g: g.dma_start(out=dbg["OT"], in_=BUF_B[:, :]), dma="dbg")
        add_dbg("JT", [128, 16384])
        p.emit("P", lambda g: g.dma_start(out=dbg["JT"], in_=BUF_A[:, :]), dma="dbg")
        add_dbg("s0b", [128, 512])
        p.emit("P", lambda g: g.dma_start(out=dbg["s0b"], in_=s0b[:, :]), dma="dbg")
        add_dbg("slocb", [128, 512])
        p.emit("P", lambda g: g.dma_start(out=dbg["slocb"], in_=slocb[:, :]), dma="dbg")
        p.fence(); p.final_waits()
        return _finish(nc, p, ctx, dbg)

    p.fence()

    bob_bf = SCR_B[:, 0:1024]
    p.emit("P", lambda g: g.dma_start(out=bob_bf, in_=bob_d), writes=["bob"], dma="dp")

    def wochk(par):
        return BUF_A[:, 8192 + par * 4096: 8192 + (par + 1) * 4096]

    for chn in range(2):
        par = chn % 2
        p.emit("P", lambda g, c=chn, pa=par: g.dma_start(
            out=wochk(pa).rearrange("p (k f) -> p k f", k=DT),
            in_=wo_d[:, c * 512:(c + 1) * 512].rearrange("(k p) f -> p k f", p=128)),
            writes=[f"woch{par}"], dma=f"w{par}")
        for t in range(NT):
            bank = t % 2
            p.group("T", [
                (lambda te, kk=kk, pa=par, tg=t, b=bank: te.matmul(
                    pb[b][:, 0:512],
                    BUF_B[:, kk * TOK + tg * 128: kk * TOK + (tg + 1) * 128],
                    wochk(pa)[:, kk * 512:(kk + 1) * 512],
                    start=(kk == 0), stop=(kk == DT - 1)))
                for kk in range(DT)],
                reads=[f"woch{par}"] + [f"B{hh}_{t}" for hh in range(H)],
                writes=[f"pb{bank}"])
            p.emit("V", lambda v, tg=t, c=chn, b=bank: v.tensor_tensor(
                xv(tg)[:, c * 512:(c + 1) * 512], xv(tg)[:, c * 512:(c + 1) * 512],
                pb[b][:, 0:512], ALU.add),
                reads=[f"pb{bank}", f"x{t}"], writes=[f"x{t}"])
    for t in range(NT):
        p.emit("V", lambda v, tg=t: v.tensor_tensor(xv(tg), xv(tg), bob_bf, ALU.add),
               reads=["bob", f"x{t}"], writes=[f"x{t}"])

    if debug == "A3":
        p.fence()
        add_dbg("x1", [128, NT * D])
        p.emit("Y", lambda y: y.dma_start(out=dbg["x1"], in_=BUF_X[:, :]), dma=True)
        p.fence(); p.final_waits()
        return _finish(nc, p, ctx, dbg)

    p.fence()

    def w2t(r):
        if r < 16:
            return BUF_B[:, r * 1024:(r + 1) * 1024]
        return BUF_C[:, (r - 16) * 1024:(r - 16 + 1) * 1024]

    for r in range(32):
        p.emit("P", lambda g, rr=r: g.dma_start(
            out=w2t(rr), in_=w2_d[rr * 128:(rr + 1) * 128, :]),
            writes=[f"w2c{r}"], dma="dp")
    b2b_bf = SCR_B[:, 7168:8192]
    p.emit("P", lambda g: g.dma_start(out=b2b_bf, in_=b2b_d), writes=["b2b"], dma="dp")
    p.retoken([f"w2c{r}" for r in range(32)] + ["b2b"])

    def xn2T(dt_, j):
        return SCR_B[:, dt_ * 256 + j * 128: dt_ * 256 + (j + 1) * 128]

    xntr2 = SCR_B[:, 2048:3072]

    def w1ch(par):
        return SCR_B[:, 3072 + par * 2048: 3072 + (par + 1) * 2048]

    def h1t(ft):
        return PSV[:, ft * 256: (ft + 1) * 256]

    for hmc in range(8):
        tt0 = hmc * 2
        for j in range(2):
            t = tt0 + j
            emit_ln(t, xntr2[:, :], [f"xn2_{t}"])
            for dt_ in range(DT):
                p.group("T", [lambda te, d=dt_: te.transpose(
                    pbtA[:, 0:128], xntr2[:, d * 128:(d + 1) * 128], identb[:, :])],
                    reads=[f"xn2_{t}", "identb"], writes=["pbtA"])
                p.emit("V", lambda v, d=dt_, jj=j: v.tensor_copy(xn2T(d, jj), pbtA[:, 0:128]),
                       reads=["pbtA"], writes=[f"x2T{t}_{dt_}"])
        xnk = [f"x2T{tt0 + j}_{d}" for j in range(2) for d in range(DT)]
        for chn in range(16):
            par = chn % 2
            p.emit("P", lambda g, c=chn, pa=par: g.dma_start(
                out=w1ch(pa).rearrange("p (k f) -> p k f", k=DT),
                in_=w1_d[:, c * 256:(c + 1) * 256].rearrange("(k p) f -> p k f", p=128)),
                writes=[f"w1ch{par}"], dma=f"w{par}")
            if hmc == 0:
                for sub in range(2):
                    ft = chn * 2 + sub
                    p.group("T", [
                        (lambda te, kk=kk, s_=sub, pa=par, f=ft: te.matmul(
                            pb[4][:, (f % 2):(f % 2) + 1],
                            w1ch(pa)[:, kk * 256 + s_ * 128: kk * 256 + (s_ + 1) * 128],
                            b2lb[:, kk:kk + 1], start=(kk == 0), stop=(kk == DT - 1)))
                        for kk in range(DT)],
                        reads=[f"w1ch{par}", "b2lb"], writes=["pb4"])
                    p.emit("V", lambda v, f=ft: v.tensor_tensor(
                        b1tot[:, f:f + 1], b1c[:, f:f + 1], pb[4][:, (f % 2):(f % 2) + 1], ALU.add),
                        reads=["pb4", "b1c"], writes=["b1tot"])
            for kk in range(DT):
                p.emit("V", lambda v, k2=kk, pa=par: v.tensor_scalar(
                    w1ch(pa)[:, k2 * 256:(k2 + 1) * 256], w1ch(pa)[:, k2 * 256:(k2 + 1) * 256],
                    g2c[:, k2:k2 + 1], None, ALU.mult),
                    reads=[f"w1ch{par}", "g2c"], writes=[f"w1ch{par}"])
            for sub in range(2):
                ft = chn * 2 + sub
                bank = ft % 2
                p.group("T", [
                    (lambda te, kk=kk, s_=sub, pa=par, b=bank: te.matmul(
                        pb[b][:, 0:256],
                        w1ch(pa)[:, kk * 256 + s_ * 128: kk * 256 + (s_ + 1) * 128],
                        SCR_B[:, kk * 256:(kk + 1) * 256],
                        start=(kk == 0), stop=(kk == DT - 1)))
                    for kk in range(DT)],
                    reads=[f"w1ch{par}"] + xnk, writes=[f"pb{bank}"])
                p.emit("S", lambda s, f=ft, b=bank: s.activation(
                    h1t(f), pb[b][:, 0:256], AF.Gelu, bias=b1tot[:, f:f + 1]),
                    reads=[f"pb{bank}", "b1tot"], writes=[f"h1_{ft}"])
        for j in range(2):
            t = tt0 + j
            for dc in range(2):
                bank = 2 + (j * 2 + dc) % 2
                p.group("T", [
                    (lambda te, kk=kk, jj=j, d=dc, b=bank: te.matmul(
                        pb[b][:, 0:512], h1t(kk)[:, jj * 128:(jj + 1) * 128],
                        w2t(kk)[:, d * 512:(d + 1) * 512],
                        start=(kk == 0), stop=(kk == 31)))
                    for kk in range(32)],
                    reads=[f"h1_{kk}" for kk in range(32)] + [f"w2c{kk}" for kk in range(32)],
                    writes=[f"pb{bank}"])
                p.emit("V", lambda v, tg=t, d=dc, b=bank: v.tensor_tensor(
                    xv(tg)[:, d * 512:(d + 1) * 512], xv(tg)[:, d * 512:(d + 1) * 512],
                    pb[b][:, 0:512], ALU.add),
                    reads=[f"pb{bank}", f"x{t}"], writes=[f"x{t}"])
            p.emit("V", lambda v, tg=t: v.tensor_tensor(xv(tg), xv(tg), b2b_bf, ALU.add),
                   reads=["b2b", f"x{t}"], writes=[f"x{t}"])

    if debug == "A4":
        p.fence()
        add_dbg("x1", [128, NT * D])
        p.emit("Y", lambda y: y.dma_start(out=dbg["x1"], in_=BUF_X[:, :]), dma=True)
        p.fence(); p.final_waits()
        return _finish(nc, p, ctx, dbg)

    p.fence()

    BUF_Bf = BUF_B[:, :].bitcast(F32)
    BUF_Cf = BUF_C[:, :].bitcast(F32)

    def xorig(t):
        if t < 8:
            return BUF_Bf[:, t * D:(t + 1) * D]
        return BUF_Cf[:, (t - 8) * D:(t - 7) * D]

    p.emit("Y", lambda y: y.dma_start(
        out=BUF_Bf[:, 0:8 * D].rearrange("p (t d) -> p t d", t=8),
        in_=x_d[0:8 * 128, :].rearrange("(t p) d -> p t d", p=128)),
        writes=[f"xo{t}" for t in range(8)], dma=True)
    p.emit("Y", lambda y: y.dma_start(
        out=BUF_Cf[:, 0:8 * D].rearrange("p (t d) -> p t d", t=8),
        in_=x_d[8 * 128:16 * 128, :].rearrange("(t p) d -> p t d", p=128)),
        writes=[f"xo{t}" for t in range(8, 16)], dma=True)
    p.retoken([f"xo{t}" for t in range(NT)])
    xtr = SCR_B[:, 0:1024]
    for t in range(NT):
        p.emit("V", lambda v, tg=t: v.tensor_tensor(xv(tg), xv(tg), xorig(tg), ALU.add),
               reads=[f"x{t}", f"xo{t}"], writes=[f"x{t}"])
        emit_ln(t, xtr[:, :], [f"x2b{t}"])
        p.emit("V", lambda v, tg=t: v.tensor_scalar(
            xv(tg), xv(tg), stats[:, 0:1], stats[:, 4:5], ALU.add, ALU.mult),
            reads=[f"x{t}", "st_neg", "st_rst"], writes=[f"x{t}"])
        for dt_ in range(DT):
            p.group("T", [lambda te, d=dt_: te.transpose(
                pbtA[:, 0:128], xtr[:, d * 128:(d + 1) * 128], identb[:, :])],
                reads=[f"x2b{t}", "identb"], writes=["pbtA"])
            p.emit("V", lambda v, d=dt_, tg=t: v.tensor_copy(
                BUF_A[:, d * TOK + tg * 128: d * TOK + (tg + 1) * 128], pbtA[:, 0:128]),
                reads=["pbtA"], writes=[f"xT2_{t}_{dt_}"])

    if debug == "A5":
        p.fence()
        add_dbg("x2", [128, NT * D])
        p.emit("Y", lambda y: y.dma_start(out=dbg["x2"], in_=BUF_X[:, :]), dma=True)
        p.fence(); p.final_waits()
        return _finish(nc, p, ctx, dbg)

    p.fence()

    for t in range(NT):
        bank = t % 2
        p.group("T", [
            (lambda te, kk=kk, tg=t, b=bank: te.matmul(
                pb[b][:, 0:NE], BUF_A[:, kk * TOK + tg * 128: kk * TOK + (tg + 1) * 128],
                gwc[:, kk * NE:(kk + 1) * NE], start=(kk == 0), stop=(kk == DT - 1)))
            for kk in range(DT)],
            reads=["gwc"] + [f"xT2_{t}_{d}" for d in range(DT)], writes=[f"pb{bank}"])
        sc = SCR_F[:, 0:8]; es = SCR_F[:, 8:16]; c1 = SCR_F[:, 16:24]; c2 = SCR_F[:, 24:32]
        es2 = SCR_F[:, 32:40]
        m1 = stats[:, 36:37]; m2 = stats[:, 40:41]; rs = stats[:, 44:45]; rw = stats[:, 48:49]
        p.emit("V", lambda v, b=bank: v.tensor_tensor(sc, pb[b][:, 0:NE], gbb[:, :], ALU.add),
               reads=[f"pb{bank}", "gbb"], writes=["g_sc"])
        p.emit("V", lambda v: v.reduce_max(m1, sc, axis=AX.X, negate=True),
               reads=["g_sc"], writes=["g_m1"])
        p.emit("S", lambda s: s.activation(es, sc, AF.Exp, bias=m1),
               reads=["g_sc", "g_m1"], writes=["g_es"])
        p.emit("V", lambda v: v.reduce_sum(m2, es, axis=AX.X), reads=["g_es"], writes=["g_sm"])
        p.emit("V", lambda v: v.reciprocal(rs, m2), reads=["g_sm"], writes=["g_rs"])
        p.emit("V", lambda v: v.tensor_scalar(es, es, rs, None, ALU.mult),
               reads=["g_es", "g_rs"], writes=["g_es"])
        p.emit("V", lambda v: v.reduce_max(m1, es, axis=AX.X), reads=["g_es"], writes=["g_m1"])
        p.emit("V", lambda v: v.tensor_scalar(c1, es, m1, None, ALU.is_ge),
               reads=["g_es", "g_m1"], writes=["g_c1"])
        p.emit("V", lambda v: v.tensor_copy(c2, c1), reads=["g_c1"], writes=["g_c2"])
        p.emit("V", lambda v: v.tensor_tensor(c2[:, 1:8], c2[:, 1:8], c1[:, 0:7], ALU.add),
               reads=["g_c1", "g_c2"], writes=["g_c2"])
        p.emit("V", lambda v: v.tensor_copy(c1, c2), reads=["g_c2"], writes=["g_c1b"])
        p.emit("V", lambda v: v.tensor_tensor(c2[:, 2:8], c2[:, 2:8], c1[:, 0:6], ALU.add),
               reads=["g_c1b", "g_c2"], writes=["g_c2"])
        p.emit("V", lambda v: v.tensor_copy(c1, c2), reads=["g_c2"], writes=["g_c1b"])
        p.emit("V", lambda v: v.tensor_tensor(c2[:, 4:8], c2[:, 4:8], c1[:, 0:4], ALU.add),
               reads=["g_c1b", "g_c2"], writes=["g_c2"])
        p.emit("V", lambda v: v.tensor_scalar(c2, c2, 1.0, None, ALU.is_equal),
               reads=["g_c2"], writes=["g_c2"])
        p.emit("V", lambda v: v.tensor_scalar(c1, es, m1, None, ALU.is_ge),
               reads=["g_es", "g_m1"], writes=["g_c1"])
        p.emit("V", lambda v: v.tensor_tensor(c1, c1, c2, ALU.mult),
               reads=["g_c1", "g_c2"], writes=["g_c1"])
        p.emit("V", lambda v: v.scalar_tensor_tensor(es2, c1, -1e30, es, ALU.mult, ALU.add),
               reads=["g_c1", "g_es"], writes=["g_es2"])
        p.emit("V", lambda v: v.reduce_max(m2, es2, axis=AX.X), reads=["g_es2"], writes=["g_m2"])
        p.emit("V", lambda v: v.tensor_scalar(c2, es2, m2, None, ALU.is_ge),
               reads=["g_es2", "g_m2"], writes=["g_c2"])
        p.emit("V", lambda v: v.tensor_tensor(rs, m1, m2, ALU.add),
               reads=["g_m1", "g_m2"], writes=["g_rs"])
        p.emit("V", lambda v: v.tensor_scalar(rs, rs, 1e-20, None, ALU.add),
               reads=["g_rs"], writes=["g_rs"])
        p.emit("V", lambda v: v.reciprocal(rw, rs), reads=["g_rs"], writes=["g_rw"])
        p.emit("V", lambda v: v.tensor_scalar(c1, c1, m1, None, ALU.mult),
               reads=["g_c1", "g_m1"], writes=["g_c1"])
        p.emit("V", lambda v: v.tensor_scalar(c2, c2, m2, None, ALU.mult),
               reads=["g_c2", "g_m2"], writes=["g_c2"])
        p.emit("V", lambda v: v.tensor_tensor(c1, c1, c2, ALU.add),
               reads=["g_c1", "g_c2"], writes=["g_c1"])
        p.emit("V", lambda v, tg=t: v.tensor_scalar(
            comb[:, tg * 8:(tg + 1) * 8], c1, rw, None, ALU.mult),
            reads=["g_c1", "g_rw"], writes=[f"comb{t}"])

    if debug == "B1":
        p.fence()
        add_dbg("comb", [128, 128])
        p.emit("Y", lambda y: y.dma_start(out=dbg["comb"], in_=comb[:, :]), dma=True)
        p.final_waits()
        return _finish(nc, p, ctx, dbg)

    def moebuf(par):
        return BUF_C if par == 0 else SCR_B

    def dwc(par, k):
        i = par * 4 + k
        return PSV[:, i * 1024:(i + 1) * 1024]

    def hreg(ft, sub):
        return BUF_B[:, ft * 2048 + sub * 128: ft * 2048 + (sub + 1) * 128]

    PRD = [SCR_F[:, 0:512], SCR_F[:, 0:512]]
    dbbt = BUF_B[:, 8192:9216]
    UPS = [BUF_C[:, 8192 + i * 512: 8192 + (i + 1) * 512] for i in range(2)]

    for e in range(NE):
        ep = e % 2
        p.emit("P", lambda g, ee=e, pa=ep: g.dma_start(
            out=upbt[:, pa * FT:(pa + 1) * FT], in_=upb_d[ee]),
            writes=[f"upb{ep}"], dma=f"eb{ep}")
        p.emit("P", lambda g, ee=e, pa=ep: g.dma_start(
            out=gpbt[:, pa * FT:(pa + 1) * FT], in_=gpb_d[ee]),
            writes=[f"gpb{ep}"], dma=f"eb{ep}")
        p.retoken([f"upb{ep}", f"gpb{ep}"])
        p.emit("P", lambda g, ee=e: g.dma_start(out=dbbt[:, :], in_=dbb_d[ee]),
            writes=["dbb"], dma="dbbs")
        for ci, (f0, fw) in enumerate(MOE_CHUNKS):
            par = ci % 2
            nk = fw // 128
            p.emit("P", lambda g, ee=e, a=f0, w=fw, pa=par: g.dma_start(
                out=moebuf(pa)[:, 0:DT * w].rearrange("p (k f) -> p k f", k=DT),
                in_=upw_d[ee, :, a:a + w].rearrange("(k p) f -> p k f", p=128)),
                writes=[f"upw{par}"], dma=f"m{par}")
            p.emit("P", lambda g, ee=e, a=f0, w=fw, pa=par: g.dma_start(
                out=moebuf(pa)[:, 4096:4096 + DT * w].rearrange("p (k f) -> p k f", k=DT),
                in_=gpw_d[ee, :, a:a + w].rearrange("(k p) f -> p k f", p=128)),
                writes=[f"gpw{par}"], dma=f"m{par}")
            for k2 in range(nk):
                p.emit("P", lambda g, ee=e, a=f0, kk=k2, pa=par: g.dma_start(
                    out=dwc(pa, kk), in_=dwt_d[ee, a + kk * 128:a + (kk + 1) * 128, :]),
                    writes=[f"dw{par}_{k2}"], dma=f"m{par}")
            p.retoken([f"upw{par}", f"gpw{par}"] + [f"dw{par}_{k2}" for k2 in range(nk)])
            for ft in range(nk):
                ftg = f0 // 128 + ft
                for mc in range(4):
                    bank = mc % 2
                    bank2 = 2 + mc % 2
                    p.group("T", [
                        (lambda te, kk=kk, f=ft, m=mc, b=bank, pa=par, w=fw: te.matmul(
                            pb[b][:, 0:512], moebuf(pa)[:, kk * w + f * 128: kk * w + (f + 1) * 128],
                            BUF_A[:, kk * TOK + m * 512:kk * TOK + (m + 1) * 512],
                            start=(kk == 0), stop=(kk == DT - 1)))
                        for kk in range(DT)],
                        reads=[f"upw{par}"], writes=[f"pb{bank}"])
                    p.group("T", [
                        (lambda te, kk=kk, f=ft, m=mc, b=bank2, pa=par, w=fw: te.matmul(
                            pb[b][:, 0:512],
                            moebuf(pa)[:, 4096 + kk * w + f * 128: 4096 + kk * w + (f + 1) * 128],
                            BUF_A[:, kk * TOK + m * 512:kk * TOK + (m + 1) * 512],
                            start=(kk == 0), stop=(kk == DT - 1)))
                        for kk in range(DT)],
                        reads=[f"gpw{par}"], writes=[f"pb{bank2}"])
                    ub = UPS[mc % 2]
                    p.emit("S", lambda s, _u=ub, b=bank, f=ftg, pa=ep: s.activation(
                        _u, pb[b][:, 0:512], AF.Identity, bias=upbt[:, pa * FT + f:pa * FT + f + 1]),
                        reads=[f"pb{bank}", f"upb{ep}"], writes=[f"UPS{mc % 2}"])
                    p.emit("V", lambda v, _u=ub, b=bank2, f=ftg, pa=ep, m=mc: v.scalar_tensor_tensor(
                        PRD[m % 2], pb[b][:, 0:512], gpbt[:, pa * FT + f:pa * FT + f + 1], _u,
                        ALU.add, ALU.mult),
                        reads=[f"pb{bank2}", f"gpb{ep}", f"UPS{mc % 2}"], writes=[f"PRD{mc % 2}"])
                    p.emit("S", lambda s, f=ft, m=mc: s.activation(
                        BUF_B[:, f * 2048 + m * 512: f * 2048 + (m + 1) * 512],
                        PRD[m % 2], AF.Silu),
                        reads=[f"PRD{mc % 2}"], writes=[f"h{ft}_{mc}"])
            hkeys = [f"h{ft}_{m}" for ft in range(nk) for m in range(4)]
            for sub in range(NT):
                for dc in range(2):
                    bank = 4 + (sub * 2 + dc) % 2
                    p.group("T", [
                        (lambda te, kk=kk, s_=sub, d=dc, b=bank, pa=par: te.matmul(
                            pb[b][:, 0:512], hreg(kk, s_), dwc(pa, kk)[:, d * 512:(d + 1) * 512],
                            start=(kk == 0), stop=(kk == nk - 1)))
                        for kk in range(nk)],
                        reads=hkeys + [f"dw{par}_{kk}" for kk in range(nk)],
                        writes=[f"pb{bank}"])
                    p.emit("V", lambda v, s_=sub, d=dc, b=bank, ee=e: v.scalar_tensor_tensor(
                        xv(s_)[:, d * 512:(d + 1) * 512], pb[b][:, 0:512],
                        comb[:, s_ * 8 + ee: s_ * 8 + ee + 1],
                        xv(s_)[:, d * 512:(d + 1) * 512], ALU.mult, ALU.add),
                        reads=[f"pb{bank}", f"comb{sub}", f"x{sub}"], writes=[f"x{sub}"])
        for sub in range(NT):
            p.emit("V", lambda v, s_=sub, ee=e: v.scalar_tensor_tensor(
                xv(s_), dbbt[:, :],
                comb[:, s_ * 8 + ee: s_ * 8 + ee + 1], xv(s_), ALU.mult, ALU.add),
                reads=["dbb", f"comb{sub}", f"x{sub}"], writes=[f"x{sub}"])

    if debug == "B2":
        p.fence()
        add_dbg("acc", [128, NT * D])
        p.emit("Y", lambda y: y.dma_start(out=dbg["acc"], in_=BUF_X[:, :]), dma=True)
        p.final_waits()
        return _finish(nc, p, ctx, dbg)

    p.fence()

    for t in range(NT):
        emit_ln(t, SCR_B[:, 0:1024], [f"fln{t}"])
        p.emit("V", lambda v, tg=t: v.tensor_scalar(
            xv(tg), xv(tg), stats[:, 0:1], stats[:, 4:5], ALU.add, ALU.mult),
            reads=[f"x{t}", "st_neg", "st_rst"], writes=[f"x{t}"])
    p.emit("Y", lambda y: y.dma_start(
        out=out_d.rearrange("(t p) d -> p t d", p=128),
        in_=BUF_X[:, :].rearrange("p (t d) -> p t d", t=NT)),
        reads=[f"x{t}" for t in range(NT)], dma=True)
    p.final_waits()
    return _finish(nc, p, ctx, dbg)


def _finish(nc, p, ctx, dbg):
    with nc.Block() as block:
        p.run(block)
    for t in reversed(ctx):
        t.__exit__(None, None, None)
    return nc, sorted(dbg.keys())


def prep_inputs(inputs, core):
    ii = {}
    b, q = core // 4, core % 4
    ii["x"] = np.ascontiguousarray(inputs["L0_attn_x"][b, q * TOK:(q + 1) * TOK, :],
                                   dtype=np.float32)
    ii["wq"] = inputs["L0_attn_Wq"]; ii["wk"] = inputs["L0_attn_Wk"]; ii["wv"] = inputs["L0_attn_Wv"]
    ii["wb"] = inputs["L0_attn_Wb"]; ii["wg"] = inputs["L0_attn_Wg"]; ii["wo"] = inputs["L0_attn_Wo"]
    ii["w1"] = inputs["L0_attn_w1"]; ii["w2"] = inputs["L0_attn_w2"]
    ii["gwt"] = np.ascontiguousarray(inputs["L0_ffn_gate_w"].T)
    upw = np.zeros((NE, D, FP), dtype=np.float32)
    upw[:, :, :F_RAW] = inputs["L0_ffn_up_w"].transpose(0, 2, 1)
    gpw = np.zeros((NE, D, FP), dtype=np.float32)
    gpw[:, :, :F_RAW] = inputs["L0_ffn_gp_w"].transpose(0, 2, 1)
    dwt = np.zeros((NE, FP, D), dtype=np.float32)
    dwt[:, :F_RAW, :] = inputs["L0_ffn_down_w"].transpose(0, 2, 1)
    ii["upw"] = upw; ii["gpw"] = gpw; ii["dwt"] = dwt
    ub = np.zeros((NE, FP), dtype=np.float32); ub[:, :F_RAW] = inputs["L0_ffn_up_b"]
    gb = np.zeros((NE, FP), dtype=np.float32); gb[:, :F_RAW] = inputs["L0_ffn_gp_b"]
    ii["upb"] = np.ascontiguousarray(ub.reshape(NE, FT, 128).transpose(0, 2, 1))
    ii["gpb"] = np.ascontiguousarray(gb.reshape(NE, FT, 128).transpose(0, 2, 1))
    ii["dbb"] = np.ascontiguousarray(
        np.broadcast_to(inputs["L0_ffn_down_b"][:, None, :], (NE, 128, D)))
    ii["bob"] = np.ascontiguousarray(np.broadcast_to(inputs["L0_attn_bo"][None, :], (128, D)))
    ii["b2b"] = np.ascontiguousarray(np.broadcast_to(inputs["L0_attn_b2"][None, :], (128, D)))
    ii["b1c"] = np.ascontiguousarray(inputs["L0_attn_b1"].reshape(FFN // 128, 128).T)
    ii["g1c"] = np.ascontiguousarray(inputs["L0_attn_ln1_g"].reshape(DT, 128).T)
    ii["b1lc"] = np.ascontiguousarray(inputs["L0_attn_ln1_b"].reshape(DT, 128).T)
    ii["g2c"] = np.ascontiguousarray(inputs["L0_attn_ln2_g"].reshape(DT, 128).T)
    ii["b2lc"] = np.ascontiguousarray(inputs["L0_attn_ln2_b"].reshape(DT, 128).T)
    ii["gbb"] = np.ascontiguousarray(np.broadcast_to(inputs["L0_ffn_gate_b"][None, :], (128, NE)))
    ii["identf"] = np.eye(128, dtype=np.float32)
    ii["identb"] = np.eye(128).astype(BFD)
    ii["miu"] = np.triu(np.ones((128, 128), dtype=np.float32))
    ii["ones128"] = np.ones((128, 128), dtype=np.float32)
    ii["msu"] = np.triu(np.ones((128, 128)), 1).astype(BFD)
    ii["miub"] = np.triu(np.ones((128, 128))).astype(BFD)
    ii["onesb"] = np.ones((128, 128)).astype(BFD)
    cm = np.zeros((128, 8), dtype=np.float32)
    for j in range(4):
        cm[:, j] = 1.0 if j < q else 0.0
        cm[:, 4 + j] = 0.0 if j < q else 1.0
    ii["cmask"] = cm
    for k in ii:
        if ii[k].dtype == np.float64:
            ii[k] = ii[k].astype(np.float32)
        ii[k] = np.ascontiguousarray(ii[k])
    return ii


_CACHE = {}


def kernel(**inputs):
    if "nc" not in _CACHE:
        _CACHE["nc"] = build_nc()
    nc, _ = _CACHE["nc"]
    in_maps = [prep_inputs(inputs, core) for core in range(8)]
    res = run_bass_kernel_spmd(nc, in_maps, core_ids=list(range(8)))
    outs = [np.asarray(res.results[core]["out"]) for core in range(8)]
    full = np.stack([np.concatenate(outs[0:4], 0), np.concatenate(outs[4:8], 0)], 0)
    return full.astype(np.float32)
